# version 02
# baseline (speedup 1.0000x reference)
import numpy as np
import ml_dtypes
from collections import defaultdict

import concourse.bass as bass
import concourse.mybir as mybir
from concourse.bass_utils import run_bass_kernel_spmd

F32 = mybir.dt.float32
BF16 = mybir.dt.bfloat16
I32 = mybir.dt.int32
AF = mybir.ActivationFunctionType
ALU = mybir.AluOpType
AX = mybir.AxisListType
BF16NP = ml_dtypes.bfloat16


class CFG:
    def __init__(self, S=8192, FH=3072, MH=4096, layers=(0, 1), debug_taps=()):
        self.S, self.FH, self.MH = S, FH, MH
        self.D, self.H, self.DH, self.E, self.B, self.NC = 1024, 16, 64, 8, 2, 8
        self.layers = tuple(layers)
        self.debug_taps = tuple(debug_taps)
        self.TPC = self.S * self.B // self.NC
        self.CB = min(512, self.TPC)
        self.NCB = self.TPC // self.CB
        self.QW = self.CB
        self.NQB = self.TPC // self.QW
        self.NTB = self.TPC // 128
        self.TP = min(1024, self.TPC)
        self.NTP = self.TPC // self.TP
        self.YW = min(512, self.TP)
        self.NYW = self.TP // self.YW
        self.FCH = self.D // 128
        self.NHB_F = self.FH // 128
        self.NHB_M = self.MH // 128
        self.KTPR = self.TPC // 128
        assert self.FH % 256 == 0 and self.MH % 512 == 0 and self.TPC % 128 == 0


ENGINES = ("tensor", "vector", "scalar", "sync", "gpsimd")


def tmax(*toks):
    best = {}
    for t in toks:
        if t is None:
            continue
        s, v = t
        if v > best.get(s, -1):
            best[s] = v
    return [(s, v) for s, v in best.items()]


class Prog:
    def __init__(self):
        self.ev = {e: [] for e in ENGINES}
        self.cnt = defaultdict(int)
        self.sems = set()
        self.bank_r = {}

    def emit(self, eng, fn, waits=(), inc=None, amt=1):
        w = [t for t in waits if t is not None]
        tok = None
        if inc is not None:
            self.sems.add(inc)
            self.cnt[inc] += amt
            tok = (inc, self.cnt[inc])
        self.ev[eng].append((fn, w, inc, amt))
        return tok

    def replay_into(self, sems):
        waited = {e: defaultdict(int) for e in ENGINES}

        def mk(eng_name):
            def run(eng):
                for fn, waits, inc, amt in self.ev[eng_name]:
                    for s, v in waits:
                        if v > waited[eng_name][s]:
                            eng.wait_ge(sems[s], v)
                            waited[eng_name][s] = v
                    inst = fn(eng)
                    if inc is not None:
                        inst.then_inc(sems[inc], amt)
            return run
        return {e: mk(e) for e in ENGINES}


class Stream:

    def __init__(self, P, q, tag, bufs):
        self.P, self.q, self.bufs = P, q, bufs
        self.tag = tag
        self.i = 0
        self.free = [[] for _ in bufs]
        self.ld = [None] * len(bufs)

    def load(self, src_ap):
        s = self.i % len(self.bufs)
        self.i += 1
        buf = self.bufs[s]
        self.ld[s] = self.P.emit(
            self.q, lambda e, d=buf, src=src_ap: e.dma_start(out=d[:], in_=src),
            waits=self.free[s], inc=f"{self.tag}{s}", amt=16)
        return buf, s

    def release(self, s, tok):
        self.free[s] = [tok]

    def gate(self, tok):
        self.free = [[tok] for _ in self.bufs]



def host_prepare(cfg, inp):
    D, H, DH, E, S = cfg.D, cfg.H, cfg.DH, cfg.E, cfg.S
    sh = {}
    if 0 in cfg.layers:
        sh["l0_wq"] = np.ascontiguousarray(
            np.asarray(inp["L0_attn_W_Q"]).transpose(2, 0, 1).reshape(D, H * DH))
        sh["l0_wk"] = np.ascontiguousarray(
            np.asarray(inp["L0_attn_W_K"]).transpose(2, 0, 1).reshape(D, H * DH))
        sh["l0_wv"] = np.ascontiguousarray(
            np.asarray(inp["L0_attn_W_V"]).transpose(2, 0, 1).reshape(D, H * DH))
        sh["l0_wo"] = np.ascontiguousarray(np.asarray(inp["L0_attn_W_O"]).T)
        sh["l0_wg"] = np.ascontiguousarray(np.asarray(inp["L0_ffn_w_gate"]).T)
        sh["l0_wu"] = np.ascontiguousarray(np.asarray(inp["L0_ffn_w_up"]).T)
        sh["l0_wd"] = np.ascontiguousarray(np.asarray(inp["L0_ffn_w_down"]).T)
    if 1 in cfg.layers:
        wqkv = np.asarray(inp["L1_attn_Wqkv"]).reshape(H, 3 * DH, D)
        bqkv = np.asarray(inp["L1_attn_bqkv"]).reshape(H, 3 * DH)
        sh["l1_wq"] = np.ascontiguousarray(
            wqkv[:, 0:DH, :].transpose(2, 0, 1).reshape(D, H * DH))
        sh["l1_wk"] = np.ascontiguousarray(
            wqkv[:, DH:2 * DH, :].transpose(2, 0, 1).reshape(D, H * DH))
        sh["l1_wv"] = np.ascontiguousarray(
            wqkv[:, 2 * DH:, :].transpose(2, 0, 1).reshape(D, H * DH))
        sh["l1_bq"] = np.ascontiguousarray(bqkv[:, 0:DH].reshape(8, 128).T)
        sh["l1_bk"] = np.ascontiguousarray(bqkv[:, DH:2 * DH].reshape(8, 128).T)
        sh["l1_bv"] = np.ascontiguousarray(bqkv[:, 2 * DH:].reshape(1, H * DH))
        sh["l1_wo"] = np.ascontiguousarray(np.asarray(inp["L1_attn_Wo"]).T)
        sh["l1_bo"] = np.ascontiguousarray(
            np.asarray(inp["L1_attn_bo"]).reshape(8, 128).T)
        sh["l1_rw"] = np.ascontiguousarray(np.asarray(inp["L1_ffn_router_w"]).T)
        w1 = np.asarray(inp["L1_ffn_w1"])
        b1 = np.asarray(inp["L1_ffn_b1"])
        w2 = np.asarray(inp["L1_ffn_w2"])
        w1aug = np.zeros((E, D + 128, cfg.MH), np.float32)
        for e in range(E):
            w1aug[e, :D, :] = w1[e].T
            w1aug[e, D, :] = b1[e]
        sh["l1_w1"] = w1aug
        sh["l1_w2"] = np.ascontiguousarray(w2.transpose(0, 2, 1))
        sh["l1_b2"] = np.ascontiguousarray(np.asarray(inp["L1_ffn_b2"]))

    sh["fconst"] = np.broadcast_to(
        np.array([0.0, 1.0, 1e-5, -0.5, 0.125, 0.5, -1.0, 1.0 / 127.0],
                 np.float32), (128, 8)).copy()
    sh["ones_col"] = np.ones((128, 1), np.float32)
    sh["ones_r1"] = np.ones((1, 128), np.float32)
    sh["ones_r1_bf"] = np.ones((1, 128), BF16NP)
    sh["ones_row_bf"] = np.ones((1, 512), BF16NP)
    sh["iota_e"] = np.broadcast_to(np.arange(8, dtype=np.float32), (128, 8)).copy()
    sel8 = np.zeros((8, 8 * 128), np.float32)
    for e in range(8):
        sel8[e, e * 128:(e + 1) * 128] = 1.0
    sh["sel8"] = sel8
    sh["ident"] = np.eye(128, dtype=np.float32)

    x = np.asarray(inp["L0_attn_x"])
    in_maps, idxs = [], []
    kk = np.arange(128)[:, None]
    qq = np.arange(cfg.QW)[None, :]
    for c in range(8):
        b, j = divmod(c, 4)
        idx = j + 4 * np.arange(cfg.TPC)
        idxs.append((b, idx))
        m = dict(sh)
        m["x_fm"] = np.ascontiguousarray(x[b, idx, :].T)
        masks = np.zeros((128, 4 * cfg.QW), BF16NP)
        for r in range(4):
            d = 0 if r <= j else 1
            masks[:, r * cfg.QW:(r + 1) * cfg.QW] = (kk <= qq - d)
        m["masks"] = masks
        in_maps.append(m)
    return in_maps, idxs


def host_assemble(cfg, results, idxs, key="out_fm"):
    out = np.zeros((cfg.B, cfg.S, cfg.D), np.float32)
    for c in range(8):
        b, idx = idxs[c]
        out[b, idx, :] = results[c][key].T
    return out


OFF_END = 212832


def build_program(cfg, layers):
    nc = bass.Bass("TRN2", target_bir_lowering=False, debug=False)
    D, H, DH, E, TPC = cfg.D, cfg.H, cfg.DH, cfg.E, cfg.TPC
    FCH, CB, NCB, QW, NQB = cfg.FCH, cfg.CB, cfg.NCB, cfg.QW, cfg.NQB
    TP, NTP, YW, NYW, KTPR = cfg.TP, cfg.NTP, cfg.YW, cfg.NYW, cfg.KTPR
    NTB = cfg.NTB

    OFF_A = 0
    OFF_B = OFF_A + FCH * TPC * 4
    OFF_C = OFF_B + FCH * TPC * 2
    OFF_D = OFF_C + FCH * TPC * 2
    OFF_SM = OFF_END - 21312

    din = {}
    def dram_in(name, shape, dt):
        din[name] = nc.dram_tensor(name, list(shape), dt, kind="ExternalInput").ap()

    dram_in("x_fm", (D, TPC), F32)
    dram_in("fconst", (128, 8), F32)
    for n, s, dt in [("ones_col", (128, 1), F32), ("ones_r1", (1, 128), F32),
                     ("ones_r1_bf", (1, 128), BF16), ("ones_row_bf", (1, 512), BF16),
                     ("iota_e", (128, 8), F32), ("ident", (128, 128), F32),
                     ("masks", (128, 4 * QW), BF16), ("sel8", (8, 1024), F32)]:
        dram_in(n, s, dt)
    if 0 in layers:
        for n, s, dt in [("l0_wq", (D, D), F32), ("l0_wk", (D, D), F32),
                         ("l0_wv", (D, D), F32), ("l0_wo", (D, D), F32),
                         ("l0_wg", (D, cfg.FH), I32), ("l0_wu", (D, cfg.FH), I32),
                         ("l0_wd", (cfg.FH, D), I32)]:
            dram_in(n, s, dt)
    if 1 in layers:
        for n, s, dt in [("l1_wq", (D, D), F32), ("l1_wk", (D, D), F32),
                         ("l1_wv", (D, D), F32), ("l1_bq", (128, 8), F32),
                         ("l1_bk", (128, 8), F32), ("l1_bv", (1, D), F32),
                         ("l1_wo", (D, D), F32), ("l1_bo", (128, 8), F32),
                         ("l1_rw", (D, 8), F32), ("l1_w1", (E, D + 128, cfg.MH), F32),
                         ("l1_w2", (E, cfg.MH, D), F32), ("l1_b2", (E, D), F32)]:
            dram_in(n, s, dt)

    out_fm = nc.dram_tensor("out_fm", [D, TPC], F32, kind="ExternalOutput").ap()
    gate_tap = (nc.dram_tensor("tap_gates", [8, TPC], F32, kind="ExternalOutput").ap()
                if "gates" in cfg.debug_taps else None)
    h_tap = (nc.dram_tensor("tap_h", [cfg.MH, TP], F32, kind="ExternalOutput").ap()
             if "moe_h" in cfg.debug_taps else None)
    taps = {t: nc.dram_tensor("tap_" + t, [D, TPC], F32, kind="ExternalOutput").ap()
            for t in cfg.debug_taps if t != "gates"}

    k_contrib = nc.dram_tensor("k_contrib", [D + 1, TPC], BF16, kind="Internal").ap()
    k_gath = nc.dram_tensor("k_gath", [4 * (D + 1), TPC], BF16, kind="Internal").ap()
    v_contrib = nc.dram_tensor("v_contrib", [4 * TPC, 260], BF16, kind="Internal").ap()
    v_gath = nc.dram_tensor("v_gath", [16 * TPC, 260], BF16, kind="Internal").ap()
    q_park = nc.dram_tensor("q_park", [D, TPC], BF16, kind="Internal").ap()

    SBUF_BASE = 16512

    def sb(name, shape, dt, off):
        nbytes = int(np.prod(shape[1:])) * mybir.dt.size(dt)
        assert off >= 0 and off + nbytes <= OFF_END, (name, off, nbytes)
        return nc.alloc_sbuf_tensor_at(name, list(shape), dt,
                                       offset=SBUF_BASE + off)

    xacc = [sb(f"xacc{i}", (128, TPC), F32, OFF_A + i * TPC * 4) for i in range(FCH)]
    xn = [sb(f"xn{i}", (128, TPC), BF16, OFF_B + i * TPC * 2) for i in range(FCH)]
    kwin = [sb(f"kw{i}", (128, TPC), BF16, OFF_B + i * TPC * 2) for i in range(8)]
    q2 = [sb(f"q2_{i}", (128, TPC), BF16, OFF_C + i * TPC * 2) for i in range(8)]

    sm = [OFF_SM]
    def sbs(name, shape, dt):
        h = sb(name, shape, dt, sm[0])
        sm[0] += (int(np.prod(shape[1:])) * mybir.dt.size(dt) + 31) & ~31
        assert sm[0] <= OFF_END, name
        return h

    fconst_sb = sbs("c_fconst", (128, 8), F32)
    FCONSTS = [0.0, 1.0, 1e-5, -0.5, 0.125, 0.5, -1.0, 1.0 / 127.0]
    for i, v in enumerate(FCONSTS):
        nc.const_aps.register(F32, v, fconst_sb[:, i:i + 1])
    ones_col = sbs("c_ones", (128, 1), F32)
    ones_r1 = sbs("c_or1", (1, 128), F32)
    ones_r1_bf = sbs("c_or1b", (1, 128), BF16)
    ones_row_bf = sbs("c_orow", (1, 512), BF16)
    iota_e = sbs("c_iota", (128, 8), F32)
    sel8_sb = sbs("c_sel8", (8, 1024), F32)
    ident = sbs("c_ident", (128, 128), F32)
    masks_sb = sbs("c_masks", (128, 4 * QW), BF16)
    bq_sb = sbs("bq", (128, 8), F32)
    bk_sb = sbs("bk", (128, 8), F32)
    bo_sb = sbs("bo", (128, 8), F32)
    bv_sb = sbs("bv", (1, D), BF16)
    rw_sb = sbs("rw", (128, 64), BF16)
    b2_sb = sbs("b2", (8, D), BF16)
    qn_sb = sbs("qn", (1, 8), F32)
    kn_sb = sbs("kn", (1, 8), F32)
    kn4_sb = sbs("kn4", (1, 32), F32)
    ntmp = sbs("ntmp", (1, 16), F32)
    msc = sbs("msc", (1, 8), F32)
    negM1 = sbs("negM1", (1, 1), F32)
    negM = sbs("negM", (128, 1), F32)
    recrow = sbs("recrow", (1, 512), F32)

    AL = lambda x: (x + 31) & ~31
    xsq = sb("xsq", (128, TPC), F32, OFF_D)
    mean_bc = sb("mean_bc", (128, TPC), F32, OFF_D + TPC * 4)
    rstd_bc = sb("rstd_bc", (128, TPC), F32, OFF_D + 2 * TPC * 4)
    mrow = sb("mrow", (1, TPC), F32, OFF_D + 3 * TPC * 4)
    vrow = sb("vrow", (1, TPC), F32, OFF_D + 3 * TPC * 4 + AL(TPC * 4))
    srow = sb("srow", (1, TPC), F32, OFF_D + 3 * TPC * 4 + 2 * AL(TPC * 4))
    o1 = OFF_D + AL(TPC * 4)
    wslab = [sb(f"wsl{i}", (128, FCH, 128), BF16, o1 + i * FCH * 256) for i in range(2)]
    o2 = o1 + 2 * FCH * 256
    vslab = [sb(f"vsl{i}", (128, FCH, 512), BF16, o2 + i * FCH * 1024) for i in range(2)]
    o3 = o2 + 2 * FCH * 1024
    k2tmp = [sb(f"k2t{i}", (128, TPC), BF16, o3 + i * TPC * 2) for i in range(2)]
    o4 = o3 + 2 * TPC * 2
    vstage = [sb(f"vst{i}", (128, 4, 260), BF16, o4 + i * 2080) for i in range(2)]

    HCAP = (8 * TPC) // TP

    def h_tiles(n, tag):
        out = []
        for i in range(n):
            off = OFF_C + i * TP * 2 if i < HCAP else OFF_D + (i - HCAP) * TP * 2
            out.append(sb(f"{tag}{i}", (128, TP), BF16, off))
        return out

    if 0 in layers:
        h_l0 = h_tiles(cfg.NHB_F, "hf")
        f1 = OFF_D + max(0, cfg.NHB_F - HCAP) * TP * 2
        gslab = [sb(f"gsl{i}", (128, FCH, 128), BF16, f1 + i * FCH * 256)
                 for i in range(2)]
        uslab = [sb(f"usl{i}", (128, FCH, 128), BF16, f1 + 2 * FCH * 256 + i * FCH * 256)
                 for i in range(2)]
        g_sb = [sb(f"g_s{i}", (128, CB), BF16, f1 + 4 * FCH * 256 + i * CB * 2)
                for i in range(2)]
        dslab = [sb(f"dsl{i}", (128, cfg.NHB_F, 128), BF16,
                    f1 + 4 * FCH * 256 + 2 * CB * 2 + i * cfg.NHB_F * 256)
                 for i in range(2)]
    if 1 in layers:
        h_moe = h_tiles(cfg.NHB_M, "hm")
        m0 = OFF_D + max(0, cfg.NHB_M - HCAP) * TP * 2
        m1slab = [sb(f"m1s{i}", (128, 9, 128), BF16, m0 + i * 9 * 256)
                  for i in range(2)]
        m1e = m0 + 2 * 9 * 256
        m2slab = [sb(f"m2s{i}", (128, 2, 512), BF16, m1e + i * 2 * 1024)
                  for i in range(2)]
        m2e = m1e + 2 * 2 * 1024
        gb_sb = [sb("gb0", (128, TP), F32, m2e)]
        gbe = m2e + TP * 4
        cT_sb = sb("cT", (8, TPC), F32, gbe)
        cTb_sb = sb("cTb", (8, TPC), BF16, gbe + TPC * 4)
        r0 = gbe + TPC * 4 + TPC * 2
        U32 = mybir.dt.uint32
        lg_sb = sb("lg", (128, 8), F32, r0)
        r_m8 = sb("r_m8", (128, 8), F32, r0 + 32)
        r_i8 = sb("r_i8", (128, 8), U32, r0 + 64)
        r_if = sb("r_if", (128, 2), F32, r0 + 96)
        r_oh1 = sb("r_oh1", (128, 8), F32, r0 + 128)
        r_oh2 = sb("r_oh2", (128, 8), F32, r0 + 160)
        r_neg = sb("r_neg", (128, 1), F32, r0 + 192)
        r_p = sb("r_p", (128, 8), F32, r0 + 224)
        r_s = sb("r_s", (128, 1), F32, r0 + 256)
        r_t = sb("r_t", (128, 1), F32, r0 + 288)
        r_cf = sb("r_cf", (128, 8), F32, r0 + 320)

    vwstride = (KTPR * 520 + 31) & ~31
    vw = [sb(f"vw{i}", (128, KTPR, 260), BF16, OFF_D + i * vwstride)
          for i in range(4)]
    a0 = OFF_D + 4 * vwstride
    z_sb = [sb(f"z{i}", (128, TPC), BF16, a0 + i * TPC * 2) for i in range(2)]
    a1 = a0 + 2 * TPC * 2
    p_buf = [sb(f"pbf{i}", (128, QW), BF16, a1 + i * QW * 2) for i in range(4)]
    a2 = a1 + 4 * QW * 2
    ptmp = sb("ptmp", (128, QW), BF16, a2)
    a3 = a2 + QW * 2
    oslab = [sb(f"osl{i}", (128, D), BF16, a3 + i * D * 2) for i in range(4)]
    a4 = a3 + 4 * D * 2
    recb = sb("recb", (64, 512), F32, a4)
    assert a4 + 2048 <= OFF_SM, (a4, OFF_SM)

    pbk = [nc.alloc_psum_tensor(f"bank{i}", [128, 512], F32) for i in range(8)]

    P = Prog()

    def dma(q, dst, src, waits=(), sem="MISC"):
        return P.emit(q, lambda e, d=dst, s=src: e.dma_start(out=d, in_=s),
                      waits=waits, inc=sem, amt=16)

    def mm(out, lhsT, rhs, start, stop, waits=(), inc=None):
        return P.emit("tensor",
                      lambda e, o=out, l=lhsT, r=rhs, a=start, b=stop:
                      e.matmul(o, l, r, start=a, stop=b),
                      waits=waits, inc=inc)

    def bk_acq(i, waits=()):
        t = P.bank_r.get(i)
        return (list(waits) + [t]) if t else list(waits)

    def bk_read(i, tok):
        P.bank_r[i] = tok

    def vec(fn, waits=(), inc="DV"):
        return P.emit("vector", fn, waits=waits, inc=inc)

    def act(fn, waits=(), inc="AC"):
        return P.emit("scalar", fn, waits=list(waits) + [cst], inc=inc)

    dma("sync", fconst_sb[:], din["fconst"], sem="CST")
    for h, name in [(ones_col, "ones_col"), (ones_r1, "ones_r1"),
                    (iota_e, "iota_e"), (ident, "ident"),
                    (ones_r1_bf, "ones_r1_bf"), (ones_row_bf, "ones_row_bf"),
                    (masks_sb, "masks"), (sel8_sb, "sel8")]:
        dma("sync", h[:], din[name], sem="CST")
    if 1 in layers:
        for h, name in [(bq_sb, "l1_bq"), (bk_sb, "l1_bk"), (bo_sb, "l1_bo")]:
            dma("sync", h[:], din[name], sem="CST")
        for h, name in [(bv_sb, "l1_bv"), (b2_sb, "l1_b2")]:
            dma("gpsimd", h[:], din[name], sem="CSG")
        dma("gpsimd", rw_sb.ap().rearrange("p (fc e) -> p fc e", e=8),
            din["l1_rw"].rearrange("(fc p) e -> p fc e", p=128), sem="CSG")
    cst = ("CST", P.cnt["CST"])
    cstg = ("CSG", P.cnt["CSG"]) if P.cnt["CSG"] else None

    for fc in range(FCH):
        dma("sync", xacc[fc][:], din["x_fm"][fc * 128:(fc + 1) * 128, :], sem="XL")
    xacc_tok = [("XL", P.cnt["XL"])] * FCH

    zt = P.emit("vector", lambda e: e.memset(mrow[0:1, :], 0.0), inc="DV")
    norm_zero = dma("gpsimd", k_contrib[D:D + 1, :], mrow[0:1, :], waits=[zt],
                    sem="KN")

    st = {
        "xacc": xacc_tok,
        "B_free": [None] * 8,
        "q2_free": [None] * 8,
        "h_free": None,
        "arena_free": None,
        "kgath_free": None,
        "vgath_free": None,
        "qpark_free": None,
        "contrib_free": None,
        "k2tmp_free": [None, None],
        "vstage_free": [None, None],
        "xsq_gate": None,
        "wstream": Stream(P, "gpsimd", "sW", wslab),
        "vstream": Stream(P, "gpsimd", "sV", vslab),
        "ostream": Stream(P, "gpsimd", "sO", oslab),
    }
    if 0 in layers:
        st["gs"] = Stream(P, "gpsimd", "sG", gslab)
        st["us"] = Stream(P, "gpsimd", "sU", uslab)
        st["ds"] = Stream(P, "gpsimd", "sD", dslab)
    if 1 in layers:
        st["m1s"] = Stream(P, "gpsimd", "s1", m1slab)
        st["m2s"] = Stream(P, "gpsimd", "s2", m2slab)

    def emit_ln(lid):
        arena_w = [st["arena_free"]] if st["arena_free"] else []
        sum_tok = [None] * NCB
        sq_tok_f = [None] * NCB
        last_sq = st["xsq_gate"]
        for fc in range(FCH):
            for tb in range(NCB):
                w = [st["xacc"][fc], cst]
                if fc == 0:
                    w = bk_acq(tb, w)
                t = mm(pbk[tb][0:1, 0:CB], ones_col[:],
                       xacc[fc][:, tb * CB:(tb + 1) * CB],
                       start=(fc == 0), stop=(fc == FCH - 1),
                       waits=w, inc="TE" if fc == FCH - 1 else None)
                if fc == FCH - 1:
                    sum_tok[tb] = t
            w = [st["xacc"][fc], last_sq] + (arena_w if fc == 0 else [])
            s = act(lambda e, fc=fc: e.activation(xsq[:], xacc[fc][:], AF.Square),
                    waits=w)
            for tb in range(NCB):
                w = [s]
                if fc == 0:
                    w = bk_acq(4 + tb, w)
                t = mm(pbk[4 + tb][0:1, 0:CB], ones_col[:],
                       xsq[:, tb * CB:(tb + 1) * CB],
                       start=(fc == 0), stop=(fc == FCH - 1),
                       waits=w, inc="TE")
                if fc == FCH - 1:
                    sq_tok_f[tb] = t
                last_sq = t
        m_toks = []
        for tb in range(NCB):
            t = vec(lambda e, tb=tb: e.tensor_scalar_mul(
                mrow[0:1, tb * CB:(tb + 1) * CB], pbk[tb][0:1, 0:CB], 1.0 / D),
                waits=[sum_tok[tb], norm_zero] + arena_w)
            bk_read(tb, t)
            m_toks.append(t)
        msq = vec(lambda e: e.tensor_mul(srow[0:1, :], mrow[0:1, :], mrow[0:1, :]),
                  waits=m_toks)
        v_toks = []
        for tb in range(NCB):
            t = vec(lambda e, tb=tb: e.scalar_tensor_tensor(
                vrow[0:1, tb * CB:(tb + 1) * CB], pbk[4 + tb][0:1, 0:CB],
                1.0 / D, srow[0:1, tb * CB:(tb + 1) * CB], ALU.mult, ALU.subtract),
                waits=[sq_tok_f[tb], msq])
            bk_read(4 + tb, t)
            v_toks.append(t)
        t_ln = act(lambda e: e.activation(srow[0:1, :], vrow[0:1, :], AF.Ln,
                                          bias=1e-5), waits=v_toks + [msq])
        t_rs = act(lambda e: e.activation(srow[0:1, :], srow[0:1, :], AF.Exp,
                                          scale=-0.5), waits=[t_ln])
        bc = []
        for tb in range(NCB):
            w = bk_acq(tb, [m_toks[tb], cst])
            t1 = mm(pbk[tb][:, 0:CB], ones_r1[0:1, :],
                    mrow[0:1, tb * CB:(tb + 1) * CB], True, True, w, inc="TE")
            w = bk_acq(4 + tb, [t_rs])
            t2 = mm(pbk[4 + tb][:, 0:CB], ones_r1[0:1, :],
                    srow[0:1, tb * CB:(tb + 1) * CB], True, True, w, inc="TE")
            c1 = vec(lambda e, tb=tb: e.tensor_copy(
                mean_bc[:, tb * CB:(tb + 1) * CB], pbk[tb][:, 0:CB]), waits=[t1])
            c2 = vec(lambda e, tb=tb: e.tensor_copy(
                rstd_bc[:, tb * CB:(tb + 1) * CB], pbk[4 + tb][:, 0:CB]), waits=[t2])
            bk_read(tb, c1)
            bk_read(4 + tb, c2)
            bc += [c1, c2]
        xn_tok = []
        for fc in range(FCH):
            t1 = vec(lambda e, fc=fc: e.tensor_sub(xsq[:], xacc[fc][:], mean_bc[:]),
                     waits=bc + [st["xacc"][fc], last_sq])
            w2 = [t1] + ([st["B_free"][fc]] if st["B_free"][fc] else [])
            t2 = vec(lambda e, fc=fc: e.tensor_mul(xn[fc][:], xsq[:], rstd_bc[:]),
                     waits=w2)
            xn_tok.append(t2)
            last_sq = t2
        st["xsq_gate"] = last_sq
        return xn_tok

    def emit_qkv(lid, xn_tok):
        l0 = (lid == 0)
        pre = f"l{lid}_"
        ws = st["wstream"]
        g_ = xn_tok[-1]
        ws.gate(g_)
        st["vstream"].gate(g_)
        st["k2tmp_free"] = [g_] * 2
        st["vstage_free"] = [g_] * 2

        def proj_qk(which):
            contrib_toks = []
            norm_done = None
            for hp in range(8):
                buf, slot = ws.load(
                    din[pre + ("wq" if which == "q" else "wk")]
                    [:, hp * 128:(hp + 1) * 128].rearrange("(g p) c -> p g c", p=128))
                ld = ws.ld[slot]
                if which == "q":
                    dest, dw = q2[hp], st["q2_free"][hp]
                else:
                    dest = k2tmp[hp % 2]
                    dw = contrib_toks[hp - 2] if hp >= 2 else st["k2tmp_free"][hp % 2]
                evac = []
                for tb in range(NCB):
                    bi = tb % 2
                    w = bk_acq(bi, [ld] + xn_tok)
                    t = None
                    for fc in range(FCH):
                        t = mm(pbk[bi][:, 0:CB], buf[:, fc, :],
                               xn[fc][:, tb * CB:(tb + 1) * CB],
                               start=(fc == 0), stop=(fc == FCH - 1),
                               waits=w if fc == 0 else (),
                               inc="TE" if fc == FCH - 1 else None)
                    ew = [t] + ([dw] if dw and tb == 0 else [])
                    if not l0:
                        b = bq_sb if which == "q" else bk_sb
                        et = vec(lambda e, d=dest, tb=tb, bi=bi, b=b, hp=hp:
                                 e.tensor_scalar_add(d[:, tb * CB:(tb + 1) * CB],
                                                     pbk[bi][:, 0:CB],
                                                     b[:, hp:hp + 1]),
                                 waits=ew + [cst])
                    else:
                        et = vec(lambda e, d=dest, tb=tb, bi=bi:
                                 e.tensor_copy(d[:, tb * CB:(tb + 1) * CB],
                                               pbk[bi][:, 0:CB]), waits=ew)
                    bk_read(bi, et)
                    evac.append(et)
                ws.release(slot, evac[-1])
                s = act(lambda e, d=dest: e.activation(xsq[:], d[:], AF.Square),
                        waits=evac + [st["xsq_gate"]])
                red = []
                for tb in range(NCB):
                    w = bk_acq(6, [s])
                    t = mm(pbk[6][0:1, 0:CB], ones_col[:],
                           xsq[:, tb * CB:(tb + 1) * CB], True, True, w, inc="TE")
                    rt = vec(lambda e, tb=tb: e.reduce_max(
                        ntmp[0:1, tb:tb + 1], pbk[6][0:1, 0:CB], AX.X),
                        waits=[t, norm_done])
                    bk_read(6, rt)
                    red.append(rt)
                dstrow = qn_sb if which == "q" else kn_sb
                norm_done = vec(lambda e, d=dstrow, hp=hp: e.reduce_max(
                    d[0:1, hp:hp + 1], ntmp[0:1, 0:NCB], AX.X), waits=red)
                st["xsq_gate"] = s
                st["xsq_gate"] = red[-1]
                if which == "q":
                    pk = dma("sync", q_park[hp * 128:(hp + 1) * 128, :], q2[hp][:],
                             waits=evac + ([st["qpark_free"]] if st["qpark_free"]
                                           else []), sem=f"pk{hp}")
                    st["q2_free"][hp] = pk
                else:
                    cw = [st["contrib_free"]] if st["contrib_free"] else []
                    kc = dma("sync", k_contrib[hp * 128:(hp + 1) * 128, :],
                             k2tmp[hp % 2][:], waits=evac + cw, sem=f"kc{hp % 2}")
                    contrib_toks.append(kc)
            return contrib_toks, norm_done

        _, qn_done = proj_qk("q")
        k_contrib_toks, kn_done = proj_qk("k")
        st["k2tmp_free"] = [k_contrib_toks[-2], k_contrib_toks[-1]]
        kc_norm = dma("gpsimd", k_contrib[D:D + 1, 0:8], kn_sb[0:1, 0:8],
                      waits=[kn_done], sem="KN")
        k_contrib_toks.append(kc_norm)

        vs = st["vstream"]
        ones_done = [None, None]
        for i in range(2):
            prev = st["vstage_free"][i]
            for q4 in range(4):
                prev = vec(lambda e, i=i, q4=q4: e.memset(
                    vstage[i].ap()[:, q4, :].rearrange("p (h c) -> p h c", c=65)
                    [:, :, 64:65], 1.0), waits=[prev])
            ones_done[i] = prev
        v_contrib_toks = []
        ev_last = None
        for vh in range(2):
            buf, slot = vs.load(din[pre + "wv"][:, vh * 512:(vh + 1) * 512]
                                .rearrange("(g p) c -> p g c", p=128))
            ld = vs.ld[slot]
            for tb in range(NTB):
                bi = 2 + tb % 2
                w = bk_acq(bi, [ld] + xn_tok)
                t = None
                for fc in range(FCH):
                    last = (fc == FCH - 1) and l0
                    t = mm(pbk[bi][:, 0:512], xn[fc][:, tb * 128:(tb + 1) * 128],
                           buf[:, fc, :], start=(fc == 0), stop=last,
                           waits=w if fc == 0 else (), inc="TE" if last else None)
                if not l0:
                    t = mm(pbk[bi][:, 0:512], ones_r1_bf[0:1, :],
                           bv_sb[0:1, vh * 512:(vh + 1) * 512], False, True,
                           waits=[cstg], inc="TE")
                si = tb % 2
                sw = [t, ones_done[si]]
                if len(v_contrib_toks) >= 2:
                    sw.append(v_contrib_toks[-2])
                ev = vec(lambda e, si=si, bi=bi, vh=vh: e.tensor_copy(
                    vstage[si].ap()[:, 2 * vh:2 * vh + 2, :]
                    .rearrange("p q (h c) -> p (q h) c", c=65)[:, :, 0:64],
                    pbk[bi][:, 0:512]), waits=sw)
                bk_read(bi, ev)
                cw = [st["contrib_free"]] if st["contrib_free"] else []
                vc = dma("sync",
                         v_contrib.rearrange("(q t p) c -> p q t c", q=4, p=128)
                         [:, 2 * vh:2 * vh + 2, tb, :],
                         vstage[si][:, 2 * vh:2 * vh + 2, :], waits=[ev] + cw,
                         sem=f"vc{si}")
                v_contrib_toks.append(vc)
                ev_last = ev
            vs.release(slot, ev_last)
        st["vstage_free"] = [v_contrib_toks[-2], v_contrib_toks[-1]]

        coll_w = k_contrib_toks + [v_contrib_toks[-1], v_contrib_toks[-2]]
        if st["kgath_free"]:
            coll_w.append(st["kgath_free"])
        ck = P.emit("gpsimd", lambda e: e.collective_compute(
            "AllGather", ALU.bypass, ins=[k_contrib.opt()], outs=[k_gath.opt()],
            replica_groups=[[0, 1, 2, 3], [4, 5, 6, 7]]), waits=coll_w, inc="CCK")
        cv_w = [v_contrib_toks[-1]]
        if st["vgath_free"]:
            cv_w.append(st["vgath_free"])
        cv = P.emit("gpsimd", lambda e: e.collective_compute(
            "AllGather", ALU.bypass, ins=[v_contrib.opt()], outs=[v_gath.opt()],
            replica_groups=[[0, 1, 2, 3], [4, 5, 6, 7]]), waits=cv_w, inc="CCV")
        st["contrib_free"] = cv
        return ck, cv

    def emit_ffn_l0(xn_tok):
        gs, us, ds = st["gs"], st["us"], st["ds"]
        NHB = cfg.NHB_F
        h_tok = [None] * NHB
        g_free = [None, None]
        for tp in range(NTP):
            for hb in range(NHB):
                b1, s1 = gs.load(din["l0_wg"][:, hb * 128:(hb + 1) * 128]
                                 .rearrange("(g p) c -> p g c", p=128))
                b2, s2 = us.load(din["l0_wu"][:, hb * 128:(hb + 1) * 128]
                                 .rearrange("(g p) c -> p g c", p=128))
                for tbl in range(TP // CB):
                    col0 = tp * TP + tbl * CB
                    bg, bu = tbl % 2, 2 + tbl % 2
                    w = bk_acq(bg, [gs.ld[s1]] + xn_tok)
                    tg = None
                    for fc in range(FCH):
                        tg = mm(pbk[bg][:, 0:CB], b1[:, fc, :],
                                xn[fc][:, col0:col0 + CB],
                                start=(fc == 0), stop=(fc == FCH - 1),
                                waits=w if fc == 0 else (),
                                inc="TE" if fc == FCH - 1 else None)
                    w = bk_acq(bu, [us.ld[s2]])
                    tu = None
                    for fc in range(FCH):
                        tu = mm(pbk[bu][:, 0:CB], b2[:, fc, :],
                                xn[fc][:, col0:col0 + CB],
                                start=(fc == 0), stop=(fc == FCH - 1),
                                waits=w if fc == 0 else (),
                                inc="TE" if fc == FCH - 1 else None)
                    gi = tbl % 2
                    ga = act(lambda e, gi=gi, bg=bg: e.activation(
                        g_sb[gi][:, 0:CB], pbk[bg][:, 0:CB], AF.Silu,
                        scale=1.0 / 127.0), waits=[tg, g_free[gi]])
                    bk_read(bg, ga)
                    hw = [tu, ga]
                    if tp == 0 and tbl == 0:
                        if hb < HCAP:
                            hw.append(st["q2_free"][hb * TP // TPC])
                        elif st["arena_free"]:
                            hw.append(st["arena_free"])
                    ht = vec(lambda e, hb=hb, tbl=tbl, bu=bu, gi=gi:
                             e.scalar_tensor_tensor(
                                 h_l0[hb][:, tbl * CB:(tbl + 1) * CB],
                                 pbk[bu][:, 0:CB], 1.0 / 127.0, g_sb[gi][:, 0:CB],
                                 ALU.mult, ALU.mult), waits=hw)
                    bk_read(bu, ht)
                    g_free[gi] = ht
                    h_tok[hb] = ht
                gs.release(s1, h_tok[hb])
                us.release(s2, h_tok[hb])
            for ob in range(FCH):
                b3, s3 = ds.load(din["l0_wd"][:, ob * 128:(ob + 1) * 128]
                                 .rearrange("(g p) c -> p g c", p=128))
                last_at = None
                for tbl in range(TP // CB):
                    col0 = tp * TP + tbl * CB
                    bd = 4 + tbl % 2
                    w = bk_acq(bd, [ds.ld[s3]] + h_tok)
                    t = None
                    for hb in range(NHB):
                        t = mm(pbk[bd][:, 0:CB], b3[:, hb, :],
                               h_l0[hb][:, tbl * CB:(tbl + 1) * CB],
                               start=(hb == 0), stop=(hb == NHB - 1),
                               waits=w if hb == 0 else (),
                               inc="TE" if hb == NHB - 1 else None)
                    at = vec(lambda e, ob=ob, col0=col0, bd=bd:
                             e.scalar_tensor_tensor(
                                 xacc[ob][:, col0:col0 + CB], pbk[bd][:, 0:CB],
                                 1.0 / 127.0, xacc[ob][:, col0:col0 + CB],
                                 ALU.mult, ALU.add),
                             waits=[t, st["xacc"][ob]])
                    bk_read(bd, at)
                    st["xacc"][ob] = at
                    last_at = at
                ds.release(s3, last_at)
        return

    def emit_moe(xn_tok):
        NHB = cfg.NHB_M
        arena_w = [st["arena_free"]] if st["arena_free"] else []
        cT_toks = []
        rprev = None
        for tb in range(NTB):
            w = bk_acq(7, xn_tok + [cstg] + arena_w)
            t = None
            for fc in range(FCH):
                t = mm(pbk[7][:, 0:8], xn[fc][:, tb * 128:(tb + 1) * 128],
                       rw_sb[:, fc * 8:(fc + 1) * 8],
                       start=(fc == 0), stop=(fc == FCH - 1),
                       waits=w if fc == 0 else (), inc="TE" if fc == FCH - 1 else None)
            c0 = vec(lambda e: e.tensor_copy(lg_sb[:], pbk[7][:, 0:8]),
                     waits=[t, rprev])
            bk_read(7, c0)
            t1 = vec(lambda e: e.max(r_m8[:], lg_sb[:]), waits=[c0])
            t2 = vec(lambda e: e.max_index(r_i8[:], r_m8[:], lg_sb[:]), waits=[t1])
            t2b = vec(lambda e: e.tensor_copy(r_if[:], r_i8[:, 0:2]), waits=[t2])
            t3 = vec(lambda e: e.tensor_single_scalar(r_oh1[:], iota_e[:],
                                                      r_if[:, 0:1], ALU.is_equal),
                     waits=[t2b, cst])
            t7 = vec(lambda e: e.tensor_single_scalar(r_oh2[:], iota_e[:],
                                                      r_if[:, 1:2], ALU.is_equal),
                     waits=[t3])
            t8 = vec(lambda e: e.tensor_scalar_mul(r_neg[:], r_m8[:, 0:1], -1.0),
                     waits=[t1])
            t9 = act(lambda e: e.activation(r_p[:], lg_sb[:], AF.Exp,
                                            bias=r_neg[:, 0:1]), waits=[t8, t2])
            ta = vec(lambda e: e.reduce_sum(r_s[:], r_p[:], AX.X), waits=[t9])
            tb_ = vec(lambda e: e.reciprocal(r_t[:], r_s[:]), waits=[ta])
            tc = vec(lambda e: e.tensor_scalar_mul(r_p[:], r_p[:], r_t[:, 0:1]),
                     waits=[tb_])
            td = vec(lambda e: e.tensor_add(r_oh2[:], r_oh1[:], r_oh2[:]),
                     waits=[t7, tc])
            te_ = vec(lambda e: e.tensor_mul(r_p[:], r_p[:], r_oh2[:]), waits=[td])
            tf = vec(lambda e: e.reduce_sum(r_s[:], r_p[:], AX.X), waits=[te_])
            tg = vec(lambda e: e.tensor_scalar_add(r_s[:], r_s[:], 1e-9), waits=[tf])
            th = vec(lambda e: e.reciprocal(r_t[:], r_s[:]), waits=[tg])
            ti = vec(lambda e: e.tensor_scalar_mul(r_cf[:], r_p[:], r_t[:, 0:1]),
                     waits=[th])
            w = bk_acq(6, [ti, cst])
            tj = P.emit("tensor", lambda e: e.transpose(pbk[6][0:8, 0:128],
                                                        r_cf[:], ident[:]),
                        waits=w, inc="TE")
            tk = vec(lambda e, tb=tb: e.tensor_copy(
                cT_sb[0:8, tb * 128:(tb + 1) * 128], pbk[6][0:8, 0:128]),
                waits=[tj] + arena_w)
            bk_read(6, tk)
            tl = vec(lambda e, tb=tb: e.tensor_copy(
                cTb_sb[0:8, tb * 128:(tb + 1) * 128],
                cT_sb[0:8, tb * 128:(tb + 1) * 128]), waits=[tk])
            cT_toks.append(tl)
            rprev = ti

        if gate_tap is not None:
            dma("sync", gate_tap, cT_sb[0:8, :], waits=cT_toks, sem="OUT")
        m1s, m2s = st["m1s"], st["m2s"]
        h_tok = [None] * NHB
        gb_tok = [None]
        for e_i in range(E):
            for tp in range(NTP):
                gi = 0
                gt = None
                for yw in range(TP // YW):
                    w = bk_acq(6, cT_toks + ([gb_tok[gi]] if gb_tok[gi] else [])
                               + (arena_w if e_i == 0 else []))
                    t = mm(pbk[6][:, 0:YW],
                           sel8_sb[0:8, e_i * 128:(e_i + 1) * 128],
                           cT_sb[0:8, tp * TP + yw * YW:tp * TP + (yw + 1) * YW],
                           True, True, waits=w, inc="TE")
                    gt = vec(lambda e, gi=gi, yw=yw: e.tensor_copy(
                        gb_sb[gi][:, yw * YW:(yw + 1) * YW], pbk[6][:, 0:YW]),
                        waits=[t])
                    bk_read(6, gt)
                gb_tok[gi] = gt
                for hb in range(NHB):
                    b1, s1 = m1s.load(din["l1_w1"][e_i, :, hb * 128:(hb + 1) * 128]
                                      .rearrange("(g p) c -> p g c", p=128))
                    for tbl in range(TP // CB):
                        col0 = tp * TP + tbl * CB
                        bi = tbl % 2
                        w = bk_acq(bi, [m1s.ld[s1]] + xn_tok)
                        for fc in range(FCH):
                            mm(pbk[bi][:, 0:CB], b1[:, fc, :],
                               xn[fc][:, col0:col0 + CB],
                               start=(fc == 0), stop=False,
                               waits=w if fc == 0 else ())
                        t = mm(pbk[bi][:, 0:CB], b1[0:1, FCH, :],
                               ones_row_bf[0:1, 0:CB], False, True,
                               waits=[cst], inc="TE")
                        hw = [t, gb_tok[gi]]
                        if e_i == 0 and tp == 0 and tbl == 0:
                            if hb < HCAP:
                                hw.append(st["q2_free"][hb * TP // TPC])
                            elif st["arena_free"]:
                                hw.append(st["arena_free"])
                        ht = vec(lambda e, hb=hb, tbl=tbl, bi=bi, gi=gi:
                                 e.scalar_tensor_tensor(
                                     h_moe[hb][:, tbl * CB:(tbl + 1) * CB],
                                     pbk[bi][:, 0:CB], 0.0,
                                     gb_sb[gi][:, tbl * CB:(tbl + 1) * CB],
                                     ALU.max, ALU.mult), waits=hw)
                        bk_read(bi, ht)
                        h_tok[hb] = ht
                    m1s.release(s1, h_tok[hb])
                if h_tap is not None and e_i == 3 and tp == 0:
                    for hb in range(NHB):
                        dma("gpsimd", h_tap[hb * 128:(hb + 1) * 128, :],
                            h_moe[hb][:], waits=[h_tok[hb]], sem="HT")
                    htt = ("HT", P.cnt["HT"])
                    h_tok = [htt] * NHB
                ytiles = [(cb, th) for cb in range(4) for th in range(NYW)]
                for ch in range(2):
                    if e_i == 0:
                        for (cb, th) in ytiles:
                            bi = cb * NYW + th
                            w = bk_acq(bi, cT_toks + [cstg])
                            mm(pbk[bi][:, 0:YW],
                               b2_sb[0:8, ch * 512 + cb * 128:ch * 512 + (cb + 1) * 128],
                               cTb_sb[0:8, tp * TP + th * YW:tp * TP + (th + 1) * YW],
                               True, False, waits=w)
                    last = {}
                    for hbg in range(NHB // 2):
                        b2w, s2 = m2s.load(
                            din["l1_w2"][e_i, hbg * 256:(hbg + 1) * 256,
                                         ch * 512:(ch + 1) * 512]
                            .rearrange("(g p) c -> p g c", p=128))
                        slab_last = None
                        for g4 in range(2):
                            hb = hbg * 2 + g4
                            for (cb, th) in ytiles:
                                bi = cb * NYW + th
                                isend = (hb == NHB - 1)
                                isfirst = (e_i > 0 and hb == 0)
                                w = [m2s.ld[s2], h_tok[hb]] if (cb, th) == ytiles[0] else ()
                                if isfirst:
                                    w = bk_acq(bi, list(w) if w else [])
                                t = mm(pbk[bi][:, 0:YW],
                                       b2w[:, g4, cb * 128:(cb + 1) * 128],
                                       h_moe[hb][:, th * YW:(th + 1) * YW],
                                       isfirst, isend, waits=w,
                                       inc="TE" if isend or ((g4 == 1) and
                                                             (cb, th) == ytiles[-1])
                                       else None)
                                if isend:
                                    last[(cb, th)] = t
                                if (g4 == 1) and (cb, th) == ytiles[-1]:
                                    slab_last = t
                        m2s.release(s2, slab_last)
                    for (cb, th) in ytiles:
                        bi = cb * NYW + th
                        ob = ch * 4 + cb
                        col0 = tp * TP + th * YW
                        at = vec(lambda e, ob=ob, col0=col0, bi=bi:
                                 e.tensor_add(xacc[ob][:, col0:col0 + YW],
                                              pbk[bi][:, 0:YW],
                                              xacc[ob][:, col0:col0 + YW]),
                                 waits=[last[(cb, th)], st["xacc"][ob]])
                        bk_read(bi, at)
                        st["xacc"][ob] = at
        return

    def emit_attn(lid, ck, cv, xn_dead):
        l0 = (lid == 0)
        pre = f"l{lid}_"
        for r in range(4):
            dma("gpsimd", kn4_sb[0:1, r * 8:(r + 1) * 8],
                k_gath[r * (D + 1) + D:r * (D + 1) + D + 1, 0:8],
                waits=[ck], sem="K4")
        kn_toks = [("K4", P.cnt["K4"])]
        t1 = vec(lambda e: e.reduce_max(
            msc[0:1, 0:8],
            kn4_sb.ap()[0:1, 0:32].rearrange("a (r h) -> a h r", r=4), AX.X),
            waits=kn_toks)
        t2 = vec(lambda e: e.tensor_mul(msc[0:1, 0:8], msc[0:1, 0:8],
                                        qn_sb[0:1, 0:8]), waits=[t1])
        t3 = vec(lambda e: e.reduce_max(msc[0:1, 0:1], msc[0:1, 0:8], AX.X),
                 waits=[t2])
        t4 = act(lambda e: e.activation(msc[0:1, 0:1], msc[0:1, 0:1], AF.Ln),
                 waits=[t3])
        t5 = act(lambda e: e.activation(msc[0:1, 0:1], msc[0:1, 0:1], AF.Exp,
                                        scale=0.5), waits=[t4])
        t6 = vec(lambda e: e.tensor_scalar(negM1[0:1, 0:1], msc[0:1, 0:1],
                                           -0.1275, -1.0, ALU.mult, ALU.add),
                 waits=[t5])
        w = bk_acq(6, [t6, cst])
        t7 = mm(pbk[6][:, 0:1], ones_r1[0:1, :], negM1[0:1, 0:1], True, True, w,
                inc="TE")
        negM_tok = vec(lambda e: e.tensor_copy(negM[:], pbk[6][:, 0:1]), waits=[t7])
        bk_read(6, negM_tok)

        unpark = []
        for hp in range(8):
            w = [st["q2_free"][hp], st["h_free"]]
            unpark.append(dma("sync", q2[hp][:], q_park[hp * 128:(hp + 1) * 128, :],
                              waits=w, sem=f"up{hp}"))
        st["qpark_free"] = unpark[-1]

        os_ = st["ostream"]
        gate = st["h_free"]
        os_.gate(gate)
        o_last = None
        kg_last = None
        vg_last = None
        rec_chain = gate
        recb_chain = gate
        attn_gate = gate
        pbuf_tok_g = [gate] * 4
        ptmp_tok_g = [gate]
        for quad in range(4):
            k_toks = {}
            for hpi in range(2):
                hp = quad * 2 + hpi
                for r in range(4):
                    ki = hpi * 4 + r
                    w = [ck] + ([st["B_free"][ki]] if st["B_free"][ki] else [])
                    if quad == 0:
                        w += xn_dead
                    t = dma("sync", kwin[ki][:],
                            k_gath[r * (D + 1) + hp * 128:
                                   r * (D + 1) + (hp + 1) * 128, :], waits=w,
                            sem=f"kw{ki}")
                    k_toks[(hpi, r)] = t
                    kg_last = t
            v_toks = {}
            for r in range(4):
                w = [cv, attn_gate]
                t = dma("sync", vw[r][:],
                        v_gath.rearrange("(r q t p) c -> p r q t c", r=4, q=4, p=128)
                        [:, r, quad, :, :], waits=w, sem=f"vq{r}")
                v_toks[r] = t
                vg_last = t
            oz_toks = []
            z_free = [attn_gate, attn_gate]
            for hpi in range(2):
                hp = quad * 2 + hpi
                q2_last = None
                for qb in range(NQB):
                    diag_kb = qb * QW // 128
                    if l0:
                        kl = [(r, kb) for r in range(4) for kb in range(diag_kb + 1)]
                    else:
                        kl = [(r, kb) for r in range(4) for kb in range(KTPR)]
                    zA, zB = 4, 5
                    pbuf_tok = pbuf_tok_g
                    ptmp_tok = ptmp_tok_g[0]
                    zlast = [None, None]
                    for n_i, (r, kb) in enumerate(kl):
                        sA, sB = n_i % 2, 2 + n_i % 2
                        w = bk_acq(sA, [k_toks[(hpi, r)], unpark[hp]])
                        ta = mm(pbk[sA][:, 0:QW],
                                kwin[hpi * 4 + r][0:64, kb * 128:(kb + 1) * 128],
                                q2[hp][0:64, qb * QW:(qb + 1) * QW],
                                True, True, w, inc="TE")
                        w = bk_acq(sB, [k_toks[(hpi, r)]])
                        tb_ = mm(pbk[sB][:, 0:QW],
                                 kwin[hpi * 4 + r][64:128, kb * 128:(kb + 1) * 128],
                                 q2[hp][64:128, qb * QW:(qb + 1) * QW],
                                 True, True, w, inc="TE")
                        q2_last = tb_
                        masked = l0 and (kb == diag_kb)
                        for idx, (sbk, tmm) in enumerate(((sA, ta), (sB, tb_))):
                            pi = idx * 2 + n_i % 2
                            ew = [tmm, negM_tok, pbuf_tok[pi]]
                            if masked:
                                et = act(lambda e, sbk=sbk: e.activation(
                                    ptmp[:, 0:QW], pbk[sbk][:, 0:QW], AF.Exp,
                                    bias=negM[:, 0:1], scale=0.125),
                                    waits=[tmm, negM_tok, ptmp_tok])
                                pdone = vec(lambda e, pi=pi, r=r: e.tensor_mul(
                                    p_buf[pi][:], ptmp[:, 0:QW],
                                    masks_sb[:, r * QW:(r + 1) * QW]),
                                    waits=[et, pbuf_tok[pi], cst])
                                ptmp_tok = pdone
                                ptmp_tok_g[0] = pdone
                            else:
                                pdone = act(lambda e, sbk=sbk, pi=pi: e.activation(
                                    p_buf[pi][:], pbk[sbk][:, 0:QW], AF.Exp,
                                    bias=negM[:, 0:1], scale=0.125), waits=ew)
                            bk_read(sbk, pdone)
                            zb = zA if idx == 0 else zB
                            hloc = 2 * hpi + idx
                            w2 = [pdone, v_toks[r]]
                            if n_i == 0:
                                w2 = bk_acq(zb, w2)
                            zt = mm(pbk[zb][0:65, 0:QW],
                                    vw[r][:, kb, hloc * 65:(hloc + 1) * 65],
                                    p_buf[pi][:],
                                    start=(n_i == 0), stop=(n_i == len(kl) - 1),
                                    waits=w2, inc="TE")
                            pbuf_tok[pi] = zt
                            if n_i == len(kl) - 1:
                                zlast[idx] = zt
                    for zi, zb in ((0, zA), (1, zB)):
                        e1 = act(lambda e, zb=zb: e.activation(
                            recrow[0:1, 0:QW], pbk[zb][64:65, 0:QW], AF.Ln),
                            waits=[zlast[zi], rec_chain])
                        e2 = act(lambda e: e.activation(
                            recrow[0:1, 0:QW], recrow[0:1, 0:QW], AF.Exp,
                            scale=-1.0), waits=[e1])
                        w = bk_acq(6, [e2])
                        e3 = mm(pbk[6][0:64, 0:QW], ones_r1[0:1, 0:64],
                                recrow[0:1, 0:QW], True, True, w, inc="TE")
                        rec_chain = e3
                        e4 = vec(lambda e: e.tensor_copy(recb[:, 0:QW],
                                                         pbk[6][0:64, 0:QW]),
                                 waits=[e3, recb_chain])
                        bk_read(6, e4)
                        e5 = vec(lambda e, zb=zb, hpi=hpi, zi=zi, qb=qb:
                                 e.tensor_mul(
                                     z_sb[hpi][zi * 64:(zi + 1) * 64,
                                               qb * QW:(qb + 1) * QW],
                                     pbk[zb][0:64, 0:QW], recb[:, 0:QW]),
                                 waits=[e4, zlast[zi], z_free[hpi]])
                        bk_read(zb, e5)
                        recb_chain = e5
                        oz_toks.append(e5)
                st["q2_free"][hp] = q2_last
            buf0, s0 = os_.load(din[pre + "wo"]
                                [(quad * 2) * 128:(quad * 2 + 1) * 128, :])
            buf1, s1 = os_.load(din[pre + "wo"]
                                [(quad * 2 + 1) * 128:(quad * 2 + 2) * 128, :])
            for ob in range(FCH):
                for tbl in range(NCB):
                    bo_i = 6 + tbl % 2
                    col0 = tbl * CB
                    w = bk_acq(bo_i, [os_.ld[s0], os_.ld[s1]] + oz_toks)
                    mm(pbk[bo_i][:, 0:CB], buf0[:, ob * 128:(ob + 1) * 128],
                       z_sb[0][:, col0:col0 + CB], True, False, w)
                    t = mm(pbk[bo_i][:, 0:CB], buf1[:, ob * 128:(ob + 1) * 128],
                           z_sb[1][:, col0:col0 + CB], False, True, (), inc="TE")
                    if (not l0) and quad == 0:
                        at = vec(lambda e, ob=ob, col0=col0, bo_i=bo_i:
                                 e.scalar_tensor_tensor(
                                     xacc[ob][:, col0:col0 + CB],
                                     pbk[bo_i][:, 0:CB], bo_sb[:, ob:ob + 1],
                                     xacc[ob][:, col0:col0 + CB],
                                     ALU.add, ALU.add),
                                 waits=[t, st["xacc"][ob], cst])
                    else:
                        at = vec(lambda e, ob=ob, col0=col0, bo_i=bo_i:
                                 e.tensor_add(xacc[ob][:, col0:col0 + CB],
                                              pbk[bo_i][:, 0:CB],
                                              xacc[ob][:, col0:col0 + CB]),
                                 waits=[t, st["xacc"][ob]])
                    bk_read(bo_i, at)
                    st["xacc"][ob] = at
                    o_last = at
            os_.release(s0, o_last)
            os_.release(s1, o_last)
            for ki in range(8):
                st["B_free"][ki] = o_last
            attn_gate = o_last
        st["kgath_free"] = kg_last
        st["vgath_free"] = vg_last
        return o_last

    for lid in layers:
        xn_tok = emit_ln(lid)
        ck, cv = emit_qkv(lid, xn_tok)
        if lid == 0:
            emit_ffn_l0(xn_tok)
        else:
            emit_moe(xn_tok)
        lastacc = max(st["xacc"], key=lambda t: t[1])
        st["h_free"] = lastacc
        if f"mid{lid}" in taps:
            for fc in range(FCH):
                dma("sync", taps[f"mid{lid}"][fc * 128:(fc + 1) * 128, :],
                    xacc[fc][:], waits=[st["xacc"][fc]], sem="OUT")
            ttok = ("OUT", P.cnt["OUT"])
            st["xacc"] = [ttok] * FCH
        xn_dead = [lastacc]
        o_last = emit_attn(lid, ck, cv, xn_dead)
        st["arena_free"] = o_last
        if f"x{lid}" in taps:
            for fc in range(FCH):
                dma("sync", taps[f"x{lid}"][fc * 128:(fc + 1) * 128, :], xacc[fc][:],
                    waits=[st["xacc"][fc]], sem="OUT")

    for fc in range(FCH):
        dma("sync", out_fm[fc * 128:(fc + 1) * 128, :], xacc[fc][:],
            waits=[st["xacc"][fc]], sem="OUT")
    final_v = P.cnt["OUT"]
    P.emit("sync", lambda e: e.nop(), waits=[("OUT", final_v)])

    import contextlib
    ctx = contextlib.ExitStack()
    sems = {s: ctx.enter_context(nc.semaphore(s)) for s in sorted(P.sems)}
    runners = P.replay_into(sems)
    with nc.Block() as blk:
        blk.tensor(lambda e: runners["tensor"](e))
        blk.vector(lambda e: runners["vector"](e))
        blk.scalar(lambda e: runners["scalar"](e))
        blk.sync(lambda e: runners["sync"](e))
        blk.gpsimd(lambda e: runners["gpsimd"](e))
    ctx.close()
    return nc


_BUILD_CACHE = {}


def kernel(**inputs):
    cfg = CFG()
    in_maps, idxs = host_prepare(cfg, inputs)
    key = (cfg.S, cfg.FH, cfg.MH, cfg.layers)
    if key not in _BUILD_CACHE:
        _BUILD_CACHE[key] = build_program(cfg, cfg.layers)
    nc = _BUILD_CACHE[key]
    res = run_bass_kernel_spmd(nc, in_maps, core_ids=list(range(8)))
    return host_assemble(cfg, res.results, idxs)


# version 03
# speedup vs baseline: 1.1948x; 1.1948x over previous
import numpy as np
import ml_dtypes
from collections import defaultdict

import concourse.bass as bass
import concourse.mybir as mybir
from concourse.bass_utils import run_bass_kernel_spmd

F32 = mybir.dt.float32
BF16 = mybir.dt.bfloat16
I32 = mybir.dt.int32
AF = mybir.ActivationFunctionType
ALU = mybir.AluOpType
AX = mybir.AxisListType
BF16NP = ml_dtypes.bfloat16


class CFG:
    def __init__(self, S=8192, FH=3072, MH=4096, layers=(0, 1), debug_taps=()):
        self.S, self.FH, self.MH = S, FH, MH
        self.D, self.H, self.DH, self.E, self.B, self.NC = 1024, 16, 64, 8, 2, 8
        self.layers = tuple(layers)
        self.debug_taps = tuple(debug_taps)
        self.TPC = self.S * self.B // self.NC
        self.CB = min(512, self.TPC)
        self.NCB = self.TPC // self.CB
        self.QW = self.CB
        self.NQB = self.TPC // self.QW
        self.NTB = self.TPC // 128
        self.TP = min(1024, self.TPC)
        self.NTP = self.TPC // self.TP
        self.YW = min(512, self.TP)
        self.NYW = self.TP // self.YW
        self.FCH = self.D // 128
        self.NHB_F = self.FH // 128
        self.NHB_M = self.MH // 128
        self.KTPR = self.TPC // 128
        assert self.FH % 256 == 0 and self.MH % 512 == 0 and self.TPC % 128 == 0


ENGINES = ("tensor", "vector", "scalar", "sync", "gpsimd")


def tmax(*toks):
    best = {}
    for t in toks:
        if t is None:
            continue
        s, v = t
        if v > best.get(s, -1):
            best[s] = v
    return [(s, v) for s, v in best.items()]


class Prog:
    def __init__(self):
        self.ev = {e: [] for e in ENGINES}
        self.cnt = defaultdict(int)
        self.sems = set()
        self.bank_r = {}

    def emit(self, eng, fn, waits=(), inc=None, amt=1):
        w = [t for t in waits if t is not None]
        tok = None
        if inc is not None:
            self.sems.add(inc)
            self.cnt[inc] += amt
            tok = (inc, self.cnt[inc])
        self.ev[eng].append((fn, w, inc, amt))
        return tok

    def replay_into(self, sems):
        waited = {e: defaultdict(int) for e in ENGINES}

        def mk(eng_name):
            def run(eng):
                for fn, waits, inc, amt in self.ev[eng_name]:
                    for s, v in waits:
                        if v > waited[eng_name][s]:
                            eng.wait_ge(sems[s], v)
                            waited[eng_name][s] = v
                    inst = fn(eng)
                    if inc is not None:
                        inst.then_inc(sems[inc], amt)
            return run
        return {e: mk(e) for e in ENGINES}


class Stream:

    def __init__(self, P, q, tag, bufs):
        self.P, self.q, self.bufs = P, q, bufs
        self.tag = tag
        self.i = 0
        self.free = [[] for _ in bufs]
        self.ld = [None] * len(bufs)

    def load(self, src_ap):
        s = self.i % len(self.bufs)
        self.i += 1
        buf = self.bufs[s]
        self.ld[s] = self.P.emit(
            self.q, lambda e, d=buf, src=src_ap: e.dma_start(out=d[:], in_=src),
            waits=self.free[s], inc=f"{self.tag}{s}", amt=16)
        return buf, s

    def release(self, s, tok):
        self.free[s] = [tok]

    def gate(self, tok):
        self.free = [[tok] for _ in self.bufs]



def host_prepare(cfg, inp):
    D, H, DH, E, S = cfg.D, cfg.H, cfg.DH, cfg.E, cfg.S
    sh = {}
    if 0 in cfg.layers:
        sh["l0_wq"] = np.ascontiguousarray(
            np.asarray(inp["L0_attn_W_Q"]).transpose(2, 0, 1).reshape(D, H * DH))
        sh["l0_wk"] = np.ascontiguousarray(
            np.asarray(inp["L0_attn_W_K"]).transpose(2, 0, 1).reshape(D, H * DH))
        sh["l0_wv"] = np.ascontiguousarray(
            np.asarray(inp["L0_attn_W_V"]).transpose(2, 0, 1).reshape(D, H * DH))
        sh["l0_wo"] = np.ascontiguousarray(np.asarray(inp["L0_attn_W_O"]).T)
        sh["l0_wg"] = np.ascontiguousarray(np.asarray(inp["L0_ffn_w_gate"]).T)
        sh["l0_wu"] = np.ascontiguousarray(np.asarray(inp["L0_ffn_w_up"]).T)
        sh["l0_wd"] = np.ascontiguousarray(np.asarray(inp["L0_ffn_w_down"]).T)
    if 1 in cfg.layers:
        wqkv = np.asarray(inp["L1_attn_Wqkv"]).reshape(H, 3 * DH, D)
        bqkv = np.asarray(inp["L1_attn_bqkv"]).reshape(H, 3 * DH)
        sh["l1_wq"] = np.ascontiguousarray(
            wqkv[:, 0:DH, :].transpose(2, 0, 1).reshape(D, H * DH))
        sh["l1_wk"] = np.ascontiguousarray(
            wqkv[:, DH:2 * DH, :].transpose(2, 0, 1).reshape(D, H * DH))
        sh["l1_wv"] = np.ascontiguousarray(
            wqkv[:, 2 * DH:, :].transpose(2, 0, 1).reshape(D, H * DH))
        sh["l1_bq"] = np.ascontiguousarray(bqkv[:, 0:DH].reshape(8, 128).T)
        sh["l1_bk"] = np.ascontiguousarray(bqkv[:, DH:2 * DH].reshape(8, 128).T)
        sh["l1_bv"] = np.ascontiguousarray(bqkv[:, 2 * DH:].reshape(1, H * DH))
        sh["l1_wo"] = np.ascontiguousarray(np.asarray(inp["L1_attn_Wo"]).T)
        sh["l1_bo"] = np.ascontiguousarray(
            np.asarray(inp["L1_attn_bo"]).reshape(8, 128).T)
        sh["l1_rw"] = np.ascontiguousarray(np.asarray(inp["L1_ffn_router_w"]).T)
        w1 = np.asarray(inp["L1_ffn_w1"])
        b1 = np.asarray(inp["L1_ffn_b1"])
        w2 = np.asarray(inp["L1_ffn_w2"])
        w1aug = np.zeros((E, D + 128, cfg.MH), np.float32)
        for e in range(E):
            w1aug[e, :D, :] = w1[e].T
            w1aug[e, D, :] = b1[e]
        sh["l1_w1"] = w1aug
        sh["l1_w2"] = np.ascontiguousarray(w2.transpose(0, 2, 1))
        sh["l1_b2"] = np.ascontiguousarray(np.asarray(inp["L1_ffn_b2"]))

    sh["fconst"] = np.broadcast_to(
        np.array([0.0, 1.0, 1e-5, -0.5, 0.125, 0.5, -1.0, 1.0 / 127.0],
                 np.float32), (128, 8)).copy()
    sh["ones_col"] = np.ones((128, 1), np.float32)
    sh["ones_r1"] = np.ones((1, 128), np.float32)
    sh["ones_r1_bf"] = np.ones((1, 128), BF16NP)
    sh["ones_row_bf"] = np.ones((1, 512), BF16NP)
    sh["iota_e"] = np.broadcast_to(np.arange(8, dtype=np.float32), (128, 8)).copy()
    sel8 = np.zeros((8, 8 * 128), np.float32)
    for e in range(8):
        sel8[e, e * 128:(e + 1) * 128] = 1.0
    sh["sel8"] = sel8
    sh["ident"] = np.eye(128, dtype=np.float32)

    x = np.asarray(inp["L0_attn_x"])
    in_maps, idxs = [], []
    kk = np.arange(128)[:, None]
    qq = np.arange(cfg.QW)[None, :]
    for c in range(8):
        b, j = divmod(c, 4)
        idx = j + 4 * np.arange(cfg.TPC)
        idxs.append((b, idx))
        m = dict(sh)
        m["x_fm"] = np.ascontiguousarray(x[b, idx, :].T)
        masks = np.zeros((128, 4 * cfg.QW), BF16NP)
        for r in range(4):
            d = 0 if r <= j else 1
            masks[:, r * cfg.QW:(r + 1) * cfg.QW] = (kk <= qq - d)
        m["masks"] = masks
        in_maps.append(m)
    return in_maps, idxs


def host_assemble(cfg, results, idxs, key="out_fm"):
    out = np.zeros((cfg.B, cfg.S, cfg.D), np.float32)
    for c in range(8):
        b, idx = idxs[c]
        out[b, idx, :] = results[c][key].T
    return out


OFF_END = 212832


def build_program(cfg, layers):
    nc = bass.Bass("TRN2", target_bir_lowering=False, debug=False)
    D, H, DH, E, TPC = cfg.D, cfg.H, cfg.DH, cfg.E, cfg.TPC
    FCH, CB, NCB, QW, NQB = cfg.FCH, cfg.CB, cfg.NCB, cfg.QW, cfg.NQB
    TP, NTP, YW, NYW, KTPR = cfg.TP, cfg.NTP, cfg.YW, cfg.NYW, cfg.KTPR
    NTB = cfg.NTB

    OFF_A = 0
    OFF_B = OFF_A + FCH * TPC * 4
    OFF_C = OFF_B + FCH * TPC * 2
    OFF_D = OFF_C + FCH * TPC * 2
    OFF_SM = OFF_END - 19456

    din = {}
    def dram_in(name, shape, dt):
        din[name] = nc.dram_tensor(name, list(shape), dt, kind="ExternalInput").ap()

    dram_in("x_fm", (D, TPC), F32)
    dram_in("fconst", (128, 8), F32)
    for n, s, dt in [("ones_col", (128, 1), F32), ("ones_r1", (1, 128), F32),
                     ("ones_r1_bf", (1, 128), BF16), ("ones_row_bf", (1, 512), BF16),
                     ("iota_e", (128, 8), F32), ("ident", (128, 128), F32),
                     ("masks", (128, 4 * QW), BF16), ("sel8", (8, 1024), F32)]:
        dram_in(n, s, dt)
    if 0 in layers:
        for n, s, dt in [("l0_wq", (D, D), F32), ("l0_wk", (D, D), F32),
                         ("l0_wv", (D, D), F32), ("l0_wo", (D, D), F32),
                         ("l0_wg", (D, cfg.FH), I32), ("l0_wu", (D, cfg.FH), I32),
                         ("l0_wd", (cfg.FH, D), I32)]:
            dram_in(n, s, dt)
    if 1 in layers:
        for n, s, dt in [("l1_wq", (D, D), F32), ("l1_wk", (D, D), F32),
                         ("l1_wv", (D, D), F32), ("l1_bq", (128, 8), F32),
                         ("l1_bk", (128, 8), F32), ("l1_bv", (1, D), F32),
                         ("l1_wo", (D, D), F32), ("l1_bo", (128, 8), F32),
                         ("l1_rw", (D, 8), F32), ("l1_w1", (E, D + 128, cfg.MH), F32),
                         ("l1_w2", (E, cfg.MH, D), F32), ("l1_b2", (E, D), F32)]:
            dram_in(n, s, dt)

    out_fm = nc.dram_tensor("out_fm", [D, TPC], F32, kind="ExternalOutput").ap()
    gate_tap = (nc.dram_tensor("tap_gates", [8, TPC], F32, kind="ExternalOutput").ap()
                if "gates" in cfg.debug_taps else None)
    h_tap = (nc.dram_tensor("tap_h", [cfg.MH, TP], F32, kind="ExternalOutput").ap()
             if "moe_h" in cfg.debug_taps else None)
    taps = {t: nc.dram_tensor("tap_" + t, [D, TPC], F32, kind="ExternalOutput").ap()
            for t in cfg.debug_taps if t != "gates"}

    k_contrib = nc.dram_tensor("k_contrib", [D + 1, TPC], BF16, kind="Internal").ap()
    k_gath = nc.dram_tensor("k_gath", [4 * (D + 1), TPC], BF16, kind="Internal").ap()
    v_contrib = nc.dram_tensor("v_contrib", [4 * TPC, 260], BF16, kind="Internal").ap()
    v_gath = nc.dram_tensor("v_gath", [16 * TPC, 260], BF16, kind="Internal").ap()
    q_park = nc.dram_tensor("q_park", [D, TPC], BF16, kind="Internal").ap()

    SBUF_BASE = 16512

    def sb(name, shape, dt, off):
        nbytes = int(np.prod(shape[1:])) * mybir.dt.size(dt)
        assert off >= 0 and off + nbytes <= OFF_END, (name, off, nbytes)
        return nc.alloc_sbuf_tensor_at(name, list(shape), dt,
                                       offset=SBUF_BASE + off)

    xacc = [sb(f"xacc{i}", (128, TPC), F32, OFF_A + i * TPC * 4) for i in range(FCH)]
    xn = [sb(f"xn{i}", (128, TPC), BF16, OFF_B + i * TPC * 2) for i in range(FCH)]
    kwin = [sb(f"kw{i}", (128, TPC), BF16, OFF_B + i * TPC * 2) for i in range(8)]
    q2 = [sb(f"q2_{i}", (128, TPC), BF16, OFF_C + i * TPC * 2) for i in range(8)]

    sm = [OFF_SM]
    def sbs(name, shape, dt):
        h = sb(name, shape, dt, sm[0])
        sm[0] += (int(np.prod(shape[1:])) * mybir.dt.size(dt) + 31) & ~31
        assert sm[0] <= OFF_END, name
        return h

    fconst_sb = sbs("c_fconst", (128, 8), F32)
    FCONSTS = [0.0, 1.0, 1e-5, -0.5, 0.125, 0.5, -1.0, 1.0 / 127.0]
    for i, v in enumerate(FCONSTS):
        nc.const_aps.register(F32, v, fconst_sb[:, i:i + 1])
    ones_col = sbs("c_ones", (128, 1), F32)
    ones_r1 = sbs("c_or1", (1, 128), F32)
    ones_r1_bf = sbs("c_or1b", (1, 128), BF16)
    ones_row_bf = sbs("c_orow", (1, 512), BF16)
    iota_e = sbs("c_iota", (128, 8), F32)
    sel8_sb = sbs("c_sel8", (8, 1024), F32)
    ident = sbs("c_ident", (128, 128), F32)
    masks_sb = sbs("c_masks", (128, 4 * QW), BF16)
    bq_sb = sbs("bq", (128, 8), F32)
    bk_sb = sbs("bk", (128, 8), F32)
    bo_sb = sbs("bo", (128, 8), F32)
    bv_sb = sbs("bv", (1, D), BF16)
    rw_sb = sbs("rw", (128, 64), BF16)
    b2_sb = sbs("b2", (8, D), BF16)
    qn_sb = sbs("qn", (1, 8), F32)
    kn_sb = sbs("kn", (1, 8), F32)
    kn4_sb = sbs("kn4", (1, 32), F32)
    ntmp = sbs("ntmp", (1, 16), F32)
    msc = sbs("msc", (1, 8), F32)
    negM1 = sbs("negM1", (1, 1), F32)
    negM = sbs("negM", (128, 1), F32)
    recrow = sbs("recrow", (1, 512), F32)

    AL = lambda x: (x + 31) & ~31
    xsq = sb("xsq", (128, TPC), F32, OFF_D)
    mean_bc = sb("mean_bc", (128, TPC), F32, OFF_D + TPC * 4)
    rstd_bc = sb("rstd_bc", (128, TPC), F32, OFF_D + 2 * TPC * 4)
    mrow = sb("mrow", (1, TPC), F32, OFF_D + 3 * TPC * 4)
    vrow = sb("vrow", (1, TPC), F32, OFF_D + 3 * TPC * 4 + AL(TPC * 4))
    srow = sb("srow", (1, TPC), F32, OFF_D + 3 * TPC * 4 + 2 * AL(TPC * 4))
    o1 = OFF_D + AL(TPC * 4)
    wslab = [sb(f"wsl{i}", (128, FCH, 128), BF16, o1 + i * FCH * 256) for i in range(2)]
    o2 = o1 + 2 * FCH * 256
    vslab = [sb(f"vsl{i}", (128, FCH, 512), BF16, o2 + i * FCH * 1024) for i in range(2)]
    o3 = o2 + 2 * FCH * 1024
    k2tmp = [sb(f"k2t{i}", (128, TPC), BF16, o3 + i * TPC * 2) for i in range(2)]
    o4 = o3 + 2 * TPC * 2
    vstage = [sb(f"vst{i}", (128, 4, 260), BF16, o4 + i * 2080) for i in range(2)]

    HCAP = (8 * TPC) // TP

    def h_tiles(n, tag):
        out = []
        for i in range(n):
            off = OFF_C + i * TP * 2 if i < HCAP else OFF_D + (i - HCAP) * TP * 2
            out.append(sb(f"{tag}{i}", (128, TP), BF16, off))
        return out

    if 0 in layers:
        h_l0 = h_tiles(cfg.NHB_F, "hf")
        f1 = OFF_D + max(0, cfg.NHB_F - HCAP) * TP * 2
        gslab = [sb(f"gsl{i}", (128, FCH, 128), BF16, f1 + i * FCH * 256)
                 for i in range(2)]
        uslab = [sb(f"usl{i}", (128, FCH, 128), BF16, f1 + 2 * FCH * 256 + i * FCH * 256)
                 for i in range(2)]
        g_sb = [sb(f"g_s{i}", (128, CB), BF16, f1 + 4 * FCH * 256 + i * CB * 2)
                for i in range(2)]
        dslab = [sb(f"dsl{i}", (128, cfg.NHB_F, 128), BF16,
                    f1 + 4 * FCH * 256 + 2 * CB * 2 + i * cfg.NHB_F * 256)
                 for i in range(2)]
    if 1 in layers:
        h_moe = h_tiles(cfg.NHB_M, "hm")
        m0 = OFF_D + max(0, cfg.NHB_M - HCAP) * TP * 2
        m1slab = [sb(f"m1s{i}", (128, 9, 128), BF16, m0 + i * 9 * 256)
                  for i in range(3)]
        m1e = m0 + 3 * 9 * 256
        m2slab = [sb(f"m2s{i}", (128, 2, 512), BF16, m1e + i * 2 * 1024)
                  for i in range(2)]
        m2e = m1e + 2 * 2 * 1024
        gb_sb = [sb("gb0", (128, TP), F32, m2e)]
        gbe = m2e + TP * 4
        cT_sb = sb("cT", (8, TPC), F32, gbe)
        cTb_sb = sb("cTb", (8, TPC), BF16, gbe + TPC * 4)
        r0 = gbe + TPC * 4 + TPC * 2
        U32 = mybir.dt.uint32
        lg_sb = sb("lg", (128, 8), F32, r0)
        r_m8 = sb("r_m8", (128, 8), F32, r0 + 32)
        r_i8 = sb("r_i8", (128, 8), U32, r0 + 64)
        r_if = sb("r_if", (128, 2), F32, r0 + 96)
        r_oh1 = sb("r_oh1", (128, 8), F32, r0 + 128)
        r_oh2 = sb("r_oh2", (128, 8), F32, r0 + 160)
        r_neg = sb("r_neg", (128, 1), F32, r0 + 192)
        r_p = sb("r_p", (128, 8), F32, r0 + 224)
        r_s = sb("r_s", (128, 1), F32, r0 + 256)
        r_t = sb("r_t", (128, 1), F32, r0 + 288)
        r_cf = sb("r_cf", (128, 8), F32, r0 + 320)

    vwstride = (KTPR * 520 + 31) & ~31
    vw = [sb(f"vw{i}", (128, KTPR, 260), BF16, OFF_D + i * vwstride)
          for i in range(4)]
    a0 = OFF_D + 4 * vwstride
    z_sb = [sb(f"z{i}", (128, TPC), BF16, a0 + i * TPC * 2) for i in range(2)]
    a1 = a0 + 2 * TPC * 2
    p_buf = [sb(f"pbf{i}", (128, 2 * QW), BF16, a1 + i * 2 * QW * 2)
             for i in range(2)]
    a3 = a1 + 5 * QW * 2
    oslab = [sb(f"osl{i}", (128, D), BF16, a3 + i * D * 2) for i in range(4)]
    a4 = a3 + 4 * D * 2
    recb = sb("recb", (64, 512), F32, a4)
    assert a4 + 2048 <= OFF_SM, (a4, OFF_SM)

    pq = [nc.alloc_psum_tensor(f"pq{i}", [128, 1024], F32) for i in range(2)]
    pb_s = [nc.alloc_psum_tensor(f"bank{i}", [128, 512], F32) for i in range(4, 8)]
    pbk = [pq[0][:, 0:512], pq[0][:, 512:1024],
           pq[1][:, 0:512], pq[1][:, 512:1024]] + [h[:] for h in pb_s]

    P = Prog()

    def dma(q, dst, src, waits=(), sem="MISC"):
        return P.emit(q, lambda e, d=dst, s=src: e.dma_start(out=d, in_=s),
                      waits=waits, inc=sem, amt=16)

    def mm(out, lhsT, rhs, start, stop, waits=(), inc=None):
        return P.emit("tensor",
                      lambda e, o=out, l=lhsT, r=rhs, a=start, b=stop:
                      e.matmul(o, l, r, start=a, stop=b),
                      waits=waits, inc=inc)

    def bk_acq(i, waits=()):
        t = P.bank_r.get(i)
        return (list(waits) + [t]) if t else list(waits)

    def bk_read(i, tok):
        P.bank_r[i] = tok

    def vec(fn, waits=(), inc="DV"):
        return P.emit("vector", fn, waits=waits, inc=inc)

    def act(fn, waits=(), inc="AC"):
        return P.emit("scalar", fn, waits=list(waits) + [cst], inc=inc)

    dma("sync", fconst_sb[:], din["fconst"], sem="CST")
    for h, name in [(ones_col, "ones_col"), (ones_r1, "ones_r1"),
                    (iota_e, "iota_e"), (ident, "ident"),
                    (ones_r1_bf, "ones_r1_bf"), (ones_row_bf, "ones_row_bf"),
                    (masks_sb, "masks"), (sel8_sb, "sel8")]:
        dma("sync", h[:], din[name], sem="CST")
    if 1 in layers:
        for h, name in [(bq_sb, "l1_bq"), (bk_sb, "l1_bk"), (bo_sb, "l1_bo")]:
            dma("sync", h[:], din[name], sem="CST")
        for h, name in [(bv_sb, "l1_bv"), (b2_sb, "l1_b2")]:
            dma("gpsimd", h[:], din[name], sem="CSG")
        dma("gpsimd", rw_sb.ap().rearrange("p (fc e) -> p fc e", e=8),
            din["l1_rw"].rearrange("(fc p) e -> p fc e", p=128), sem="CSG")
    cst = ("CST", P.cnt["CST"])
    cstg = ("CSG", P.cnt["CSG"]) if P.cnt["CSG"] else None

    for fc in range(FCH):
        dma("sync", xacc[fc][:], din["x_fm"][fc * 128:(fc + 1) * 128, :], sem="XL")
    xacc_tok = [("XL", P.cnt["XL"])] * FCH

    zt = P.emit("vector", lambda e: e.memset(mrow[0:1, :], 0.0), inc="DV")
    norm_zero = dma("gpsimd", k_contrib[D:D + 1, :], mrow[0:1, :], waits=[zt],
                    sem="KN")

    st = {
        "xacc": xacc_tok,
        "B_free": [None] * 8,
        "q2_free": [None] * 8,
        "h_free": None,
        "arena_free": None,
        "kgath_free": None,
        "vgath_free": None,
        "qpark_free": None,
        "contrib_free": None,
        "k2tmp_free": [None, None],
        "vstage_free": [None, None],
        "xsq_gate": None,
        "wstream": Stream(P, "gpsimd", "sW", wslab),
        "vstream": Stream(P, "gpsimd", "sV", vslab),
        "ostream": Stream(P, "gpsimd", "sO", oslab),
    }
    if 0 in layers:
        st["gs"] = Stream(P, "gpsimd", "sG", gslab)
        st["us"] = Stream(P, "gpsimd", "sU", uslab)
        st["ds"] = Stream(P, "gpsimd", "sD", dslab)
    if 1 in layers:
        st["m1s"] = Stream(P, "gpsimd", "s1", m1slab)
        st["m2s"] = Stream(P, "gpsimd", "s2", m2slab)

    def emit_ln(lid):
        arena_w = [st["arena_free"]] if st["arena_free"] else []
        sum_tok = [None] * NCB
        sq_tok_f = [None] * NCB
        last_sq = st["xsq_gate"]
        for fc in range(FCH):
            for tb in range(NCB):
                w = [st["xacc"][fc], cst]
                if fc == 0:
                    w = bk_acq(tb, w)
                t = mm(pbk[tb][0:1, 0:CB], ones_col[:],
                       xacc[fc][:, tb * CB:(tb + 1) * CB],
                       start=(fc == 0), stop=(fc == FCH - 1),
                       waits=w, inc="TE" if fc == FCH - 1 else None)
                if fc == FCH - 1:
                    sum_tok[tb] = t
            w = [st["xacc"][fc], last_sq] + (arena_w if fc == 0 else [])
            s = act(lambda e, fc=fc: e.activation(xsq[:], xacc[fc][:], AF.Square),
                    waits=w)
            for tb in range(NCB):
                w = [s]
                if fc == 0:
                    w = bk_acq(4 + tb, w)
                t = mm(pbk[4 + tb][0:1, 0:CB], ones_col[:],
                       xsq[:, tb * CB:(tb + 1) * CB],
                       start=(fc == 0), stop=(fc == FCH - 1),
                       waits=w, inc="TE")
                if fc == FCH - 1:
                    sq_tok_f[tb] = t
                last_sq = t
        m_toks = []
        for tb in range(NCB):
            t = vec(lambda e, tb=tb: e.tensor_scalar_mul(
                mrow[0:1, tb * CB:(tb + 1) * CB], pbk[tb][0:1, 0:CB], 1.0 / D),
                waits=[sum_tok[tb], norm_zero] + arena_w)
            bk_read(tb, t)
            m_toks.append(t)
        msq = vec(lambda e: e.tensor_mul(srow[0:1, :], mrow[0:1, :], mrow[0:1, :]),
                  waits=m_toks)
        v_toks = []
        for tb in range(NCB):
            t = vec(lambda e, tb=tb: e.scalar_tensor_tensor(
                vrow[0:1, tb * CB:(tb + 1) * CB], pbk[4 + tb][0:1, 0:CB],
                1.0 / D, srow[0:1, tb * CB:(tb + 1) * CB], ALU.mult, ALU.subtract),
                waits=[sq_tok_f[tb], msq])
            bk_read(4 + tb, t)
            v_toks.append(t)
        t_ln = act(lambda e: e.activation(srow[0:1, :], vrow[0:1, :], AF.Ln,
                                          bias=1e-5), waits=v_toks + [msq])
        t_rs = act(lambda e: e.activation(srow[0:1, :], srow[0:1, :], AF.Exp,
                                          scale=-0.5), waits=[t_ln])
        bc = []
        for tb in range(NCB):
            w = bk_acq(tb, [m_toks[tb], cst])
            t1 = mm(pbk[tb][:, 0:CB], ones_r1[0:1, :],
                    mrow[0:1, tb * CB:(tb + 1) * CB], True, True, w, inc="TE")
            w = bk_acq(4 + tb, [t_rs])
            t2 = mm(pbk[4 + tb][:, 0:CB], ones_r1[0:1, :],
                    srow[0:1, tb * CB:(tb + 1) * CB], True, True, w, inc="TE")
            c1 = vec(lambda e, tb=tb: e.tensor_copy(
                mean_bc[:, tb * CB:(tb + 1) * CB], pbk[tb][:, 0:CB]), waits=[t1])
            c2 = vec(lambda e, tb=tb: e.tensor_copy(
                rstd_bc[:, tb * CB:(tb + 1) * CB], pbk[4 + tb][:, 0:CB]), waits=[t2])
            bk_read(tb, c1)
            bk_read(4 + tb, c2)
            bc += [c1, c2]
        xn_tok = []
        for fc in range(FCH):
            t1 = vec(lambda e, fc=fc: e.tensor_sub(xsq[:], xacc[fc][:], mean_bc[:]),
                     waits=bc + [st["xacc"][fc], last_sq])
            w2 = [t1] + ([st["B_free"][fc]] if st["B_free"][fc] else [])
            t2 = vec(lambda e, fc=fc: e.tensor_mul(xn[fc][:], xsq[:], rstd_bc[:]),
                     waits=w2)
            xn_tok.append(t2)
            last_sq = t2
        st["xsq_gate"] = last_sq
        return xn_tok

    def emit_qkv(lid, xn_tok):
        l0 = (lid == 0)
        pre = f"l{lid}_"
        ws = st["wstream"]
        g_ = xn_tok[-1]
        ws.gate(g_)
        st["vstream"].gate(g_)
        st["k2tmp_free"] = [g_] * 2
        st["vstage_free"] = [g_] * 2

        def proj_qk(which):
            contrib_toks = []
            norm_done = None
            for hp in range(8):
                buf, slot = ws.load(
                    din[pre + ("wq" if which == "q" else "wk")]
                    [:, hp * 128:(hp + 1) * 128].rearrange("(g p) c -> p g c", p=128))
                ld = ws.ld[slot]
                if which == "q":
                    dest, dw = q2[hp], st["q2_free"][hp]
                else:
                    dest = k2tmp[hp % 2]
                    dw = contrib_toks[hp - 2] if hp >= 2 else st["k2tmp_free"][hp % 2]
                evac = []
                for tb in range(NCB):
                    bi = tb % 2
                    w = bk_acq(bi, [ld] + xn_tok)
                    t = None
                    for fc in range(FCH):
                        t = mm(pbk[bi][:, 0:CB], buf[:, fc, :],
                               xn[fc][:, tb * CB:(tb + 1) * CB],
                               start=(fc == 0), stop=(fc == FCH - 1),
                               waits=w if fc == 0 else (),
                               inc="TE" if fc == FCH - 1 else None)
                    ew = [t] + ([dw] if dw and tb == 0 else [])
                    if not l0:
                        b = bq_sb if which == "q" else bk_sb
                        et = vec(lambda e, d=dest, tb=tb, bi=bi, b=b, hp=hp:
                                 e.tensor_scalar_add(d[:, tb * CB:(tb + 1) * CB],
                                                     pbk[bi][:, 0:CB],
                                                     b[:, hp:hp + 1]),
                                 waits=ew + [cst])
                    else:
                        et = vec(lambda e, d=dest, tb=tb, bi=bi:
                                 e.tensor_copy(d[:, tb * CB:(tb + 1) * CB],
                                               pbk[bi][:, 0:CB]), waits=ew)
                    bk_read(bi, et)
                    evac.append(et)
                ws.release(slot, evac[-1])
                s = act(lambda e, d=dest: e.activation(xsq[:], d[:], AF.Square),
                        waits=evac + [st["xsq_gate"]])
                red = []
                for tb in range(NCB):
                    w = bk_acq(6, [s])
                    t = mm(pbk[6][0:1, 0:CB], ones_col[:],
                           xsq[:, tb * CB:(tb + 1) * CB], True, True, w, inc="TE")
                    rt = vec(lambda e, tb=tb: e.reduce_max(
                        ntmp[0:1, tb:tb + 1], pbk[6][0:1, 0:CB], AX.X),
                        waits=[t, norm_done])
                    bk_read(6, rt)
                    red.append(rt)
                dstrow = qn_sb if which == "q" else kn_sb
                norm_done = vec(lambda e, d=dstrow, hp=hp: e.reduce_max(
                    d[0:1, hp:hp + 1], ntmp[0:1, 0:NCB], AX.X), waits=red)
                st["xsq_gate"] = s
                st["xsq_gate"] = red[-1]
                if which == "q":
                    pk = dma("sync", q_park[hp * 128:(hp + 1) * 128, :], q2[hp][:],
                             waits=evac + ([st["qpark_free"]] if st["qpark_free"]
                                           else []), sem=f"pk{hp}")
                    st["q2_free"][hp] = pk
                else:
                    cw = [st["contrib_free"]] if st["contrib_free"] else []
                    kc = dma("sync", k_contrib[hp * 128:(hp + 1) * 128, :],
                             k2tmp[hp % 2][:], waits=evac + cw, sem=f"kc{hp % 2}")
                    contrib_toks.append(kc)
            return contrib_toks, norm_done

        _, qn_done = proj_qk("q")
        k_contrib_toks, kn_done = proj_qk("k")
        st["k2tmp_free"] = [k_contrib_toks[-2], k_contrib_toks[-1]]
        kc_norm = dma("gpsimd", k_contrib[D:D + 1, 0:8], kn_sb[0:1, 0:8],
                      waits=[kn_done], sem="KN")
        k_contrib_toks.append(kc_norm)

        vs = st["vstream"]
        ones_done = [None, None]
        for i in range(2):
            prev = st["vstage_free"][i]
            for q4 in range(4):
                prev = vec(lambda e, i=i, q4=q4: e.memset(
                    vstage[i].ap()[:, q4, :].rearrange("p (h c) -> p h c", c=65)
                    [:, :, 64:65], 1.0), waits=[prev])
            ones_done[i] = prev
        v_contrib_toks = []
        ev_last = None
        for vh in range(2):
            buf, slot = vs.load(din[pre + "wv"][:, vh * 512:(vh + 1) * 512]
                                .rearrange("(g p) c -> p g c", p=128))
            ld = vs.ld[slot]
            for tb in range(NTB):
                bi = 2 + tb % 2
                w = bk_acq(bi, [ld] + xn_tok)
                t = None
                for fc in range(FCH):
                    last = (fc == FCH - 1) and l0
                    t = mm(pbk[bi][:, 0:512], xn[fc][:, tb * 128:(tb + 1) * 128],
                           buf[:, fc, :], start=(fc == 0), stop=last,
                           waits=w if fc == 0 else (), inc="TE" if last else None)
                if not l0:
                    t = mm(pbk[bi][:, 0:512], ones_r1_bf[0:1, :],
                           bv_sb[0:1, vh * 512:(vh + 1) * 512], False, True,
                           waits=[cstg], inc="TE")
                si = tb % 2
                sw = [t, ones_done[si]]
                if len(v_contrib_toks) >= 2:
                    sw.append(v_contrib_toks[-2])
                ev = vec(lambda e, si=si, bi=bi, vh=vh: e.tensor_copy(
                    vstage[si].ap()[:, 2 * vh:2 * vh + 2, :]
                    .rearrange("p q (h c) -> p (q h) c", c=65)[:, :, 0:64],
                    pbk[bi][:, 0:512]), waits=sw)
                bk_read(bi, ev)
                cw = [st["contrib_free"]] if st["contrib_free"] else []
                vc = dma("sync",
                         v_contrib.rearrange("(q t p) c -> p q t c", q=4, p=128)
                         [:, 2 * vh:2 * vh + 2, tb, :],
                         vstage[si][:, 2 * vh:2 * vh + 2, :], waits=[ev] + cw,
                         sem=f"vc{si}")
                v_contrib_toks.append(vc)
                ev_last = ev
            vs.release(slot, ev_last)
        st["vstage_free"] = [v_contrib_toks[-2], v_contrib_toks[-1]]

        coll_w = k_contrib_toks + [v_contrib_toks[-1], v_contrib_toks[-2]]
        if st["kgath_free"]:
            coll_w.append(st["kgath_free"])
        ck = P.emit("gpsimd", lambda e: e.collective_compute(
            "AllGather", ALU.bypass, ins=[k_contrib.opt()], outs=[k_gath.opt()],
            replica_groups=[[0, 1, 2, 3], [4, 5, 6, 7]]), waits=coll_w, inc="CCK")
        cv_w = [v_contrib_toks[-1]]
        if st["vgath_free"]:
            cv_w.append(st["vgath_free"])
        cv = P.emit("gpsimd", lambda e: e.collective_compute(
            "AllGather", ALU.bypass, ins=[v_contrib.opt()], outs=[v_gath.opt()],
            replica_groups=[[0, 1, 2, 3], [4, 5, 6, 7]]), waits=cv_w, inc="CCV")
        st["contrib_free"] = cv
        return ck, cv

    def emit_ffn_l0(xn_tok):
        gs, us, ds = st["gs"], st["us"], st["ds"]
        NHB = cfg.NHB_F
        h_tok = [None] * NHB
        g_free = [None, None]
        for tp in range(NTP):
            for hb in range(NHB):
                b1, s1 = gs.load(din["l0_wg"][:, hb * 128:(hb + 1) * 128]
                                 .rearrange("(g p) c -> p g c", p=128))
                b2, s2 = us.load(din["l0_wu"][:, hb * 128:(hb + 1) * 128]
                                 .rearrange("(g p) c -> p g c", p=128))
                for tbl in range(TP // CB):
                    col0 = tp * TP + tbl * CB
                    bg, bu = tbl % 2, 2 + tbl % 2
                    w = bk_acq(bg, [gs.ld[s1]] + xn_tok)
                    tg = None
                    for fc in range(FCH):
                        tg = mm(pbk[bg][:, 0:CB], b1[:, fc, :],
                                xn[fc][:, col0:col0 + CB],
                                start=(fc == 0), stop=(fc == FCH - 1),
                                waits=w if fc == 0 else (),
                                inc="TE" if fc == FCH - 1 else None)
                    w = bk_acq(bu, [us.ld[s2]])
                    tu = None
                    for fc in range(FCH):
                        tu = mm(pbk[bu][:, 0:CB], b2[:, fc, :],
                                xn[fc][:, col0:col0 + CB],
                                start=(fc == 0), stop=(fc == FCH - 1),
                                waits=w if fc == 0 else (),
                                inc="TE" if fc == FCH - 1 else None)
                    gi = tbl % 2
                    ga = act(lambda e, gi=gi, bg=bg: e.activation(
                        g_sb[gi][:, 0:CB], pbk[bg][:, 0:CB], AF.Silu,
                        scale=1.0 / 127.0), waits=[tg, g_free[gi]])
                    bk_read(bg, ga)
                    hw = [tu, ga]
                    if tp == 0 and tbl == 0:
                        if hb < HCAP:
                            hw.append(st["q2_free"][hb * TP // TPC])
                        elif st["arena_free"]:
                            hw.append(st["arena_free"])
                    ht = vec(lambda e, hb=hb, tbl=tbl, bu=bu, gi=gi:
                             e.scalar_tensor_tensor(
                                 h_l0[hb][:, tbl * CB:(tbl + 1) * CB],
                                 pbk[bu][:, 0:CB], 1.0 / 127.0, g_sb[gi][:, 0:CB],
                                 ALU.mult, ALU.mult), waits=hw)
                    bk_read(bu, ht)
                    g_free[gi] = ht
                    h_tok[hb] = ht
                gs.release(s1, h_tok[hb])
                us.release(s2, h_tok[hb])
            for ob in range(FCH):
                b3, s3 = ds.load(din["l0_wd"][:, ob * 128:(ob + 1) * 128]
                                 .rearrange("(g p) c -> p g c", p=128))
                last_at = None
                for tbl in range(TP // CB):
                    col0 = tp * TP + tbl * CB
                    bd = 4 + tbl % 2
                    w = bk_acq(bd, [ds.ld[s3]] + h_tok)
                    t = None
                    for hb in range(NHB):
                        t = mm(pbk[bd][:, 0:CB], b3[:, hb, :],
                               h_l0[hb][:, tbl * CB:(tbl + 1) * CB],
                               start=(hb == 0), stop=(hb == NHB - 1),
                               waits=w if hb == 0 else (),
                               inc="TE" if hb == NHB - 1 else None)
                    at = vec(lambda e, ob=ob, col0=col0, bd=bd:
                             e.scalar_tensor_tensor(
                                 xacc[ob][:, col0:col0 + CB], pbk[bd][:, 0:CB],
                                 1.0 / 127.0, xacc[ob][:, col0:col0 + CB],
                                 ALU.mult, ALU.add),
                             waits=[t, st["xacc"][ob]])
                    bk_read(bd, at)
                    st["xacc"][ob] = at
                    last_at = at
                ds.release(s3, last_at)
        return

    def emit_moe(xn_tok):
        NHB = cfg.NHB_M
        arena_w = [st["arena_free"]] if st["arena_free"] else []
        cT_toks = []
        rprev = None
        for tb in range(NTB):
            w = bk_acq(7, xn_tok + [cstg] + arena_w)
            t = None
            for fc in range(FCH):
                t = mm(pbk[7][:, 0:8], xn[fc][:, tb * 128:(tb + 1) * 128],
                       rw_sb[:, fc * 8:(fc + 1) * 8],
                       start=(fc == 0), stop=(fc == FCH - 1),
                       waits=w if fc == 0 else (), inc="TE" if fc == FCH - 1 else None)
            c0 = vec(lambda e: e.tensor_copy(lg_sb[:], pbk[7][:, 0:8]),
                     waits=[t, rprev])
            bk_read(7, c0)
            t1 = vec(lambda e: e.max(r_m8[:], lg_sb[:]), waits=[c0])
            t2 = vec(lambda e: e.max_index(r_i8[:], r_m8[:], lg_sb[:]), waits=[t1])
            t2b = vec(lambda e: e.tensor_copy(r_if[:], r_i8[:, 0:2]), waits=[t2])
            t3 = vec(lambda e: e.tensor_single_scalar(r_oh1[:], iota_e[:],
                                                      r_if[:, 0:1], ALU.is_equal),
                     waits=[t2b, cst])
            t7 = vec(lambda e: e.tensor_single_scalar(r_oh2[:], iota_e[:],
                                                      r_if[:, 1:2], ALU.is_equal),
                     waits=[t3])
            t8 = vec(lambda e: e.tensor_scalar_mul(r_neg[:], r_m8[:, 0:1], -1.0),
                     waits=[t1])
            t9 = act(lambda e: e.activation(r_p[:], lg_sb[:], AF.Exp,
                                            bias=r_neg[:, 0:1]), waits=[t8, t2])
            ta = vec(lambda e: e.reduce_sum(r_s[:], r_p[:], AX.X), waits=[t9])
            tb_ = vec(lambda e: e.reciprocal(r_t[:], r_s[:]), waits=[ta])
            tc = vec(lambda e: e.tensor_scalar_mul(r_p[:], r_p[:], r_t[:, 0:1]),
                     waits=[tb_])
            td = vec(lambda e: e.tensor_add(r_oh2[:], r_oh1[:], r_oh2[:]),
                     waits=[t7, tc])
            te_ = vec(lambda e: e.tensor_mul(r_p[:], r_p[:], r_oh2[:]), waits=[td])
            tf = vec(lambda e: e.reduce_sum(r_s[:], r_p[:], AX.X), waits=[te_])
            tg = vec(lambda e: e.tensor_scalar_add(r_s[:], r_s[:], 1e-9), waits=[tf])
            th = vec(lambda e: e.reciprocal(r_t[:], r_s[:]), waits=[tg])
            ti = vec(lambda e: e.tensor_scalar_mul(r_cf[:], r_p[:], r_t[:, 0:1]),
                     waits=[th])
            w = bk_acq(6, [ti, cst])
            tj = P.emit("tensor", lambda e: e.transpose(pbk[6][0:8, 0:128],
                                                        r_cf[:], ident[:]),
                        waits=w, inc="TE")
            tk = vec(lambda e, tb=tb: e.tensor_copy(
                cT_sb[0:8, tb * 128:(tb + 1) * 128], pbk[6][0:8, 0:128]),
                waits=[tj] + arena_w)
            bk_read(6, tk)
            tl = vec(lambda e, tb=tb: e.tensor_copy(
                cTb_sb[0:8, tb * 128:(tb + 1) * 128],
                cT_sb[0:8, tb * 128:(tb + 1) * 128]), waits=[tk])
            cT_toks.append(tl)
            rprev = ti

        if gate_tap is not None:
            dma("sync", gate_tap, cT_sb[0:8, :], waits=cT_toks, sem="OUT")
        m1s, m2s = st["m1s"], st["m2s"]
        h_tok = [None] * NHB
        gb_tok = [None]
        for e_i in range(E):
            for tp in range(NTP):
                gi = 0
                gt = None
                for yw in range(TP // YW):
                    w = bk_acq(6, cT_toks + ([gb_tok[gi]] if gb_tok[gi] else [])
                               + (arena_w if e_i == 0 else []))
                    t = mm(pbk[6][:, 0:YW],
                           sel8_sb[0:8, e_i * 128:(e_i + 1) * 128],
                           cT_sb[0:8, tp * TP + yw * YW:tp * TP + (yw + 1) * YW],
                           True, True, waits=w, inc="TE")
                    gt = vec(lambda e, gi=gi, yw=yw: e.tensor_copy(
                        gb_sb[gi][:, yw * YW:(yw + 1) * YW], pbk[6][:, 0:YW]),
                        waits=[t])
                    bk_read(6, gt)
                gb_tok[gi] = gt
                for hb in range(NHB):
                    b1, s1 = m1s.load(din["l1_w1"][e_i, :, hb * 128:(hb + 1) * 128]
                                      .rearrange("(g p) c -> p g c", p=128))
                    for tbl in range(TP // CB):
                        col0 = tp * TP + tbl * CB
                        bi = tbl % 2
                        w = bk_acq(bi, [m1s.ld[s1]] + xn_tok)
                        for fc in range(FCH):
                            mm(pbk[bi][:, 0:CB], b1[:, fc, :],
                               xn[fc][:, col0:col0 + CB],
                               start=(fc == 0), stop=False,
                               waits=w if fc == 0 else ())
                        t = mm(pbk[bi][:, 0:CB], b1[0:1, FCH, :],
                               ones_row_bf[0:1, 0:CB], False, True,
                               waits=[cst], inc="TE")
                        hw = [t, gb_tok[gi]]
                        if e_i == 0 and tp == 0 and tbl == 0:
                            if hb < HCAP:
                                hw.append(st["q2_free"][hb * TP // TPC])
                            elif st["arena_free"]:
                                hw.append(st["arena_free"])
                        ht = vec(lambda e, hb=hb, tbl=tbl, bi=bi, gi=gi:
                                 e.scalar_tensor_tensor(
                                     h_moe[hb][:, tbl * CB:(tbl + 1) * CB],
                                     pbk[bi][:, 0:CB], 0.0,
                                     gb_sb[gi][:, tbl * CB:(tbl + 1) * CB],
                                     ALU.max, ALU.mult), waits=hw)
                        bk_read(bi, ht)
                        h_tok[hb] = ht
                    m1s.release(s1, h_tok[hb])
                if h_tap is not None and e_i == 3 and tp == 0:
                    for hb in range(NHB):
                        dma("gpsimd", h_tap[hb * 128:(hb + 1) * 128, :],
                            h_moe[hb][:], waits=[h_tok[hb]], sem="HT")
                    htt = ("HT", P.cnt["HT"])
                    h_tok = [htt] * NHB
                ytiles = [(cb, th) for cb in range(4) for th in range(NYW)]
                for ch in range(2):
                    if e_i == 0:
                        for (cb, th) in ytiles:
                            bi = cb * NYW + th
                            w = bk_acq(bi, cT_toks + [cstg])
                            mm(pbk[bi][:, 0:YW],
                               b2_sb[0:8, ch * 512 + cb * 128:ch * 512 + (cb + 1) * 128],
                               cTb_sb[0:8, tp * TP + th * YW:tp * TP + (th + 1) * YW],
                               True, False, waits=w)
                    last = {}
                    for hbg in range(NHB // 2):
                        b2w, s2 = m2s.load(
                            din["l1_w2"][e_i, hbg * 256:(hbg + 1) * 256,
                                         ch * 512:(ch + 1) * 512]
                            .rearrange("(g p) c -> p g c", p=128))
                        slab_last = None
                        for g4 in range(2):
                            hb = hbg * 2 + g4
                            for (cb, th) in ytiles:
                                bi = cb * NYW + th
                                isend = (hb == NHB - 1)
                                isfirst = (e_i > 0 and hb == 0)
                                w = [m2s.ld[s2], h_tok[hb]] if (cb, th) == ytiles[0] else ()
                                if isfirst:
                                    w = bk_acq(bi, list(w) if w else [])
                                t = mm(pbk[bi][:, 0:YW],
                                       b2w[:, g4, cb * 128:(cb + 1) * 128],
                                       h_moe[hb][:, th * YW:(th + 1) * YW],
                                       isfirst, isend, waits=w,
                                       inc="TE" if isend or ((g4 == 1) and
                                                             (cb, th) == ytiles[-1])
                                       else None)
                                if isend:
                                    last[(cb, th)] = t
                                if (g4 == 1) and (cb, th) == ytiles[-1]:
                                    slab_last = t
                        m2s.release(s2, slab_last)
                    for (cb, th) in ytiles:
                        bi = cb * NYW + th
                        ob = ch * 4 + cb
                        col0 = tp * TP + th * YW
                        at = vec(lambda e, ob=ob, col0=col0, bi=bi:
                                 e.tensor_add(xacc[ob][:, col0:col0 + YW],
                                              pbk[bi][:, 0:YW],
                                              xacc[ob][:, col0:col0 + YW]),
                                 waits=[last[(cb, th)], st["xacc"][ob]])
                        bk_read(bi, at)
                        st["xacc"][ob] = at
        return

    def emit_attn(lid, ck, cv, xn_dead):
        l0 = (lid == 0)
        pre = f"l{lid}_"
        for r in range(4):
            dma("gpsimd", kn4_sb[0:1, r * 8:(r + 1) * 8],
                k_gath[r * (D + 1) + D:r * (D + 1) + D + 1, 0:8],
                waits=[ck], sem="K4")
        kn_toks = [("K4", P.cnt["K4"])]
        t1 = vec(lambda e: e.reduce_max(
            msc[0:1, 0:8],
            kn4_sb.ap()[0:1, 0:32].rearrange("a (r h) -> a h r", r=4), AX.X),
            waits=kn_toks)
        t2 = vec(lambda e: e.tensor_mul(msc[0:1, 0:8], msc[0:1, 0:8],
                                        qn_sb[0:1, 0:8]), waits=[t1])
        t3 = vec(lambda e: e.reduce_max(msc[0:1, 0:1], msc[0:1, 0:8], AX.X),
                 waits=[t2])
        t4 = act(lambda e: e.activation(msc[0:1, 0:1], msc[0:1, 0:1], AF.Ln),
                 waits=[t3])
        t5 = act(lambda e: e.activation(msc[0:1, 0:1], msc[0:1, 0:1], AF.Exp,
                                        scale=0.5), waits=[t4])
        t6 = vec(lambda e: e.tensor_scalar(negM1[0:1, 0:1], msc[0:1, 0:1],
                                           -0.1275, -1.0, ALU.mult, ALU.add),
                 waits=[t5])
        w = bk_acq(6, [t6, cst])
        t7 = mm(pbk[6][:, 0:1], ones_r1[0:1, :], negM1[0:1, 0:1], True, True, w,
                inc="TE")
        negM_tok = vec(lambda e: e.tensor_copy(negM[:], pbk[6][:, 0:1]), waits=[t7])
        bk_read(6, negM_tok)

        unpark = []
        for hp in range(8):
            w = [st["q2_free"][hp], st["h_free"]]
            unpark.append(dma("sync", q2[hp][:], q_park[hp * 128:(hp + 1) * 128, :],
                              waits=w, sem=f"up{hp}"))
        st["qpark_free"] = unpark[-1]

        os_ = st["ostream"]
        gate = st["h_free"]
        os_.gate(gate)
        o_last = None
        kg_last = None
        vg_last = None
        rec_chain = gate
        recb_chain = gate
        attn_gate = gate
        slot_exp = [gate, gate]
        slot_av = [gate, gate]
        for quad in range(4):
            k_toks = {}
            for hpi in range(2):
                hp = quad * 2 + hpi
                for r in range(4):
                    ki = hpi * 4 + r
                    w = [ck] + ([st["B_free"][ki]] if st["B_free"][ki] else [])
                    if quad == 0:
                        w += xn_dead
                    t = dma("sync", kwin[ki][:],
                            k_gath[r * (D + 1) + hp * 128:
                                   r * (D + 1) + (hp + 1) * 128, :], waits=w,
                            sem=f"kw{ki}")
                    k_toks[(hpi, r)] = t
                    kg_last = t
            v_toks = {}
            for r in range(4):
                w = [cv, attn_gate]
                t = dma("sync", vw[r][:],
                        v_gath.rearrange("(r q t p) c -> p r q t c", r=4, q=4, p=128)
                        [:, r, quad, :, :], waits=w, sem=f"vq{r}")
                v_toks[r] = t
                vg_last = t
            oz_toks = []
            z_free = [attn_gate, attn_gate]
            for hpi in range(2):
                hp = quad * 2 + hpi
                hp_qk_last = None
                for hi in range(2):
                    hloc = 2 * hpi + hi
                    rlo, rhi = hi * 64, (hi + 1) * 64
                    for qb in range(NQB):
                        diag_kb = qb * QW // 128
                        if l0:
                            kl = [(r, kb) for r in range(4)
                                  for kb in range(diag_kb + 1)]
                        else:
                            kl = [(r, kb) for r in range(4) for kb in range(KTPR)]
                        pairs = [kl[i:i + 2] for i in range(0, len(kl), 2)]
                        np_ = len(pairs)
                        qk_tok = [None] * np_

                        def emit_qk(p_i):
                            pr = pairs[p_i]
                            slot = p_i % 2
                            for j, (r, kb) in enumerate(pr):
                                ww = [k_toks[(hpi, r)]]
                                if j == 0:
                                    ww += [slot_exp[slot], unpark[hp]]
                                qk_tok[p_i] = mm(
                                    pq[slot][:, j * QW:(j + 1) * QW],
                                    kwin[hpi * 4 + r][rlo:rhi,
                                                      kb * 128:(kb + 1) * 128],
                                    q2[hp][rlo:rhi, qb * QW:(qb + 1) * QW],
                                    True, True, waits=ww, inc="TE")

                        emit_qk(0)
                        zlast_t = None
                        for i in range(np_):
                            if i + 1 < np_:
                                emit_qk(i + 1)
                            pr = pairs[i]
                            slot = i % 2
                            W = len(pr) * QW
                            et = act(lambda e, slot=slot, W=W: e.activation(
                                p_buf[slot][:, 0:W], pq[slot][:, 0:W], AF.Exp,
                                bias=negM[:, 0:1], scale=0.125),
                                waits=[qk_tok[i], negM_tok, slot_av[slot]])
                            slot_exp[slot] = et
                            half_tok = [et] * len(pr)
                            for j, (r, kb) in enumerate(pr):
                                if l0 and kb == diag_kb:
                                    half_tok[j] = vec(
                                        lambda e, slot=slot, j=j, r=r:
                                        e.tensor_mul(
                                            p_buf[slot][:, j * QW:(j + 1) * QW],
                                            p_buf[slot][:, j * QW:(j + 1) * QW],
                                            masks_sb[:, r * QW:(r + 1) * QW]),
                                        waits=[et, cst])
                            for j, (r, kb) in enumerate(pr):
                                first = (i == 0 and j == 0)
                                last = (i == np_ - 1 and j == len(pr) - 1)
                                w2 = [half_tok[j], v_toks[r]]
                                if first:
                                    w2 = bk_acq(4, w2)
                                zt = mm(pbk[4][0:65, 0:QW],
                                        vw[r][:, kb, hloc * 65:(hloc + 1) * 65],
                                        p_buf[slot][:, j * QW:(j + 1) * QW],
                                        start=first, stop=last,
                                        waits=w2, inc="TE")
                                slot_av[slot] = zt
                                if last:
                                    zlast_t = zt
                        hp_qk_last = qk_tok[np_ - 1]
                        e1 = act(lambda e: e.activation(
                            recrow[0:1, 0:QW], pbk[4][64:65, 0:QW], AF.Ln),
                            waits=[zlast_t, rec_chain])
                        e2 = act(lambda e: e.activation(
                            recrow[0:1, 0:QW], recrow[0:1, 0:QW], AF.Exp,
                            scale=-1.0), waits=[e1])
                        w = bk_acq(6, [e2])
                        e3 = mm(pbk[6][0:64, 0:QW], ones_r1[0:1, 0:64],
                                recrow[0:1, 0:QW], True, True, w, inc="TE")
                        rec_chain = e3
                        e4 = vec(lambda e: e.tensor_copy(recb[:, 0:QW],
                                                         pbk[6][0:64, 0:QW]),
                                 waits=[e3, recb_chain])
                        bk_read(6, e4)
                        e5 = vec(lambda e, hpi=hpi, hi=hi, qb=qb: e.tensor_mul(
                            z_sb[hpi][hi * 64:(hi + 1) * 64,
                                      qb * QW:(qb + 1) * QW],
                            pbk[4][0:64, 0:QW], recb[:, 0:QW]),
                            waits=[e4, zlast_t, z_free[hpi]])
                        bk_read(4, e5)
                        recb_chain = e5
                        oz_toks.append(e5)
                st["q2_free"][hp] = hp_qk_last
            buf0, s0 = os_.load(din[pre + "wo"]
                                [(quad * 2) * 128:(quad * 2 + 1) * 128, :])
            buf1, s1 = os_.load(din[pre + "wo"]
                                [(quad * 2 + 1) * 128:(quad * 2 + 2) * 128, :])
            for ob in range(FCH):
                for tbl in range(NCB):
                    bo_i = 6 + tbl % 2
                    col0 = tbl * CB
                    w = bk_acq(bo_i, [os_.ld[s0], os_.ld[s1]] + oz_toks)
                    mm(pbk[bo_i][:, 0:CB], buf0[:, ob * 128:(ob + 1) * 128],
                       z_sb[0][:, col0:col0 + CB], True, False, w)
                    t = mm(pbk[bo_i][:, 0:CB], buf1[:, ob * 128:(ob + 1) * 128],
                           z_sb[1][:, col0:col0 + CB], False, True, (), inc="TE")
                    if (not l0) and quad == 0:
                        at = vec(lambda e, ob=ob, col0=col0, bo_i=bo_i:
                                 e.scalar_tensor_tensor(
                                     xacc[ob][:, col0:col0 + CB],
                                     pbk[bo_i][:, 0:CB], bo_sb[:, ob:ob + 1],
                                     xacc[ob][:, col0:col0 + CB],
                                     ALU.add, ALU.add),
                                 waits=[t, st["xacc"][ob], cst])
                    else:
                        at = vec(lambda e, ob=ob, col0=col0, bo_i=bo_i:
                                 e.tensor_add(xacc[ob][:, col0:col0 + CB],
                                              pbk[bo_i][:, 0:CB],
                                              xacc[ob][:, col0:col0 + CB]),
                                 waits=[t, st["xacc"][ob]])
                    bk_read(bo_i, at)
                    st["xacc"][ob] = at
                    o_last = at
            os_.release(s0, o_last)
            os_.release(s1, o_last)
            for ki in range(8):
                st["B_free"][ki] = o_last
            attn_gate = o_last
        for s_i in range(2):
            P.bank_r[2 * s_i] = slot_exp[s_i]
            P.bank_r[2 * s_i + 1] = slot_exp[s_i]
        st["kgath_free"] = kg_last
        st["vgath_free"] = vg_last
        return o_last

    for lid in layers:
        xn_tok = emit_ln(lid)
        ck, cv = emit_qkv(lid, xn_tok)
        if lid == 0:
            emit_ffn_l0(xn_tok)
        else:
            emit_moe(xn_tok)
        lastacc = max(st["xacc"], key=lambda t: t[1])
        st["h_free"] = lastacc
        if f"mid{lid}" in taps:
            for fc in range(FCH):
                dma("sync", taps[f"mid{lid}"][fc * 128:(fc + 1) * 128, :],
                    xacc[fc][:], waits=[st["xacc"][fc]], sem="OUT")
            ttok = ("OUT", P.cnt["OUT"])
            st["xacc"] = [ttok] * FCH
        xn_dead = [lastacc]
        o_last = emit_attn(lid, ck, cv, xn_dead)
        st["arena_free"] = o_last
        if f"x{lid}" in taps:
            for fc in range(FCH):
                dma("sync", taps[f"x{lid}"][fc * 128:(fc + 1) * 128, :], xacc[fc][:],
                    waits=[st["xacc"][fc]], sem="OUT")

    for fc in range(FCH):
        dma("sync", out_fm[fc * 128:(fc + 1) * 128, :], xacc[fc][:],
            waits=[st["xacc"][fc]], sem="OUT")
    final_v = P.cnt["OUT"]
    P.emit("sync", lambda e: e.nop(), waits=[("OUT", final_v)])

    import contextlib
    ctx = contextlib.ExitStack()
    sems = {s: ctx.enter_context(nc.semaphore(s)) for s in sorted(P.sems)}
    runners = P.replay_into(sems)
    with nc.Block() as blk:
        blk.tensor(lambda e: runners["tensor"](e))
        blk.vector(lambda e: runners["vector"](e))
        blk.scalar(lambda e: runners["scalar"](e))
        blk.sync(lambda e: runners["sync"](e))
        blk.gpsimd(lambda e: runners["gpsimd"](e))
    ctx.close()
    return nc


_BUILD_CACHE = {}


def kernel(**inputs):
    cfg = CFG()
    in_maps, idxs = host_prepare(cfg, inputs)
    key = (cfg.S, cfg.FH, cfg.MH, cfg.layers)
    if key not in _BUILD_CACHE:
        _BUILD_CACHE[key] = build_program(cfg, cfg.layers)
    nc = _BUILD_CACHE[key]
    res = run_bass_kernel_spmd(nc, in_maps, core_ids=list(range(8)))
    return host_assemble(cfg, res.results, idxs)


# version 04
# speedup vs baseline: 1.3311x; 1.1141x over previous
import numpy as np
import ml_dtypes
from collections import defaultdict

import concourse.bass as bass
import concourse.mybir as mybir
from concourse.bass_utils import run_bass_kernel_spmd

F32 = mybir.dt.float32
BF16 = mybir.dt.bfloat16
I32 = mybir.dt.int32
AF = mybir.ActivationFunctionType
ALU = mybir.AluOpType
AX = mybir.AxisListType
BF16NP = ml_dtypes.bfloat16


class CFG:
    def __init__(self, S=8192, FH=3072, MH=4096, layers=(0, 1), debug_taps=()):
        self.S, self.FH, self.MH = S, FH, MH
        self.D, self.H, self.DH, self.E, self.B, self.NC = 1024, 16, 64, 8, 2, 8
        self.layers = tuple(layers)
        self.debug_taps = tuple(debug_taps)
        self.TPC = self.S * self.B // self.NC
        self.CB = min(512, self.TPC)
        self.NCB = self.TPC // self.CB
        self.QW = self.CB
        self.NQB = self.TPC // self.QW
        self.NTB = self.TPC // 128
        self.TP = min(1024, self.TPC)
        self.NTP = self.TPC // self.TP
        self.YW = min(512, self.TP)
        self.NYW = self.TP // self.YW
        self.FCH = self.D // 128
        self.NHB_F = self.FH // 128
        self.NHB_M = self.MH // 128
        self.KTPR = self.TPC // 128
        assert self.FH % 256 == 0 and self.MH % 512 == 0 and self.TPC % 128 == 0


ENGINES = ("tensor", "vector", "scalar", "sync", "gpsimd")


def tmax(*toks):
    best = {}
    for t in toks:
        if t is None:
            continue
        s, v = t
        if v > best.get(s, -1):
            best[s] = v
    return [(s, v) for s, v in best.items()]


class Prog:
    def __init__(self):
        self.ev = {e: [] for e in ENGINES}
        self.cnt = defaultdict(int)
        self.sems = set()
        self.bank_r = {}

    def emit(self, eng, fn, waits=(), inc=None, amt=1):
        w = [t for t in waits if t is not None]
        tok = None
        if inc is not None:
            self.sems.add(inc)
            self.cnt[inc] += amt
            tok = (inc, self.cnt[inc])
        self.ev[eng].append((fn, w, inc, amt))
        return tok

    def replay_into(self, sems):
        waited = {e: defaultdict(int) for e in ENGINES}

        def mk(eng_name):
            def run(eng):
                for fn, waits, inc, amt in self.ev[eng_name]:
                    for s, v in waits:
                        if v > waited[eng_name][s]:
                            eng.wait_ge(sems[s], v)
                            waited[eng_name][s] = v
                    inst = fn(eng)
                    if inc is not None:
                        inst.then_inc(sems[inc], amt)
            return run
        return {e: mk(e) for e in ENGINES}


class Stream:

    def __init__(self, P, q, tag, bufs):
        self.P, self.q, self.bufs = P, q, bufs
        self.tag = tag
        self.i = 0
        self.free = [[] for _ in bufs]
        self.ld = [None] * len(bufs)

    def load(self, src_ap):
        s = self.i % len(self.bufs)
        self.i += 1
        buf = self.bufs[s]
        self.ld[s] = self.P.emit(
            self.q, lambda e, d=buf, src=src_ap: e.dma_start(out=d[:], in_=src),
            waits=self.free[s], inc=f"{self.tag}{s}", amt=16)
        return buf, s

    def release(self, s, tok):
        self.free[s] = [tok]

    def gate(self, tok):
        self.free = [[tok] for _ in self.bufs]



def host_prepare(cfg, inp):
    D, H, DH, E, S = cfg.D, cfg.H, cfg.DH, cfg.E, cfg.S
    sh = {}
    if 0 in cfg.layers:
        sh["l0_wq"] = np.ascontiguousarray(
            np.asarray(inp["L0_attn_W_Q"]).transpose(2, 0, 1).reshape(D, H * DH))
        sh["l0_wk"] = np.ascontiguousarray(
            np.asarray(inp["L0_attn_W_K"]).transpose(2, 0, 1).reshape(D, H * DH))
        sh["l0_wv"] = np.ascontiguousarray(
            np.asarray(inp["L0_attn_W_V"]).transpose(2, 0, 1).reshape(D, H * DH))
        sh["l0_wo"] = np.ascontiguousarray(np.asarray(inp["L0_attn_W_O"]).T)
        sh["l0_wg"] = np.ascontiguousarray(np.asarray(inp["L0_ffn_w_gate"]).T)
        sh["l0_wu"] = np.ascontiguousarray(np.asarray(inp["L0_ffn_w_up"]).T)
        sh["l0_wd"] = np.ascontiguousarray(np.asarray(inp["L0_ffn_w_down"]).T)
    if 1 in cfg.layers:
        wqkv = np.asarray(inp["L1_attn_Wqkv"]).reshape(H, 3 * DH, D)
        bqkv = np.asarray(inp["L1_attn_bqkv"]).reshape(H, 3 * DH)
        sh["l1_wq"] = np.ascontiguousarray(
            wqkv[:, 0:DH, :].transpose(2, 0, 1).reshape(D, H * DH))
        sh["l1_wk"] = np.ascontiguousarray(
            wqkv[:, DH:2 * DH, :].transpose(2, 0, 1).reshape(D, H * DH))
        sh["l1_wv"] = np.ascontiguousarray(
            wqkv[:, 2 * DH:, :].transpose(2, 0, 1).reshape(D, H * DH))
        sh["l1_bq"] = np.ascontiguousarray(bqkv[:, 0:DH].reshape(8, 128).T)
        sh["l1_bk"] = np.ascontiguousarray(bqkv[:, DH:2 * DH].reshape(8, 128).T)
        sh["l1_bv"] = np.ascontiguousarray(bqkv[:, 2 * DH:].reshape(1, H * DH))
        sh["l1_wo"] = np.ascontiguousarray(np.asarray(inp["L1_attn_Wo"]).T)
        sh["l1_bo"] = np.ascontiguousarray(
            np.asarray(inp["L1_attn_bo"]).reshape(8, 128).T)
        sh["l1_rw"] = np.ascontiguousarray(np.asarray(inp["L1_ffn_router_w"]).T)
        w1 = np.asarray(inp["L1_ffn_w1"])
        b1 = np.asarray(inp["L1_ffn_b1"])
        w2 = np.asarray(inp["L1_ffn_w2"])
        w1aug = np.zeros((E, D + 128, cfg.MH), np.float32)
        for e in range(E):
            w1aug[e, :D, :] = w1[e].T
            w1aug[e, D, :] = b1[e]
        sh["l1_w1"] = w1aug
        sh["l1_w2"] = np.ascontiguousarray(w2.transpose(0, 2, 1))
        sh["l1_b2"] = np.ascontiguousarray(np.asarray(inp["L1_ffn_b2"]))

    sh["fconst"] = np.broadcast_to(
        np.array([0.0, 1.0, 1e-5, -0.5, 0.125, 0.5, -1.0, 1.0 / 127.0],
                 np.float32), (128, 8)).copy()
    sh["ones_col"] = np.ones((128, 1), np.float32)
    sh["ones_r1"] = np.ones((1, 128), np.float32)
    sh["ones_r1_bf"] = np.ones((1, 128), BF16NP)
    sh["ones_row_bf"] = np.ones((1, 512), BF16NP)
    sh["iota_e"] = np.broadcast_to(np.arange(8, dtype=np.float32), (128, 8)).copy()
    sel8 = np.zeros((8, 8 * 128), np.float32)
    for e in range(8):
        sel8[e, e * 128:(e + 1) * 128] = 1.0
    sh["sel8"] = sel8
    sh["ident"] = np.eye(128, dtype=np.float32)

    x = np.asarray(inp["L0_attn_x"])
    in_maps, idxs = [], []
    kk = np.arange(128)[:, None]
    qq = np.arange(cfg.QW)[None, :]
    for c in range(8):
        b, j = divmod(c, 4)
        idx = j + 4 * np.arange(cfg.TPC)
        idxs.append((b, idx))
        m = dict(sh)
        m["x_fm"] = np.ascontiguousarray(x[b, idx, :].T)
        masks = np.zeros((128, 4 * cfg.QW), BF16NP)
        for r in range(4):
            d = 0 if r <= j else 1
            masks[:, r * cfg.QW:(r + 1) * cfg.QW] = (kk <= qq - d)
        m["masks"] = masks
        in_maps.append(m)
    return in_maps, idxs


def host_assemble(cfg, results, idxs, key="out_fm"):
    out = np.zeros((cfg.B, cfg.S, cfg.D), np.float32)
    for c in range(8):
        b, idx = idxs[c]
        out[b, idx, :] = results[c][key].T
    return out


OFF_END = 212832


def build_program(cfg, layers):
    nc = bass.Bass("TRN2", target_bir_lowering=False, debug=False)
    D, H, DH, E, TPC = cfg.D, cfg.H, cfg.DH, cfg.E, cfg.TPC
    FCH, CB, NCB, QW, NQB = cfg.FCH, cfg.CB, cfg.NCB, cfg.QW, cfg.NQB
    TP, NTP, YW, NYW, KTPR = cfg.TP, cfg.NTP, cfg.YW, cfg.NYW, cfg.KTPR
    NTB = cfg.NTB

    OFF_A = 0
    OFF_B = OFF_A + FCH * TPC * 4
    OFF_C = OFF_B + FCH * TPC * 2
    OFF_D = OFF_C + FCH * TPC * 2
    OFF_SM = OFF_END - 19456

    din = {}
    def dram_in(name, shape, dt):
        din[name] = nc.dram_tensor(name, list(shape), dt, kind="ExternalInput").ap()

    dram_in("x_fm", (D, TPC), F32)
    dram_in("fconst", (128, 8), F32)
    for n, s, dt in [("ones_col", (128, 1), F32), ("ones_r1", (1, 128), F32),
                     ("ones_r1_bf", (1, 128), BF16), ("ones_row_bf", (1, 512), BF16),
                     ("iota_e", (128, 8), F32), ("ident", (128, 128), F32),
                     ("masks", (128, 4 * QW), BF16), ("sel8", (8, 1024), F32)]:
        dram_in(n, s, dt)
    if 0 in layers:
        for n, s, dt in [("l0_wq", (D, D), F32), ("l0_wk", (D, D), F32),
                         ("l0_wv", (D, D), F32), ("l0_wo", (D, D), F32),
                         ("l0_wg", (D, cfg.FH), I32), ("l0_wu", (D, cfg.FH), I32),
                         ("l0_wd", (cfg.FH, D), I32)]:
            dram_in(n, s, dt)
    if 1 in layers:
        for n, s, dt in [("l1_wq", (D, D), F32), ("l1_wk", (D, D), F32),
                         ("l1_wv", (D, D), F32), ("l1_bq", (128, 8), F32),
                         ("l1_bk", (128, 8), F32), ("l1_bv", (1, D), F32),
                         ("l1_wo", (D, D), F32), ("l1_bo", (128, 8), F32),
                         ("l1_rw", (D, 8), F32), ("l1_w1", (E, D + 128, cfg.MH), F32),
                         ("l1_w2", (E, cfg.MH, D), F32), ("l1_b2", (E, D), F32)]:
            dram_in(n, s, dt)

    out_fm = nc.dram_tensor("out_fm", [D, TPC], F32, kind="ExternalOutput").ap()
    gate_tap = (nc.dram_tensor("tap_gates", [8, TPC], F32, kind="ExternalOutput").ap()
                if "gates" in cfg.debug_taps else None)
    h_tap = (nc.dram_tensor("tap_h", [cfg.MH, TP], F32, kind="ExternalOutput").ap()
             if "moe_h" in cfg.debug_taps else None)
    taps = {t: nc.dram_tensor("tap_" + t, [D, TPC], F32, kind="ExternalOutput").ap()
            for t in cfg.debug_taps if t != "gates"}

    k_contrib = nc.dram_tensor("k_contrib", [D + 1, TPC], BF16, kind="Internal").ap()
    k_gath = nc.dram_tensor("k_gath", [4 * (D + 1), TPC], BF16, kind="Internal").ap()
    v_contrib = nc.dram_tensor("v_contrib", [4 * TPC, 260], BF16, kind="Internal").ap()
    v_gath = nc.dram_tensor("v_gath", [16 * TPC, 260], BF16, kind="Internal").ap()
    q_park = nc.dram_tensor("q_park", [D, TPC], BF16, kind="Internal").ap()

    SBUF_BASE = 16512

    def sb(name, shape, dt, off):
        nbytes = int(np.prod(shape[1:])) * mybir.dt.size(dt)
        assert off >= 0 and off + nbytes <= OFF_END, (name, off, nbytes)
        return nc.alloc_sbuf_tensor_at(name, list(shape), dt,
                                       offset=SBUF_BASE + off)

    xacc = [sb(f"xacc{i}", (128, TPC), F32, OFF_A + i * TPC * 4) for i in range(FCH)]
    xn = [sb(f"xn{i}", (128, TPC), BF16, OFF_B + i * TPC * 2) for i in range(FCH)]
    kwin = [sb(f"kw{i}", (128, TPC), BF16, OFF_B + i * TPC * 2) for i in range(8)]
    q2 = [sb(f"q2_{i}", (128, TPC), BF16, OFF_C + i * TPC * 2) for i in range(8)]

    sm = [OFF_SM]
    def sbs(name, shape, dt):
        h = sb(name, shape, dt, sm[0])
        sm[0] += (int(np.prod(shape[1:])) * mybir.dt.size(dt) + 31) & ~31
        assert sm[0] <= OFF_END, name
        return h

    fconst_sb = sbs("c_fconst", (128, 8), F32)
    FCONSTS = [0.0, 1.0, 1e-5, -0.5, 0.125, 0.5, -1.0, 1.0 / 127.0]
    for i, v in enumerate(FCONSTS):
        nc.const_aps.register(F32, v, fconst_sb[:, i:i + 1])
    ones_col = sbs("c_ones", (128, 1), F32)
    ones_r1 = sbs("c_or1", (1, 128), F32)
    ones_r1_bf = sbs("c_or1b", (1, 128), BF16)
    ones_row_bf = sbs("c_orow", (1, 512), BF16)
    iota_e = sbs("c_iota", (128, 8), F32)
    sel8_sb = sbs("c_sel8", (8, 1024), F32)
    ident = sbs("c_ident", (128, 128), F32)
    masks_sb = sbs("c_masks", (128, 4 * QW), BF16)
    bq_sb = sbs("bq", (128, 8), F32)
    bk_sb = sbs("bk", (128, 8), F32)
    bo_sb = sbs("bo", (128, 8), F32)
    bv_sb = sbs("bv", (1, D), BF16)
    rw_sb = sbs("rw", (128, 64), BF16)
    b2_sb = sbs("b2", (8, D), BF16)
    qn_sb = sbs("qn", (1, 8), F32)
    kn_sb = sbs("kn", (1, 8), F32)
    kn4_sb = sbs("kn4", (1, 32), F32)
    ntmp = sbs("ntmp", (1, 16), F32)
    msc = sbs("msc", (1, 8), F32)
    negM1 = sbs("negM1", (1, 1), F32)
    negM = sbs("negM", (128, 1), F32)
    recrow = sbs("recrow", (1, 512), F32)

    AL = lambda x: (x + 31) & ~31
    xsq = sb("xsq", (128, TPC), F32, OFF_D)
    mean_bc = sb("mean_bc", (128, TPC), F32, OFF_D + TPC * 4)
    rstd_bc = sb("rstd_bc", (128, TPC), F32, OFF_D + 2 * TPC * 4)
    mrow = sb("mrow", (1, TPC), F32, OFF_D + 3 * TPC * 4)
    vrow = sb("vrow", (1, TPC), F32, OFF_D + 3 * TPC * 4 + AL(TPC * 4))
    srow = sb("srow", (1, TPC), F32, OFF_D + 3 * TPC * 4 + 2 * AL(TPC * 4))
    o1 = OFF_D + AL(TPC * 4)
    wslab = [sb(f"wsl{i}", (128, FCH, 128), BF16, o1 + i * FCH * 256) for i in range(2)]
    o2 = o1 + 2 * FCH * 256
    vslab = [sb(f"vsl{i}", (128, FCH, 512), BF16, o2 + i * FCH * 1024) for i in range(2)]
    o3 = o2 + 2 * FCH * 1024
    k2tmp = [sb(f"k2t{i}", (128, TPC), BF16, o3 + i * TPC * 2) for i in range(2)]
    o4 = o3 + 2 * TPC * 2
    vstage = [sb(f"vst{i}", (128, 4, 260), BF16, o4 + i * 2080) for i in range(2)]

    HCAP = (8 * TPC) // TP

    def h_tiles(n, tag):
        out = []
        for i in range(n):
            off = OFF_C + i * TP * 2 if i < HCAP else OFF_D + (i - HCAP) * TP * 2
            out.append(sb(f"{tag}{i}", (128, TP), BF16, off))
        return out

    if 0 in layers:
        h_l0 = h_tiles(cfg.NHB_F, "hf")
        f1 = OFF_D + max(0, cfg.NHB_F - HCAP) * TP * 2
        gslab = [sb(f"gsl{i}", (128, FCH, 128), BF16, f1 + i * FCH * 256)
                 for i in range(2)]
        uslab = [sb(f"usl{i}", (128, FCH, 128), BF16, f1 + 2 * FCH * 256 + i * FCH * 256)
                 for i in range(2)]
        g_sb = [sb(f"g_s{i}", (128, CB), BF16, f1 + 4 * FCH * 256 + i * CB * 2)
                for i in range(2)]
        dslab = [sb(f"dsl{i}", (128, cfg.NHB_F, 128), BF16,
                    f1 + 4 * FCH * 256 + 2 * CB * 2 + i * cfg.NHB_F * 256)
                 for i in range(2)]
    if 1 in layers:
        h_moe = h_tiles(cfg.NHB_M, "hm")
        m0 = OFF_D + max(0, cfg.NHB_M - HCAP) * TP * 2
        m1slab = [sb(f"m1s{i}", (128, 9, 128), BF16, m0 + i * 9 * 256)
                  for i in range(3)]
        m1e = m0 + 3 * 9 * 256
        m2slab = [sb(f"m2s{i}", (128, 2, 512), BF16, m1e + i * 2 * 1024)
                  for i in range(2)]
        m2e = m1e + 2 * 2 * 1024
        gb_sb = [sb("gb0", (128, TP), F32, m2e)]
        gbe = m2e + TP * 4
        cT_sb = sb("cT", (8, TPC), F32, gbe)
        cTb_sb = sb("cTb", (8, TPC), BF16, gbe + TPC * 4)
        r0 = gbe + TPC * 4 + TPC * 2
        U32 = mybir.dt.uint32
        lg_sb = sb("lg", (128, 8), F32, r0)
        r_m8 = sb("r_m8", (128, 8), F32, r0 + 32)
        r_i8 = sb("r_i8", (128, 8), U32, r0 + 64)
        r_if = sb("r_if", (128, 2), F32, r0 + 96)
        r_oh1 = sb("r_oh1", (128, 8), F32, r0 + 128)
        r_oh2 = sb("r_oh2", (128, 8), F32, r0 + 160)
        r_neg = sb("r_neg", (128, 1), F32, r0 + 192)
        r_p = sb("r_p", (128, 8), F32, r0 + 224)
        r_s = sb("r_s", (128, 1), F32, r0 + 256)
        r_t = sb("r_t", (128, 1), F32, r0 + 288)
        r_cf = sb("r_cf", (128, 8), F32, r0 + 320)

    vwstride = (KTPR * 520 + 31) & ~31
    vw = [sb(f"vw{i}", (128, KTPR, 260), BF16, OFF_D + i * vwstride)
          for i in range(4)]
    a0 = OFF_D + 4 * vwstride
    z_sb = [sb(f"z{i}", (128, TPC), BF16, a0 + i * TPC * 2) for i in range(2)]
    a1 = a0 + 2 * TPC * 2
    p_buf = [sb(f"pbf{i}", (128, 2 * QW), BF16, a1 + i * 2 * QW * 2)
             for i in range(2)]
    a3 = a1 + 5 * QW * 2
    oslab = [sb(f"osl{i}", (128, D), BF16, a3 + i * D * 2) for i in range(4)]
    a4 = a3 + 4 * D * 2
    recb = sb("recb", (64, 512), F32, a4)
    assert a4 + 2048 <= OFF_SM, (a4, OFF_SM)

    pq = [nc.alloc_psum_tensor(f"pq{i}", [128, 1024], F32) for i in range(2)]
    pb_s = [nc.alloc_psum_tensor(f"bank{i}", [128, 512], F32) for i in range(4, 8)]
    pbk = [pq[0][:, 0:512], pq[0][:, 512:1024],
           pq[1][:, 0:512], pq[1][:, 512:1024]] + [h[:] for h in pb_s]

    P = Prog()

    def dma(q, dst, src, waits=(), sem="MISC"):
        return P.emit(q, lambda e, d=dst, s=src: e.dma_start(out=d, in_=s),
                      waits=waits, inc=sem, amt=16)

    def mm(out, lhsT, rhs, start, stop, waits=(), inc=None):
        return P.emit("tensor",
                      lambda e, o=out, l=lhsT, r=rhs, a=start, b=stop:
                      e.matmul(o, l, r, start=a, stop=b),
                      waits=waits, inc=inc)

    def bk_acq(i, waits=()):
        t = P.bank_r.get(i)
        return (list(waits) + [t]) if t else list(waits)

    def bk_read(i, tok):
        P.bank_r[i] = tok

    def vec(fn, waits=(), inc="DV"):
        return P.emit("vector", fn, waits=waits, inc=inc)

    def act(fn, waits=(), inc="AC"):
        return P.emit("scalar", fn, waits=list(waits) + [cst], inc=inc)

    dma("sync", fconst_sb[:], din["fconst"], sem="CST")
    for h, name in [(ones_col, "ones_col"), (ones_r1, "ones_r1"),
                    (iota_e, "iota_e"), (ident, "ident"),
                    (ones_r1_bf, "ones_r1_bf"), (ones_row_bf, "ones_row_bf"),
                    (masks_sb, "masks"), (sel8_sb, "sel8")]:
        dma("sync", h[:], din[name], sem="CST")
    if 1 in layers:
        for h, name in [(bq_sb, "l1_bq"), (bk_sb, "l1_bk"), (bo_sb, "l1_bo")]:
            dma("sync", h[:], din[name], sem="CST")
        for h, name in [(bv_sb, "l1_bv"), (b2_sb, "l1_b2")]:
            dma("gpsimd", h[:], din[name], sem="CSG")
        dma("gpsimd", rw_sb.ap().rearrange("p (fc e) -> p fc e", e=8),
            din["l1_rw"].rearrange("(fc p) e -> p fc e", p=128), sem="CSG")
    cst = ("CST", P.cnt["CST"])
    cstg = ("CSG", P.cnt["CSG"]) if P.cnt["CSG"] else None

    for fc in range(FCH):
        dma("sync", xacc[fc][:], din["x_fm"][fc * 128:(fc + 1) * 128, :], sem="XL")
    xacc_tok = [("XL", P.cnt["XL"])] * FCH

    zt = P.emit("vector", lambda e: e.memset(mrow[0:1, :], 0.0), inc="DV")
    norm_zero = dma("gpsimd", k_contrib[D:D + 1, :], mrow[0:1, :], waits=[zt],
                    sem="KN")

    st = {
        "xacc": xacc_tok,
        "B_free": [None] * 8,
        "q2_free": [None] * 8,
        "h_free": None,
        "arena_free": None,
        "kgath_free": None,
        "vgath_free": None,
        "qpark_free": None,
        "contrib_free": None,
        "k2tmp_free": [None, None],
        "vstage_free": [None, None],
        "xsq_gate": None,
        "wstream": Stream(P, "gpsimd", "sW", wslab),
        "vstream": Stream(P, "gpsimd", "sV", vslab),
        "ostream": Stream(P, "gpsimd", "sO", oslab),
    }
    if 0 in layers:
        st["gs"] = Stream(P, "gpsimd", "sG", gslab)
        st["us"] = Stream(P, "gpsimd", "sU", uslab)
        st["ds"] = Stream(P, "gpsimd", "sD", dslab)
    if 1 in layers:
        st["m1s"] = Stream(P, "gpsimd", "s1", m1slab)
        st["m2s"] = Stream(P, "gpsimd", "s2", m2slab)

    def emit_ln(lid):
        arena_w = [st["arena_free"]] if st["arena_free"] else []
        sum_tok = [None] * NCB
        sq_tok_f = [None] * NCB
        last_sq = st["xsq_gate"]
        for fc in range(FCH):
            for tb in range(NCB):
                w = [st["xacc"][fc], cst]
                if fc == 0:
                    w = bk_acq(tb, w)
                t = mm(pbk[tb][0:1, 0:CB], ones_col[:],
                       xacc[fc][:, tb * CB:(tb + 1) * CB],
                       start=(fc == 0), stop=(fc == FCH - 1),
                       waits=w, inc="TE" if fc == FCH - 1 else None)
                if fc == FCH - 1:
                    sum_tok[tb] = t
            w = [st["xacc"][fc], last_sq] + (arena_w if fc == 0 else [])
            s = act(lambda e, fc=fc: e.activation(xsq[:], xacc[fc][:], AF.Square),
                    waits=w)
            for tb in range(NCB):
                w = [s]
                if fc == 0:
                    w = bk_acq(4 + tb, w)
                t = mm(pbk[4 + tb][0:1, 0:CB], ones_col[:],
                       xsq[:, tb * CB:(tb + 1) * CB],
                       start=(fc == 0), stop=(fc == FCH - 1),
                       waits=w, inc="TE")
                if fc == FCH - 1:
                    sq_tok_f[tb] = t
                last_sq = t
        m_toks = []
        for tb in range(NCB):
            t = vec(lambda e, tb=tb: e.tensor_scalar_mul(
                mrow[0:1, tb * CB:(tb + 1) * CB], pbk[tb][0:1, 0:CB], 1.0 / D),
                waits=[sum_tok[tb], norm_zero] + arena_w)
            bk_read(tb, t)
            m_toks.append(t)
        msq = vec(lambda e: e.tensor_mul(srow[0:1, :], mrow[0:1, :], mrow[0:1, :]),
                  waits=m_toks)
        v_toks = []
        for tb in range(NCB):
            t = vec(lambda e, tb=tb: e.scalar_tensor_tensor(
                vrow[0:1, tb * CB:(tb + 1) * CB], pbk[4 + tb][0:1, 0:CB],
                1.0 / D, srow[0:1, tb * CB:(tb + 1) * CB], ALU.mult, ALU.subtract),
                waits=[sq_tok_f[tb], msq])
            bk_read(4 + tb, t)
            v_toks.append(t)
        t_ln = act(lambda e: e.activation(srow[0:1, :], vrow[0:1, :], AF.Ln,
                                          bias=1e-5), waits=v_toks + [msq])
        t_rs = act(lambda e: e.activation(srow[0:1, :], srow[0:1, :], AF.Exp,
                                          scale=-0.5), waits=[t_ln])
        bc = []
        for tb in range(NCB):
            w = bk_acq(tb, [m_toks[tb], cst])
            t1 = mm(pbk[tb][:, 0:CB], ones_r1[0:1, :],
                    mrow[0:1, tb * CB:(tb + 1) * CB], True, True, w, inc="TE")
            w = bk_acq(4 + tb, [t_rs])
            t2 = mm(pbk[4 + tb][:, 0:CB], ones_r1[0:1, :],
                    srow[0:1, tb * CB:(tb + 1) * CB], True, True, w, inc="TE")
            c1 = vec(lambda e, tb=tb: e.tensor_copy(
                mean_bc[:, tb * CB:(tb + 1) * CB], pbk[tb][:, 0:CB]), waits=[t1])
            c2 = vec(lambda e, tb=tb: e.tensor_copy(
                rstd_bc[:, tb * CB:(tb + 1) * CB], pbk[4 + tb][:, 0:CB]), waits=[t2])
            bk_read(tb, c1)
            bk_read(4 + tb, c2)
            bc += [c1, c2]
        xn_tok = []
        for fc in range(FCH):
            t1 = vec(lambda e, fc=fc: e.tensor_sub(xsq[:], xacc[fc][:], mean_bc[:]),
                     waits=bc + [st["xacc"][fc], last_sq])
            w2 = [t1] + ([st["B_free"][fc]] if st["B_free"][fc] else [])
            t2 = vec(lambda e, fc=fc: e.tensor_mul(xn[fc][:], xsq[:], rstd_bc[:]),
                     waits=w2)
            xn_tok.append(t2)
            last_sq = t2
        st["xsq_gate"] = last_sq
        return xn_tok

    def emit_qkv(lid, xn_tok):
        l0 = (lid == 0)
        pre = f"l{lid}_"
        ws = st["wstream"]
        g_ = xn_tok[-1]
        ws.gate(g_)
        st["vstream"].gate(g_)
        st["k2tmp_free"] = [g_] * 2
        st["vstage_free"] = [g_] * 2

        def proj_qk(which):
            contrib_toks = []
            norm_done = None
            for hp in range(8):
                buf, slot = ws.load(
                    din[pre + ("wq" if which == "q" else "wk")]
                    [:, hp * 128:(hp + 1) * 128].rearrange("(g p) c -> p g c", p=128))
                ld = ws.ld[slot]
                if which == "q":
                    dest, dw = q2[hp], st["q2_free"][hp]
                else:
                    dest = k2tmp[hp % 2]
                    dw = contrib_toks[hp - 2] if hp >= 2 else st["k2tmp_free"][hp % 2]
                evac = []
                for tb in range(NCB):
                    bi = tb % 2
                    w = bk_acq(bi, [ld] + xn_tok)
                    t = None
                    for fc in range(FCH):
                        t = mm(pbk[bi][:, 0:CB], buf[:, fc, :],
                               xn[fc][:, tb * CB:(tb + 1) * CB],
                               start=(fc == 0), stop=(fc == FCH - 1),
                               waits=w if fc == 0 else (),
                               inc="TE" if fc == FCH - 1 else None)
                    ew = [t] + ([dw] if dw and tb == 0 else [])
                    if not l0:
                        b = bq_sb if which == "q" else bk_sb
                        et = vec(lambda e, d=dest, tb=tb, bi=bi, b=b, hp=hp:
                                 e.tensor_scalar_add(d[:, tb * CB:(tb + 1) * CB],
                                                     pbk[bi][:, 0:CB],
                                                     b[:, hp:hp + 1]),
                                 waits=ew + [cst])
                    else:
                        et = vec(lambda e, d=dest, tb=tb, bi=bi:
                                 e.tensor_copy(d[:, tb * CB:(tb + 1) * CB],
                                               pbk[bi][:, 0:CB]), waits=ew)
                    bk_read(bi, et)
                    evac.append(et)
                ws.release(slot, evac[-1])
                s = act(lambda e, d=dest: e.activation(xsq[:], d[:], AF.Square),
                        waits=evac + [st["xsq_gate"]])
                red = []
                for tb in range(NCB):
                    w = bk_acq(6, [s])
                    t = mm(pbk[6][0:1, 0:CB], ones_col[:],
                           xsq[:, tb * CB:(tb + 1) * CB], True, True, w, inc="TE")
                    rt = vec(lambda e, tb=tb: e.reduce_max(
                        ntmp[0:1, tb:tb + 1], pbk[6][0:1, 0:CB], AX.X),
                        waits=[t, norm_done])
                    bk_read(6, rt)
                    red.append(rt)
                dstrow = qn_sb if which == "q" else kn_sb
                norm_done = vec(lambda e, d=dstrow, hp=hp: e.reduce_max(
                    d[0:1, hp:hp + 1], ntmp[0:1, 0:NCB], AX.X), waits=red)
                st["xsq_gate"] = s
                st["xsq_gate"] = red[-1]
                if which == "q":
                    pk = dma("sync", q_park[hp * 128:(hp + 1) * 128, :], q2[hp][:],
                             waits=evac + ([st["qpark_free"]] if st["qpark_free"]
                                           else []), sem=f"pk{hp}")
                    st["q2_free"][hp] = pk
                else:
                    cw = [st["contrib_free"]] if st["contrib_free"] else []
                    kc = dma("sync", k_contrib[hp * 128:(hp + 1) * 128, :],
                             k2tmp[hp % 2][:], waits=evac + cw, sem=f"kc{hp % 2}")
                    contrib_toks.append(kc)
            return contrib_toks, norm_done

        _, qn_done = proj_qk("q")
        k_contrib_toks, kn_done = proj_qk("k")
        st["k2tmp_free"] = [k_contrib_toks[-2], k_contrib_toks[-1]]
        kc_norm = dma("gpsimd", k_contrib[D:D + 1, 0:8], kn_sb[0:1, 0:8],
                      waits=[kn_done], sem="KN")
        k_contrib_toks.append(kc_norm)

        vs = st["vstream"]
        ones_done = [None, None]
        for i in range(2):
            prev = st["vstage_free"][i]
            for q4 in range(4):
                prev = vec(lambda e, i=i, q4=q4: e.memset(
                    vstage[i].ap()[:, q4, :].rearrange("p (h c) -> p h c", c=65)
                    [:, :, 64:65], 1.0), waits=[prev])
            ones_done[i] = prev
        v_contrib_toks = []
        ev_last = None
        for vh in range(2):
            buf, slot = vs.load(din[pre + "wv"][:, vh * 512:(vh + 1) * 512]
                                .rearrange("(g p) c -> p g c", p=128))
            ld = vs.ld[slot]
            for tb in range(NTB):
                bi = 2 + tb % 2
                w = bk_acq(bi, [ld] + xn_tok)
                t = None
                for fc in range(FCH):
                    last = (fc == FCH - 1) and l0
                    t = mm(pbk[bi][:, 0:512], xn[fc][:, tb * 128:(tb + 1) * 128],
                           buf[:, fc, :], start=(fc == 0), stop=last,
                           waits=w if fc == 0 else (), inc="TE" if last else None)
                if not l0:
                    t = mm(pbk[bi][:, 0:512], ones_r1_bf[0:1, :],
                           bv_sb[0:1, vh * 512:(vh + 1) * 512], False, True,
                           waits=[cstg], inc="TE")
                si = tb % 2
                sw = [t, ones_done[si]]
                if len(v_contrib_toks) >= 2:
                    sw.append(v_contrib_toks[-2])
                ev = vec(lambda e, si=si, bi=bi, vh=vh: e.tensor_copy(
                    vstage[si].ap()[:, 2 * vh:2 * vh + 2, :]
                    .rearrange("p q (h c) -> p (q h) c", c=65)[:, :, 0:64],
                    pbk[bi][:, 0:512]), waits=sw)
                bk_read(bi, ev)
                cw = [st["contrib_free"]] if st["contrib_free"] else []
                vc = dma("sync",
                         v_contrib.rearrange("(q t p) c -> p q t c", q=4, p=128)
                         [:, 2 * vh:2 * vh + 2, tb, :],
                         vstage[si][:, 2 * vh:2 * vh + 2, :], waits=[ev] + cw,
                         sem=f"vc{si}")
                v_contrib_toks.append(vc)
                ev_last = ev
            vs.release(slot, ev_last)
        st["vstage_free"] = [v_contrib_toks[-2], v_contrib_toks[-1]]

        coll_w = k_contrib_toks + [v_contrib_toks[-1], v_contrib_toks[-2]]
        if st["kgath_free"]:
            coll_w.append(st["kgath_free"])
        ck = P.emit("gpsimd", lambda e: e.collective_compute(
            "AllGather", ALU.bypass, ins=[k_contrib.opt()], outs=[k_gath.opt()],
            replica_groups=[[0, 1, 2, 3], [4, 5, 6, 7]]), waits=coll_w, inc="CCK")
        cv_w = [v_contrib_toks[-1]]
        if st["vgath_free"]:
            cv_w.append(st["vgath_free"])
        cv = P.emit("gpsimd", lambda e: e.collective_compute(
            "AllGather", ALU.bypass, ins=[v_contrib.opt()], outs=[v_gath.opt()],
            replica_groups=[[0, 1, 2, 3], [4, 5, 6, 7]]), waits=cv_w, inc="CCV")
        st["contrib_free"] = cv
        return ck, cv

    def emit_ffn_l0(xn_tok):
        gs, us, ds = st["gs"], st["us"], st["ds"]
        NHB = cfg.NHB_F
        h_tok = [None] * NHB
        g_free = [None, None]
        for tp in range(NTP):
            for hb in range(NHB):
                b1, s1 = gs.load(din["l0_wg"][:, hb * 128:(hb + 1) * 128]
                                 .rearrange("(g p) c -> p g c", p=128))
                b2, s2 = us.load(din["l0_wu"][:, hb * 128:(hb + 1) * 128]
                                 .rearrange("(g p) c -> p g c", p=128))
                for tbl in range(TP // CB):
                    col0 = tp * TP + tbl * CB
                    bg, bu = tbl % 2, 2 + tbl % 2
                    w = bk_acq(bg, [gs.ld[s1]] + xn_tok)
                    tg = None
                    for fc in range(FCH):
                        tg = mm(pbk[bg][:, 0:CB], b1[:, fc, :],
                                xn[fc][:, col0:col0 + CB],
                                start=(fc == 0), stop=(fc == FCH - 1),
                                waits=w if fc == 0 else (),
                                inc="TE" if fc == FCH - 1 else None)
                    w = bk_acq(bu, [us.ld[s2]])
                    tu = None
                    for fc in range(FCH):
                        tu = mm(pbk[bu][:, 0:CB], b2[:, fc, :],
                                xn[fc][:, col0:col0 + CB],
                                start=(fc == 0), stop=(fc == FCH - 1),
                                waits=w if fc == 0 else (),
                                inc="TE" if fc == FCH - 1 else None)
                    gi = tbl % 2
                    ga = act(lambda e, gi=gi, bg=bg: e.activation(
                        g_sb[gi][:, 0:CB], pbk[bg][:, 0:CB], AF.Silu,
                        scale=1.0 / 127.0), waits=[tg, g_free[gi]])
                    bk_read(bg, ga)
                    hw = [tu, ga]
                    if tp == 0 and tbl == 0:
                        if hb < HCAP:
                            hw.append(st["q2_free"][hb * TP // TPC])
                        elif st["arena_free"]:
                            hw.append(st["arena_free"])
                    ht = vec(lambda e, hb=hb, tbl=tbl, bu=bu, gi=gi:
                             e.scalar_tensor_tensor(
                                 h_l0[hb][:, tbl * CB:(tbl + 1) * CB],
                                 pbk[bu][:, 0:CB], 1.0 / 127.0, g_sb[gi][:, 0:CB],
                                 ALU.mult, ALU.mult), waits=hw)
                    bk_read(bu, ht)
                    g_free[gi] = ht
                    h_tok[hb] = ht
                gs.release(s1, h_tok[hb])
                us.release(s2, h_tok[hb])
            for ob in range(FCH):
                b3, s3 = ds.load(din["l0_wd"][:, ob * 128:(ob + 1) * 128]
                                 .rearrange("(g p) c -> p g c", p=128))
                last_at = None
                for tbl in range(TP // CB):
                    col0 = tp * TP + tbl * CB
                    bd = 4 + tbl % 2
                    w = bk_acq(bd, [ds.ld[s3]] + h_tok)
                    t = None
                    for hb in range(NHB):
                        t = mm(pbk[bd][:, 0:CB], b3[:, hb, :],
                               h_l0[hb][:, tbl * CB:(tbl + 1) * CB],
                               start=(hb == 0), stop=(hb == NHB - 1),
                               waits=w if hb == 0 else (),
                               inc="TE" if hb == NHB - 1 else None)
                    at = vec(lambda e, ob=ob, col0=col0, bd=bd:
                             e.scalar_tensor_tensor(
                                 xacc[ob][:, col0:col0 + CB], pbk[bd][:, 0:CB],
                                 1.0 / 127.0, xacc[ob][:, col0:col0 + CB],
                                 ALU.mult, ALU.add),
                             waits=[t, st["xacc"][ob]])
                    bk_read(bd, at)
                    st["xacc"][ob] = at
                    last_at = at
                ds.release(s3, last_at)
        return

    def emit_moe(xn_tok):
        NHB = cfg.NHB_M
        arena_w = [st["arena_free"]] if st["arena_free"] else []
        cT_toks = []
        rprev = None
        for tb in range(NTB):
            w = bk_acq(7, xn_tok + [cstg] + arena_w)
            t = None
            for fc in range(FCH):
                t = mm(pbk[7][:, 0:8], xn[fc][:, tb * 128:(tb + 1) * 128],
                       rw_sb[:, fc * 8:(fc + 1) * 8],
                       start=(fc == 0), stop=(fc == FCH - 1),
                       waits=w if fc == 0 else (), inc="TE" if fc == FCH - 1 else None)
            c0 = vec(lambda e: e.tensor_copy(lg_sb[:], pbk[7][:, 0:8]),
                     waits=[t, rprev])
            bk_read(7, c0)
            t1 = vec(lambda e: e.max(r_m8[:], lg_sb[:]), waits=[c0])
            t2 = vec(lambda e: e.max_index(r_i8[:], r_m8[:], lg_sb[:]), waits=[t1])
            t2b = vec(lambda e: e.tensor_copy(r_if[:], r_i8[:, 0:2]), waits=[t2])
            t3 = vec(lambda e: e.tensor_single_scalar(r_oh1[:], iota_e[:],
                                                      r_if[:, 0:1], ALU.is_equal),
                     waits=[t2b, cst])
            t7 = vec(lambda e: e.tensor_single_scalar(r_oh2[:], iota_e[:],
                                                      r_if[:, 1:2], ALU.is_equal),
                     waits=[t3])
            t8 = vec(lambda e: e.tensor_scalar_mul(r_neg[:], r_m8[:, 0:1], -1.0),
                     waits=[t1])
            t9 = act(lambda e: e.activation(r_p[:], lg_sb[:], AF.Exp,
                                            bias=r_neg[:, 0:1]), waits=[t8, t2])
            ta = vec(lambda e: e.reduce_sum(r_s[:], r_p[:], AX.X), waits=[t9])
            tb_ = vec(lambda e: e.reciprocal(r_t[:], r_s[:]), waits=[ta])
            tc = vec(lambda e: e.tensor_scalar_mul(r_p[:], r_p[:], r_t[:, 0:1]),
                     waits=[tb_])
            td = vec(lambda e: e.tensor_add(r_oh2[:], r_oh1[:], r_oh2[:]),
                     waits=[t7, tc])
            te_ = vec(lambda e: e.tensor_mul(r_p[:], r_p[:], r_oh2[:]), waits=[td])
            tf = vec(lambda e: e.reduce_sum(r_s[:], r_p[:], AX.X), waits=[te_])
            tg = vec(lambda e: e.tensor_scalar_add(r_s[:], r_s[:], 1e-9), waits=[tf])
            th = vec(lambda e: e.reciprocal(r_t[:], r_s[:]), waits=[tg])
            ti = vec(lambda e: e.tensor_scalar_mul(r_cf[:], r_p[:], r_t[:, 0:1]),
                     waits=[th])
            w = bk_acq(6, [ti, cst])
            tj = P.emit("tensor", lambda e: e.transpose(pbk[6][0:8, 0:128],
                                                        r_cf[:], ident[:]),
                        waits=w, inc="TE")
            tk = vec(lambda e, tb=tb: e.tensor_copy(
                cT_sb[0:8, tb * 128:(tb + 1) * 128], pbk[6][0:8, 0:128]),
                waits=[tj] + arena_w)
            bk_read(6, tk)
            tl = vec(lambda e, tb=tb: e.tensor_copy(
                cTb_sb[0:8, tb * 128:(tb + 1) * 128],
                cT_sb[0:8, tb * 128:(tb + 1) * 128]), waits=[tk])
            cT_toks.append(tl)
            rprev = ti

        if gate_tap is not None:
            dma("sync", gate_tap, cT_sb[0:8, :], waits=cT_toks, sem="OUT")
        m1s, m2s = st["m1s"], st["m2s"]
        h_tok = [None] * NHB
        gb_tok = [None]
        for e_i in range(E):
            for tp in range(NTP):
                gi = 0
                gt = None
                for yw in range(TP // YW):
                    w = bk_acq(6, cT_toks + ([gb_tok[gi]] if gb_tok[gi] else [])
                               + (arena_w if e_i == 0 else []))
                    t = mm(pbk[6][:, 0:YW],
                           sel8_sb[0:8, e_i * 128:(e_i + 1) * 128],
                           cT_sb[0:8, tp * TP + yw * YW:tp * TP + (yw + 1) * YW],
                           True, True, waits=w, inc="TE")
                    gt = vec(lambda e, gi=gi, yw=yw: e.tensor_copy(
                        gb_sb[gi][:, yw * YW:(yw + 1) * YW], pbk[6][:, 0:YW]),
                        waits=[t])
                    bk_read(6, gt)
                gb_tok[gi] = gt
                for hb in range(NHB):
                    b1, s1 = m1s.load(din["l1_w1"][e_i, :, hb * 128:(hb + 1) * 128]
                                      .rearrange("(g p) c -> p g c", p=128))
                    for tbl in range(TP // CB):
                        col0 = tp * TP + tbl * CB
                        bi = tbl % 2
                        w = bk_acq(bi, [m1s.ld[s1]] + xn_tok)
                        for fc in range(FCH):
                            mm(pbk[bi][:, 0:CB], b1[:, fc, :],
                               xn[fc][:, col0:col0 + CB],
                               start=(fc == 0), stop=False,
                               waits=w if fc == 0 else ())
                        t = mm(pbk[bi][:, 0:CB], b1[0:1, FCH, :],
                               ones_row_bf[0:1, 0:CB], False, True,
                               waits=[cst], inc="TE")
                        hw = [t, gb_tok[gi]]
                        if e_i == 0 and tp == 0 and tbl == 0:
                            if hb < HCAP:
                                hw.append(st["q2_free"][hb * TP // TPC])
                            elif st["arena_free"]:
                                hw.append(st["arena_free"])
                        ht = vec(lambda e, hb=hb, tbl=tbl, bi=bi, gi=gi:
                                 e.scalar_tensor_tensor(
                                     h_moe[hb][:, tbl * CB:(tbl + 1) * CB],
                                     pbk[bi][:, 0:CB], 0.0,
                                     gb_sb[gi][:, tbl * CB:(tbl + 1) * CB],
                                     ALU.max, ALU.mult), waits=hw)
                        bk_read(bi, ht)
                        h_tok[hb] = ht
                    m1s.release(s1, h_tok[hb])
                if h_tap is not None and e_i == 3 and tp == 0:
                    for hb in range(NHB):
                        dma("gpsimd", h_tap[hb * 128:(hb + 1) * 128, :],
                            h_moe[hb][:], waits=[h_tok[hb]], sem="HT")
                    htt = ("HT", P.cnt["HT"])
                    h_tok = [htt] * NHB
                ytiles = [(cb, th) for cb in range(4) for th in range(NYW)]
                for ch in range(2):
                    if e_i == 0:
                        for (cb, th) in ytiles:
                            bi = cb * NYW + th
                            w = bk_acq(bi, cT_toks + [cstg])
                            mm(pbk[bi][:, 0:YW],
                               b2_sb[0:8, ch * 512 + cb * 128:ch * 512 + (cb + 1) * 128],
                               cTb_sb[0:8, tp * TP + th * YW:tp * TP + (th + 1) * YW],
                               True, False, waits=w)
                    last = {}
                    for hbg in range(NHB // 2):
                        b2w, s2 = m2s.load(
                            din["l1_w2"][e_i, hbg * 256:(hbg + 1) * 256,
                                         ch * 512:(ch + 1) * 512]
                            .rearrange("(g p) c -> p g c", p=128))
                        slab_last = None
                        for g4 in range(2):
                            hb = hbg * 2 + g4
                            for (cb, th) in ytiles:
                                bi = cb * NYW + th
                                isend = (hb == NHB - 1)
                                isfirst = (e_i > 0 and hb == 0)
                                w = [m2s.ld[s2], h_tok[hb]] if (cb, th) == ytiles[0] else ()
                                if isfirst:
                                    w = bk_acq(bi, list(w) if w else [])
                                t = mm(pbk[bi][:, 0:YW],
                                       b2w[:, g4, cb * 128:(cb + 1) * 128],
                                       h_moe[hb][:, th * YW:(th + 1) * YW],
                                       isfirst, isend, waits=w,
                                       inc="TE" if isend or ((g4 == 1) and
                                                             (cb, th) == ytiles[-1])
                                       else None)
                                if isend:
                                    last[(cb, th)] = t
                                if (g4 == 1) and (cb, th) == ytiles[-1]:
                                    slab_last = t
                        m2s.release(s2, slab_last)
                    for (cb, th) in ytiles:
                        bi = cb * NYW + th
                        ob = ch * 4 + cb
                        col0 = tp * TP + th * YW
                        at = vec(lambda e, ob=ob, col0=col0, bi=bi:
                                 e.tensor_add(xacc[ob][:, col0:col0 + YW],
                                              pbk[bi][:, 0:YW],
                                              xacc[ob][:, col0:col0 + YW]),
                                 waits=[last[(cb, th)], st["xacc"][ob]])
                        bk_read(bi, at)
                        st["xacc"][ob] = at
        return

    def emit_attn(lid, ck, cv, xn_dead):
        l0 = (lid == 0)
        pre = f"l{lid}_"
        for r in range(4):
            dma("gpsimd", kn4_sb[0:1, r * 8:(r + 1) * 8],
                k_gath[r * (D + 1) + D:r * (D + 1) + D + 1, 0:8],
                waits=[ck], sem="K4")
        kn_toks = [("K4", P.cnt["K4"])]
        t1 = vec(lambda e: e.reduce_max(
            msc[0:1, 0:8],
            kn4_sb.ap()[0:1, 0:32].rearrange("a (r h) -> a h r", r=4), AX.X),
            waits=kn_toks)
        t2 = vec(lambda e: e.tensor_mul(msc[0:1, 0:8], msc[0:1, 0:8],
                                        qn_sb[0:1, 0:8]), waits=[t1])
        t3 = vec(lambda e: e.reduce_max(msc[0:1, 0:1], msc[0:1, 0:8], AX.X),
                 waits=[t2])
        t4 = act(lambda e: e.activation(msc[0:1, 0:1], msc[0:1, 0:1], AF.Ln),
                 waits=[t3])
        t5 = act(lambda e: e.activation(msc[0:1, 0:1], msc[0:1, 0:1], AF.Exp,
                                        scale=0.5), waits=[t4])
        t6 = vec(lambda e: e.tensor_scalar(negM1[0:1, 0:1], msc[0:1, 0:1],
                                           -0.1275, -1.0, ALU.mult, ALU.add),
                 waits=[t5])
        w = bk_acq(6, [t6, cst])
        t7 = mm(pbk[6][:, 0:1], ones_r1[0:1, :], negM1[0:1, 0:1], True, True, w,
                inc="TE")
        negM_tok = vec(lambda e: e.tensor_copy(negM[:], pbk[6][:, 0:1]), waits=[t7])
        bk_read(6, negM_tok)

        unpark = []
        for hp in range(8):
            w = [st["q2_free"][hp], st["h_free"]]
            unpark.append(dma("sync", q2[hp][:], q_park[hp * 128:(hp + 1) * 128, :],
                              waits=w, sem=f"up{hp}"))
        st["qpark_free"] = unpark[-1]

        def jmm(n=1):
            for _ in range(n):
                mm(pbk[5][:, 0:QW], masks_sb[:, 0:128], masks_sb[:, 0:QW],
                   True, True, waits=[cst])

        os_ = st["ostream"]
        gate = st["h_free"]
        os_.gate(gate)
        o_last = None
        kg_last = None
        vg_last = None
        rec_chain = gate
        recb_chain = gate
        attn_gate = gate
        slot_exp = [gate, gate]
        slot_av = [gate, gate]
        for quad in range(4):
            k_toks = {}
            for hpi in range(2):
                hp = quad * 2 + hpi
                for r in range(4):
                    ki = hpi * 4 + r
                    w = [ck] + ([st["B_free"][ki]] if st["B_free"][ki] else [])
                    if quad == 0:
                        w += xn_dead
                    t = dma("sync", kwin[ki][:],
                            k_gath[r * (D + 1) + hp * 128:
                                   r * (D + 1) + (hp + 1) * 128, :], waits=w,
                            sem=f"kw{ki}")
                    k_toks[(hpi, r)] = t
                    kg_last = t
            v_toks = {}
            for r in range(4):
                w = [cv, attn_gate]
                t = dma("sync", vw[r][:],
                        v_gath.rearrange("(r q t p) c -> p r q t c", r=4, q=4, p=128)
                        [:, r, quad, :, :], waits=w, sem=f"vq{r}")
                v_toks[r] = t
                vg_last = t
            oz_toks = []
            z_free = [attn_gate, attn_gate]
            for hpi in range(2):
                hp = quad * 2 + hpi
                hp_qk_last = None
                for hi in range(2):
                    hloc = 2 * hpi + hi
                    rlo, rhi = hi * 64, (hi + 1) * 64
                    for qb in range(NQB):
                        diag_kb = qb * QW // 128
                        if l0:
                            kl = [(r, kb) for r in range(4)
                                  for kb in range(diag_kb + 1)]
                        else:
                            kl = [(r, kb) for r in range(4) for kb in range(KTPR)]
                        pairs = [kl[i:i + 2] for i in range(0, len(kl), 2)]
                        np_ = len(pairs)
                        qk_tok = [None] * np_

                        def emit_qk(p_i):
                            pr = pairs[p_i]
                            slot = p_i % 2
                            for j, (r, kb) in enumerate(pr):
                                ww = [k_toks[(hpi, r)]]
                                if j == 0:
                                    ww += [slot_exp[slot], unpark[hp]]
                                qk_tok[p_i] = mm(
                                    pq[slot][:, j * QW:(j + 1) * QW],
                                    kwin[hpi * 4 + r][rlo:rhi,
                                                      kb * 128:(kb + 1) * 128],
                                    q2[hp][rlo:rhi, qb * QW:(qb + 1) * QW],
                                    True, True, waits=ww, inc="TE")

                        emit_qk(0)
                        zlast_t = None
                        for i in range(np_):
                            jmm(1)
                            if i + 1 < np_:
                                emit_qk(i + 1)
                            pr = pairs[i]
                            slot = i % 2
                            W = len(pr) * QW
                            et = act(lambda e, slot=slot, W=W: e.activation(
                                p_buf[slot][:, 0:W], pq[slot][:, 0:W], AF.Exp,
                                bias=negM[:, 0:1], scale=0.125),
                                waits=[qk_tok[i], negM_tok, slot_av[slot]])
                            slot_exp[slot] = et
                            half_tok = [et] * len(pr)
                            for j, (r, kb) in enumerate(pr):
                                if l0 and kb == diag_kb:
                                    half_tok[j] = vec(
                                        lambda e, slot=slot, j=j, r=r:
                                        e.tensor_mul(
                                            p_buf[slot][:, j * QW:(j + 1) * QW],
                                            p_buf[slot][:, j * QW:(j + 1) * QW],
                                            masks_sb[:, r * QW:(r + 1) * QW]),
                                        waits=[et, cst])
                            for j, (r, kb) in enumerate(pr):
                                first = (i == 0 and j == 0)
                                last = (i == np_ - 1 and j == len(pr) - 1)
                                w2 = [half_tok[j], v_toks[r]]
                                if first:
                                    w2 = bk_acq(4, w2)
                                zt = mm(pbk[4][0:65, 0:QW],
                                        vw[r][:, kb, hloc * 65:(hloc + 1) * 65],
                                        p_buf[slot][:, j * QW:(j + 1) * QW],
                                        start=first, stop=last,
                                        waits=w2, inc="TE")
                                slot_av[slot] = zt
                                if last:
                                    zlast_t = zt
                        hp_qk_last = qk_tok[np_ - 1]
                        jmm(6)
                        e1 = act(lambda e: e.activation(
                            recrow[0:1, 0:QW], pbk[4][64:65, 0:QW], AF.Ln),
                            waits=[zlast_t, rec_chain])
                        e2 = act(lambda e: e.activation(
                            recrow[0:1, 0:QW], recrow[0:1, 0:QW], AF.Exp,
                            scale=-1.0), waits=[e1])
                        w = bk_acq(6, [e2])
                        e3 = mm(pbk[6][0:64, 0:QW], ones_r1[0:1, 0:64],
                                recrow[0:1, 0:QW], True, True, w, inc="TE")
                        rec_chain = e3
                        e4 = vec(lambda e: e.tensor_copy(recb[:, 0:QW],
                                                         pbk[6][0:64, 0:QW]),
                                 waits=[e3, recb_chain])
                        bk_read(6, e4)
                        e5 = vec(lambda e, hpi=hpi, hi=hi, qb=qb: e.tensor_mul(
                            z_sb[hpi][hi * 64:(hi + 1) * 64,
                                      qb * QW:(qb + 1) * QW],
                            pbk[4][0:64, 0:QW], recb[:, 0:QW]),
                            waits=[e4, zlast_t, z_free[hpi]])
                        bk_read(4, e5)
                        recb_chain = e5
                        oz_toks.append(e5)
                st["q2_free"][hp] = hp_qk_last
            buf0, s0 = os_.load(din[pre + "wo"]
                                [(quad * 2) * 128:(quad * 2 + 1) * 128, :])
            buf1, s1 = os_.load(din[pre + "wo"]
                                [(quad * 2 + 1) * 128:(quad * 2 + 2) * 128, :])
            for ob in range(FCH):
                for tbl in range(NCB):
                    bo_i = 6 + tbl % 2
                    col0 = tbl * CB
                    w = bk_acq(bo_i, [os_.ld[s0], os_.ld[s1]] + oz_toks)
                    mm(pbk[bo_i][:, 0:CB], buf0[:, ob * 128:(ob + 1) * 128],
                       z_sb[0][:, col0:col0 + CB], True, False, w)
                    t = mm(pbk[bo_i][:, 0:CB], buf1[:, ob * 128:(ob + 1) * 128],
                           z_sb[1][:, col0:col0 + CB], False, True, (), inc="TE")
                    if (not l0) and quad == 0:
                        at = vec(lambda e, ob=ob, col0=col0, bo_i=bo_i:
                                 e.scalar_tensor_tensor(
                                     xacc[ob][:, col0:col0 + CB],
                                     pbk[bo_i][:, 0:CB], bo_sb[:, ob:ob + 1],
                                     xacc[ob][:, col0:col0 + CB],
                                     ALU.add, ALU.add),
                                 waits=[t, st["xacc"][ob], cst])
                    else:
                        at = vec(lambda e, ob=ob, col0=col0, bo_i=bo_i:
                                 e.tensor_add(xacc[ob][:, col0:col0 + CB],
                                              pbk[bo_i][:, 0:CB],
                                              xacc[ob][:, col0:col0 + CB]),
                                 waits=[t, st["xacc"][ob]])
                    bk_read(bo_i, at)
                    st["xacc"][ob] = at
                    o_last = at
            os_.release(s0, o_last)
            os_.release(s1, o_last)
            for ki in range(8):
                st["B_free"][ki] = o_last
            attn_gate = o_last
        for s_i in range(2):
            P.bank_r[2 * s_i] = slot_exp[s_i]
            P.bank_r[2 * s_i + 1] = slot_exp[s_i]
        st["kgath_free"] = kg_last
        st["vgath_free"] = vg_last
        return o_last

    for lid in layers:
        xn_tok = emit_ln(lid)
        ck, cv = emit_qkv(lid, xn_tok)
        if lid == 0:
            emit_ffn_l0(xn_tok)
        else:
            emit_moe(xn_tok)
        lastacc = max(st["xacc"], key=lambda t: t[1])
        st["h_free"] = lastacc
        if f"mid{lid}" in taps:
            for fc in range(FCH):
                dma("sync", taps[f"mid{lid}"][fc * 128:(fc + 1) * 128, :],
                    xacc[fc][:], waits=[st["xacc"][fc]], sem="OUT")
            ttok = ("OUT", P.cnt["OUT"])
            st["xacc"] = [ttok] * FCH
        xn_dead = [lastacc]
        o_last = emit_attn(lid, ck, cv, xn_dead)
        st["arena_free"] = o_last
        if f"x{lid}" in taps:
            for fc in range(FCH):
                dma("sync", taps[f"x{lid}"][fc * 128:(fc + 1) * 128, :], xacc[fc][:],
                    waits=[st["xacc"][fc]], sem="OUT")

    for fc in range(FCH):
        dma("sync", out_fm[fc * 128:(fc + 1) * 128, :], xacc[fc][:],
            waits=[st["xacc"][fc]], sem="OUT")
    final_v = P.cnt["OUT"]
    P.emit("sync", lambda e: e.nop(), waits=[("OUT", final_v)])

    import contextlib
    ctx = contextlib.ExitStack()
    sems = {s: ctx.enter_context(nc.semaphore(s)) for s in sorted(P.sems)}
    runners = P.replay_into(sems)
    with nc.Block() as blk:
        blk.tensor(lambda e: runners["tensor"](e))
        blk.vector(lambda e: runners["vector"](e))
        blk.scalar(lambda e: runners["scalar"](e))
        blk.sync(lambda e: runners["sync"](e))
        blk.gpsimd(lambda e: runners["gpsimd"](e))
    ctx.close()
    return nc


_BUILD_CACHE = {}


def kernel(**inputs):
    cfg = CFG()
    in_maps, idxs = host_prepare(cfg, inputs)
    key = (cfg.S, cfg.FH, cfg.MH, cfg.layers)
    if key not in _BUILD_CACHE:
        _BUILD_CACHE[key] = build_program(cfg, cfg.layers)
    nc = _BUILD_CACHE[key]
    res = run_bass_kernel_spmd(nc, in_maps, core_ids=list(range(8)))
    return host_assemble(cfg, res.results, idxs)


# version 05
# speedup vs baseline: 1.3586x; 1.0206x over previous
import numpy as np
import ml_dtypes
from collections import defaultdict

import concourse.bass as bass
import concourse.mybir as mybir
from concourse.bass_utils import run_bass_kernel_spmd

F32 = mybir.dt.float32
BF16 = mybir.dt.bfloat16
I32 = mybir.dt.int32
AF = mybir.ActivationFunctionType
ALU = mybir.AluOpType
AX = mybir.AxisListType
BF16NP = ml_dtypes.bfloat16


class CFG:
    def __init__(self, S=8192, FH=3072, MH=4096, layers=(0, 1), debug_taps=()):
        self.S, self.FH, self.MH = S, FH, MH
        self.D, self.H, self.DH, self.E, self.B, self.NC = 1024, 16, 64, 8, 2, 8
        self.layers = tuple(layers)
        self.debug_taps = tuple(debug_taps)
        self.TPC = self.S * self.B // self.NC
        self.CB = min(512, self.TPC)
        self.NCB = self.TPC // self.CB
        self.QW = self.CB
        self.NQB = self.TPC // self.QW
        self.NTB = self.TPC // 128
        self.TP = min(1024, self.TPC)
        self.NTP = self.TPC // self.TP
        self.YW = min(512, self.TP)
        self.NYW = self.TP // self.YW
        self.FCH = self.D // 128
        self.NHB_F = self.FH // 128
        self.NHB_M = self.MH // 128
        self.KTPR = self.TPC // 128
        assert self.FH % 256 == 0 and self.MH % 512 == 0 and self.TPC % 128 == 0


ENGINES = ("tensor", "vector", "scalar", "sync", "gpsimd")


def tmax(*toks):
    best = {}
    for t in toks:
        if t is None:
            continue
        s, v = t
        if v > best.get(s, -1):
            best[s] = v
    return [(s, v) for s, v in best.items()]


class Prog:
    def __init__(self):
        self.ev = {e: [] for e in ENGINES}
        self.cnt = defaultdict(int)
        self.sems = set()
        self.bank_r = {}

    def emit(self, eng, fn, waits=(), inc=None, amt=1):
        w = [t for t in waits if t is not None]
        tok = None
        if inc is not None:
            self.sems.add(inc)
            self.cnt[inc] += amt
            tok = (inc, self.cnt[inc])
        self.ev[eng].append((fn, w, inc, amt))
        return tok

    def replay_into(self, sems):
        waited = {e: defaultdict(int) for e in ENGINES}

        def mk(eng_name):
            def run(eng):
                for fn, waits, inc, amt in self.ev[eng_name]:
                    for s, v in waits:
                        if v > waited[eng_name][s]:
                            eng.wait_ge(sems[s], v)
                            waited[eng_name][s] = v
                    inst = fn(eng)
                    if inc is not None:
                        inst.then_inc(sems[inc], amt)
            return run
        return {e: mk(e) for e in ENGINES}


class Stream:

    def __init__(self, P, q, tag, bufs):
        self.P, self.q, self.bufs = P, q, bufs
        self.tag = tag
        self.i = 0
        self.free = [[] for _ in bufs]
        self.ld = [None] * len(bufs)

    def load(self, src_ap):
        s = self.i % len(self.bufs)
        self.i += 1
        buf = self.bufs[s]
        self.ld[s] = self.P.emit(
            self.q, lambda e, d=buf, src=src_ap: e.dma_start(out=d[:], in_=src),
            waits=self.free[s], inc=f"{self.tag}{s}", amt=16)
        return buf, s

    def release(self, s, tok):
        self.free[s] = [tok]

    def gate(self, tok):
        self.free = [[tok] for _ in self.bufs]



def host_prepare(cfg, inp):
    D, H, DH, E, S = cfg.D, cfg.H, cfg.DH, cfg.E, cfg.S
    sh = {}
    if 0 in cfg.layers:
        sh["l0_wq"] = np.ascontiguousarray(
            np.asarray(inp["L0_attn_W_Q"]).transpose(2, 0, 1).reshape(D, H * DH))
        sh["l0_wk"] = np.ascontiguousarray(
            np.asarray(inp["L0_attn_W_K"]).transpose(2, 0, 1).reshape(D, H * DH))
        sh["l0_wv"] = np.ascontiguousarray(
            np.asarray(inp["L0_attn_W_V"]).transpose(2, 0, 1).reshape(D, H * DH))
        sh["l0_wo"] = np.ascontiguousarray(np.asarray(inp["L0_attn_W_O"]).T)
        sh["l0_wg"] = np.ascontiguousarray(np.asarray(inp["L0_ffn_w_gate"]).T)
        sh["l0_wu"] = np.ascontiguousarray(np.asarray(inp["L0_ffn_w_up"]).T)
        sh["l0_wd"] = np.ascontiguousarray(np.asarray(inp["L0_ffn_w_down"]).T)
    if 1 in cfg.layers:
        wqkv = np.asarray(inp["L1_attn_Wqkv"]).reshape(H, 3 * DH, D)
        bqkv = np.asarray(inp["L1_attn_bqkv"]).reshape(H, 3 * DH)
        sh["l1_wq"] = np.ascontiguousarray(
            wqkv[:, 0:DH, :].transpose(2, 0, 1).reshape(D, H * DH))
        sh["l1_wk"] = np.ascontiguousarray(
            wqkv[:, DH:2 * DH, :].transpose(2, 0, 1).reshape(D, H * DH))
        sh["l1_wv"] = np.ascontiguousarray(
            wqkv[:, 2 * DH:, :].transpose(2, 0, 1).reshape(D, H * DH))
        sh["l1_bq"] = np.ascontiguousarray(bqkv[:, 0:DH].reshape(8, 128).T)
        sh["l1_bk"] = np.ascontiguousarray(bqkv[:, DH:2 * DH].reshape(8, 128).T)
        sh["l1_bv"] = np.ascontiguousarray(bqkv[:, 2 * DH:].reshape(1, H * DH))
        sh["l1_wo"] = np.ascontiguousarray(np.asarray(inp["L1_attn_Wo"]).T)
        sh["l1_bo"] = np.ascontiguousarray(
            np.asarray(inp["L1_attn_bo"]).reshape(8, 128).T)
        sh["l1_rw"] = np.ascontiguousarray(np.asarray(inp["L1_ffn_router_w"]).T)
        w1 = np.asarray(inp["L1_ffn_w1"])
        b1 = np.asarray(inp["L1_ffn_b1"])
        w2 = np.asarray(inp["L1_ffn_w2"])
        w1aug = np.zeros((E, D + 128, cfg.MH), np.float32)
        for e in range(E):
            w1aug[e, :D, :] = w1[e].T
            w1aug[e, D, :] = b1[e]
        sh["l1_w1"] = w1aug
        sh["l1_b1c"] = np.ascontiguousarray(
            b1.reshape(E, cfg.MH // 128, 128).transpose(2, 0, 1).reshape(128, -1))
        sh["l1_w2"] = np.ascontiguousarray(w2.transpose(0, 2, 1))
        sh["l1_b2"] = np.ascontiguousarray(np.asarray(inp["L1_ffn_b2"]))

    sh["fconst"] = np.broadcast_to(
        np.array([0.0, 1.0, 1e-5, -0.5, 0.125, 0.5, -1.0, 1.0 / 127.0],
                 np.float32), (128, 8)).copy()
    sh["ones_col"] = np.ones((128, 1), np.float32)
    sh["ones_r1"] = np.ones((1, 128), np.float32)
    sh["ones_r1_bf"] = np.ones((1, 128), BF16NP)
    sh["ones_row_bf"] = np.ones((1, 512), BF16NP)
    sh["iota_e"] = np.broadcast_to(np.arange(8, dtype=np.float32), (128, 8)).copy()
    sel8 = np.zeros((8, 8 * 128), np.float32)
    for e in range(8):
        sel8[e, e * 128:(e + 1) * 128] = 1.0
    sh["sel8"] = sel8
    sh["ident"] = np.eye(128, dtype=np.float32)

    x = np.asarray(inp["L0_attn_x"])
    in_maps, idxs = [], []
    kk = np.arange(128)[:, None]
    qq = np.arange(cfg.QW)[None, :]
    for c in range(8):
        b, j = divmod(c, 4)
        idx = j + 4 * np.arange(cfg.TPC)
        idxs.append((b, idx))
        m = dict(sh)
        m["x_fm"] = np.ascontiguousarray(x[b, idx, :].T)
        masks = np.zeros((128, 4 * cfg.QW), BF16NP)
        for r in range(4):
            d = 0 if r <= j else 1
            masks[:, r * cfg.QW:(r + 1) * cfg.QW] = (kk <= qq - d)
        m["masks"] = masks
        in_maps.append(m)
    return in_maps, idxs


def host_assemble(cfg, results, idxs, key="out_fm"):
    out = np.zeros((cfg.B, cfg.S, cfg.D), np.float32)
    for c in range(8):
        b, idx = idxs[c]
        out[b, idx, :] = results[c][key].T
    return out


OFF_END = 212832


def build_program(cfg, layers):
    nc = bass.Bass("TRN2", target_bir_lowering=False, debug=False)
    D, H, DH, E, TPC = cfg.D, cfg.H, cfg.DH, cfg.E, cfg.TPC
    FCH, CB, NCB, QW, NQB = cfg.FCH, cfg.CB, cfg.NCB, cfg.QW, cfg.NQB
    TP, NTP, YW, NYW, KTPR = cfg.TP, cfg.NTP, cfg.YW, cfg.NYW, cfg.KTPR
    NTB = cfg.NTB

    OFF_A = 0
    OFF_B = OFF_A + FCH * TPC * 4
    OFF_C = OFF_B + FCH * TPC * 2
    OFF_D = OFF_C + FCH * TPC * 2
    OFF_SM = OFF_END - 19456

    din = {}
    def dram_in(name, shape, dt):
        din[name] = nc.dram_tensor(name, list(shape), dt, kind="ExternalInput").ap()

    dram_in("x_fm", (D, TPC), F32)
    dram_in("fconst", (128, 8), F32)
    for n, s, dt in [("ones_col", (128, 1), F32), ("ones_r1", (1, 128), F32),
                     ("ones_r1_bf", (1, 128), BF16), ("ones_row_bf", (1, 512), BF16),
                     ("iota_e", (128, 8), F32), ("ident", (128, 128), F32),
                     ("masks", (128, 4 * QW), BF16), ("sel8", (8, 1024), F32)]:
        dram_in(n, s, dt)
    if 0 in layers:
        for n, s, dt in [("l0_wq", (D, D), F32), ("l0_wk", (D, D), F32),
                         ("l0_wv", (D, D), F32), ("l0_wo", (D, D), F32),
                         ("l0_wg", (D, cfg.FH), I32), ("l0_wu", (D, cfg.FH), I32),
                         ("l0_wd", (cfg.FH, D), I32)]:
            dram_in(n, s, dt)
    if 1 in layers:
        for n, s, dt in [("l1_wq", (D, D), F32), ("l1_wk", (D, D), F32),
                         ("l1_wv", (D, D), F32), ("l1_bq", (128, 8), F32),
                         ("l1_bk", (128, 8), F32), ("l1_bv", (1, D), F32),
                         ("l1_wo", (D, D), F32), ("l1_bo", (128, 8), F32),
                         ("l1_rw", (D, 8), F32), ("l1_w1", (E, D + 128, cfg.MH), F32),
                         ("l1_b1c", (128, E * cfg.MH // 128), F32),
                         ("l1_w2", (E, cfg.MH, D), F32), ("l1_b2", (E, D), F32)]:
            dram_in(n, s, dt)

    out_fm = nc.dram_tensor("out_fm", [D, TPC], F32, kind="ExternalOutput").ap()
    gate_tap = (nc.dram_tensor("tap_gates", [8, TPC], F32, kind="ExternalOutput").ap()
                if "gates" in cfg.debug_taps else None)
    h_tap = (nc.dram_tensor("tap_h", [cfg.MH, TP], F32, kind="ExternalOutput").ap()
             if "moe_h" in cfg.debug_taps else None)
    taps = {t: nc.dram_tensor("tap_" + t, [D, TPC], F32, kind="ExternalOutput").ap()
            for t in cfg.debug_taps if t != "gates"}

    k_contrib = nc.dram_tensor("k_contrib", [D + 1, TPC], BF16, kind="Internal").ap()
    k_gath = nc.dram_tensor("k_gath", [4 * (D + 1), TPC], BF16, kind="Internal").ap()
    v_contrib = nc.dram_tensor("v_contrib", [4 * TPC, 260], BF16, kind="Internal").ap()
    v_gath = nc.dram_tensor("v_gath", [16 * TPC, 260], BF16, kind="Internal").ap()
    q_park = nc.dram_tensor("q_park", [D, TPC], BF16, kind="Internal").ap()

    SBUF_BASE = 16512

    def sb(name, shape, dt, off):
        nbytes = int(np.prod(shape[1:])) * mybir.dt.size(dt)
        assert off >= 0 and off + nbytes <= OFF_END, (name, off, nbytes)
        return nc.alloc_sbuf_tensor_at(name, list(shape), dt,
                                       offset=SBUF_BASE + off)

    xacc = [sb(f"xacc{i}", (128, TPC), F32, OFF_A + i * TPC * 4) for i in range(FCH)]
    xn = [sb(f"xn{i}", (128, TPC), BF16, OFF_B + i * TPC * 2) for i in range(FCH)]
    kwin = [sb(f"kw{i}", (128, TPC), BF16, OFF_B + i * TPC * 2) for i in range(8)]
    q2 = [sb(f"q2_{i}", (128, TPC), BF16, OFF_C + i * TPC * 2) for i in range(8)]

    sm = [OFF_SM]
    def sbs(name, shape, dt):
        h = sb(name, shape, dt, sm[0])
        sm[0] += (int(np.prod(shape[1:])) * mybir.dt.size(dt) + 31) & ~31
        assert sm[0] <= OFF_END, name
        return h

    fconst_sb = sbs("c_fconst", (128, 8), F32)
    FCONSTS = [0.0, 1.0, 1e-5, -0.5, 0.125, 0.5, -1.0, 1.0 / 127.0]
    for i, v in enumerate(FCONSTS):
        nc.const_aps.register(F32, v, fconst_sb[:, i:i + 1])
    ones_col = sbs("c_ones", (128, 1), F32)
    ones_r1 = sbs("c_or1", (1, 128), F32)
    ones_r1_bf = sbs("c_or1b", (1, 128), BF16)
    ones_row_bf = sbs("c_orow", (1, 512), BF16)
    iota_e = sbs("c_iota", (128, 8), F32)
    sel8_sb = sbs("c_sel8", (8, 1024), F32)
    ident = sbs("c_ident", (128, 128), F32)
    masks_sb = sbs("c_masks", (128, 4 * QW), BF16)
    bq_sb = sbs("bq", (128, 8), F32)
    bk_sb = sbs("bk", (128, 8), F32)
    bo_sb = sbs("bo", (128, 8), F32)
    bv_sb = sbs("bv", (1, D), BF16)
    rw_sb = sbs("rw", (128, 64), BF16)
    b1c_sb = sbs("b1c", (128, E * cfg.MH // 128), F32)
    b2_sb = sbs("b2", (8, D), BF16)
    qn_sb = sbs("qn", (1, 8), F32)
    kn_sb = sbs("kn", (1, 8), F32)
    kn4_sb = sbs("kn4", (1, 32), F32)
    ntmp = sbs("ntmp", (1, 16), F32)
    msc = sbs("msc", (1, 8), F32)
    negM1 = sbs("negM1", (1, 1), F32)
    negM = sbs("negM", (128, 1), F32)
    recrow = sbs("recrow", (1, 512), F32)

    AL = lambda x: (x + 31) & ~31
    xsq = sb("xsq", (128, TPC), F32, OFF_D)
    mean_bc = sb("mean_bc", (128, TPC), F32, OFF_D + TPC * 4)
    rstd_bc = sb("rstd_bc", (128, TPC), F32, OFF_D + 2 * TPC * 4)
    mrow = sb("mrow", (1, TPC), F32, OFF_D + 3 * TPC * 4)
    vrow = sb("vrow", (1, TPC), F32, OFF_D + 3 * TPC * 4 + AL(TPC * 4))
    srow = sb("srow", (1, TPC), F32, OFF_D + 3 * TPC * 4 + 2 * AL(TPC * 4))
    o1 = OFF_D + AL(TPC * 4)
    wslab = [sb(f"wsl{i}", (128, FCH, 128), BF16, o1 + i * FCH * 256) for i in range(3)]
    o2 = o1 + 3 * FCH * 256
    vslab = [sb(f"vsl{i}", (128, FCH, 512), BF16, o2 + i * FCH * 1024) for i in range(2)]
    o3 = o2 + 2 * FCH * 1024
    k2tmp = [sb(f"k2t{i}", (128, TPC), BF16, o3 + i * TPC * 2) for i in range(2)]
    o4 = o3 + 2 * TPC * 2
    vstage = [sb(f"vst{i}", (128, 4, 260), BF16, o4 + i * 2080) for i in range(2)]

    HCAP = (8 * TPC) // TP

    def h_tiles(n, tag):
        out = []
        for i in range(n):
            off = OFF_C + i * TP * 2 if i < HCAP else OFF_D + (i - HCAP) * TP * 2
            out.append(sb(f"{tag}{i}", (128, TP), BF16, off))
        return out

    if 0 in layers:
        h_l0 = h_tiles(cfg.NHB_F, "hf")
        f1 = OFF_D + max(0, cfg.NHB_F - HCAP) * TP * 2
        gslab = [sb(f"gsl{i}", (128, FCH, 128), BF16, f1 + i * FCH * 256)
                 for i in range(3)]
        uslab = [sb(f"usl{i}", (128, FCH, 128), BF16, f1 + 3 * FCH * 256 + i * FCH * 256)
                 for i in range(3)]
        g_sb = [sb(f"g_s{i}", (128, CB), BF16, f1 + 6 * FCH * 256 + i * CB * 2)
                for i in range(2)]
        dslab = [sb(f"dsl{i}", (128, cfg.NHB_F, 128), BF16,
                    f1 + 6 * FCH * 256 + 2 * CB * 2 + i * cfg.NHB_F * 256)
                 for i in range(3)]
    if 1 in layers:
        h_moe = h_tiles(cfg.NHB_M, "hm")
        m0 = OFF_D + max(0, cfg.NHB_M - HCAP) * TP * 2
        m1slab = [sb(f"m1s{i}", (128, 9, 128), BF16, m0 + i * 9 * 256)
                  for i in range(3)]
        m1e = m0 + 3 * 9 * 256
        m2slab = [sb(f"m2s{i}", (128, 2, 512), BF16, m1e + i * 2 * 1024)
                  for i in range(2)]
        m2e = m1e + 2 * 2 * 1024
        gb_sb = [sb("gb0", (128, TP), F32, m2e)]
        gbe = m2e + TP * 4
        cT_sb = sb("cT", (8, TPC), F32, gbe)
        cTb_sb = sb("cTb", (8, TPC), BF16, gbe + TPC * 4)
        rl_buf = sb("rl0", (128, CB), BF16, gbe + TPC * 4 + TPC * 2)
        r0 = gbe + TPC * 4 + TPC * 2 + CB * 2
        U32 = mybir.dt.uint32
        lg_sb = sb("lg", (128, 8), F32, r0)
        r_m8 = sb("r_m8", (128, 8), F32, r0 + 32)
        r_i8 = sb("r_i8", (128, 8), U32, r0 + 64)
        r_if = sb("r_if", (128, 2), F32, r0 + 96)
        r_oh1 = sb("r_oh1", (128, 8), F32, r0 + 128)
        r_oh2 = sb("r_oh2", (128, 8), F32, r0 + 160)
        r_neg = sb("r_neg", (128, 1), F32, r0 + 192)
        r_p = sb("r_p", (128, 8), F32, r0 + 224)
        r_s = sb("r_s", (128, 1), F32, r0 + 256)
        r_t = sb("r_t", (128, 1), F32, r0 + 288)
        r_cf = sb("r_cf", (128, 8), F32, r0 + 320)

    vwstride = (KTPR * 520 + 31) & ~31
    vw = [sb(f"vw{i}", (128, KTPR, 260), BF16, OFF_D + i * vwstride)
          for i in range(4)]
    a0 = OFF_D + 4 * vwstride
    z_sb = [sb(f"z{i}", (128, TPC), BF16, a0 + i * TPC * 2) for i in range(2)]
    a1 = a0 + 2 * TPC * 2
    p_buf = [sb(f"pbf{i}", (128, 2 * QW), BF16, a1 + i * 2 * QW * 2)
             for i in range(2)]
    a3 = a1 + 5 * QW * 2
    oslab = [sb(f"osl{i}", (128, D), BF16, a3 + i * D * 2) for i in range(4)]
    a4 = a3 + 4 * D * 2
    recb = sb("recb", (64, 512), F32, a4)
    assert a4 + 2048 <= OFF_SM, (a4, OFF_SM)

    pq = [nc.alloc_psum_tensor(f"pq{i}", [128, 1024], F32) for i in range(2)]
    pb_s = [nc.alloc_psum_tensor(f"bank{i}", [128, 512], F32) for i in range(4, 8)]
    pbk = [pq[0][:, 0:512], pq[0][:, 512:1024],
           pq[1][:, 0:512], pq[1][:, 512:1024]] + [h[:] for h in pb_s]

    P = Prog()

    def dma(q, dst, src, waits=(), sem="MISC"):
        return P.emit(q, lambda e, d=dst, s=src: e.dma_start(out=d, in_=s),
                      waits=waits, inc=sem, amt=16)

    def mm(out, lhsT, rhs, start, stop, waits=(), inc=None):
        return P.emit("tensor",
                      lambda e, o=out, l=lhsT, r=rhs, a=start, b=stop:
                      e.matmul(o, l, r, start=a, stop=b),
                      waits=waits, inc=inc)

    def bk_acq(i, waits=()):
        t = P.bank_r.get(i)
        return (list(waits) + [t]) if t else list(waits)

    def bk_read(i, tok):
        P.bank_r[i] = tok

    def vec(fn, waits=(), inc="DV"):
        return P.emit("vector", fn, waits=waits, inc=inc)

    def act(fn, waits=(), inc="AC"):
        return P.emit("scalar", fn, waits=list(waits) + [cst], inc=inc)

    dma("sync", fconst_sb[:], din["fconst"], sem="CST")
    for h, name in [(ones_col, "ones_col"), (ones_r1, "ones_r1"),
                    (iota_e, "iota_e"), (ident, "ident"),
                    (ones_r1_bf, "ones_r1_bf"), (ones_row_bf, "ones_row_bf"),
                    (masks_sb, "masks"), (sel8_sb, "sel8")]:
        dma("sync", h[:], din[name], sem="CST")
    if 1 in layers:
        for h, name in [(bq_sb, "l1_bq"), (bk_sb, "l1_bk"), (bo_sb, "l1_bo")]:
            dma("sync", h[:], din[name], sem="CST")
        for h, name in [(bv_sb, "l1_bv"), (b2_sb, "l1_b2")]:
            dma("gpsimd", h[:], din[name], sem="CSG")
        dma("sync", b1c_sb[:], din["l1_b1c"], sem="CST")
        dma("gpsimd", rw_sb.ap().rearrange("p (fc e) -> p fc e", e=8),
            din["l1_rw"].rearrange("(fc p) e -> p fc e", p=128), sem="CSG")
    cst = ("CST", P.cnt["CST"])
    cstg = ("CSG", P.cnt["CSG"]) if P.cnt["CSG"] else None

    for fc in range(FCH):
        dma("sync", xacc[fc][:], din["x_fm"][fc * 128:(fc + 1) * 128, :], sem="XL")
    xacc_tok = [("XL", P.cnt["XL"])] * FCH

    zt = P.emit("vector", lambda e: e.memset(mrow[0:1, :], 0.0), inc="DV")
    norm_zero = dma("gpsimd", k_contrib[D:D + 1, :], mrow[0:1, :], waits=[zt],
                    sem="KN")

    st = {
        "xacc": xacc_tok,
        "B_free": [None] * 8,
        "q2_free": [None] * 8,
        "h_free": None,
        "arena_free": None,
        "kgath_free": None,
        "vgath_free": None,
        "qpark_free": None,
        "contrib_free": None,
        "k2tmp_free": [None, None],
        "vstage_free": [None, None],
        "xsq_gate": None,
        "wstream": Stream(P, "gpsimd", "sW", wslab),
        "vstream": Stream(P, "gpsimd", "sV", vslab),
        "ostream": Stream(P, "gpsimd", "sO", oslab),
    }
    if 0 in layers:
        st["gs"] = Stream(P, "gpsimd", "sG", gslab)
        st["us"] = Stream(P, "gpsimd", "sU", uslab)
        st["ds"] = Stream(P, "gpsimd", "sD", dslab)
    if 1 in layers:
        st["m1s"] = Stream(P, "gpsimd", "s1", m1slab)
        st["m2s"] = Stream(P, "gpsimd", "s2", m2slab)

    def emit_ln(lid):
        arena_w = [st["arena_free"]] if st["arena_free"] else []
        sum_tok = [None] * NCB
        sq_tok_f = [None] * NCB
        last_sq = st["xsq_gate"]
        for fc in range(FCH):
            for tb in range(NCB):
                w = [st["xacc"][fc], cst]
                if fc == 0:
                    w = bk_acq(tb, w)
                t = mm(pbk[tb][0:1, 0:CB], ones_col[:],
                       xacc[fc][:, tb * CB:(tb + 1) * CB],
                       start=(fc == 0), stop=(fc == FCH - 1),
                       waits=w, inc="TE" if fc == FCH - 1 else None)
                if fc == FCH - 1:
                    sum_tok[tb] = t
            w = [st["xacc"][fc], last_sq] + (arena_w if fc == 0 else [])
            s = act(lambda e, fc=fc: e.activation(xsq[:], xacc[fc][:], AF.Square),
                    waits=w)
            for tb in range(NCB):
                w = [s]
                if fc == 0:
                    w = bk_acq(4 + tb, w)
                t = mm(pbk[4 + tb][0:1, 0:CB], ones_col[:],
                       xsq[:, tb * CB:(tb + 1) * CB],
                       start=(fc == 0), stop=(fc == FCH - 1),
                       waits=w, inc="TE")
                if fc == FCH - 1:
                    sq_tok_f[tb] = t
                last_sq = t
        m_toks = []
        for tb in range(NCB):
            t = vec(lambda e, tb=tb: e.tensor_scalar_mul(
                mrow[0:1, tb * CB:(tb + 1) * CB], pbk[tb][0:1, 0:CB], 1.0 / D),
                waits=[sum_tok[tb], norm_zero] + arena_w)
            bk_read(tb, t)
            m_toks.append(t)
        msq = vec(lambda e: e.tensor_mul(srow[0:1, :], mrow[0:1, :], mrow[0:1, :]),
                  waits=m_toks)
        v_toks = []
        for tb in range(NCB):
            t = vec(lambda e, tb=tb: e.scalar_tensor_tensor(
                vrow[0:1, tb * CB:(tb + 1) * CB], pbk[4 + tb][0:1, 0:CB],
                1.0 / D, srow[0:1, tb * CB:(tb + 1) * CB], ALU.mult, ALU.subtract),
                waits=[sq_tok_f[tb], msq])
            bk_read(4 + tb, t)
            v_toks.append(t)
        t_ln = act(lambda e: e.activation(srow[0:1, :], vrow[0:1, :], AF.Ln,
                                          bias=1e-5), waits=v_toks + [msq])
        t_rs = act(lambda e: e.activation(srow[0:1, :], srow[0:1, :], AF.Exp,
                                          scale=-0.5), waits=[t_ln])
        bc = []
        for tb in range(NCB):
            w = bk_acq(tb, [m_toks[tb], cst])
            t1 = mm(pbk[tb][:, 0:CB], ones_r1[0:1, :],
                    mrow[0:1, tb * CB:(tb + 1) * CB], True, True, w, inc="TE")
            w = bk_acq(4 + tb, [t_rs])
            t2 = mm(pbk[4 + tb][:, 0:CB], ones_r1[0:1, :],
                    srow[0:1, tb * CB:(tb + 1) * CB], True, True, w, inc="TE")
            c1 = vec(lambda e, tb=tb: e.tensor_copy(
                mean_bc[:, tb * CB:(tb + 1) * CB], pbk[tb][:, 0:CB]), waits=[t1])
            c2 = vec(lambda e, tb=tb: e.tensor_copy(
                rstd_bc[:, tb * CB:(tb + 1) * CB], pbk[4 + tb][:, 0:CB]), waits=[t2])
            bk_read(tb, c1)
            bk_read(4 + tb, c2)
            bc += [c1, c2]
        xn_tok = []
        for fc in range(FCH):
            t1 = vec(lambda e, fc=fc: e.tensor_sub(xsq[:], xacc[fc][:], mean_bc[:]),
                     waits=bc + [st["xacc"][fc], last_sq])
            w2 = [t1] + ([st["B_free"][fc]] if st["B_free"][fc] else [])
            t2 = vec(lambda e, fc=fc: e.tensor_mul(xn[fc][:], xsq[:], rstd_bc[:]),
                     waits=w2)
            xn_tok.append(t2)
            last_sq = t2
        st["xsq_gate"] = last_sq
        return xn_tok

    def emit_qkv(lid, xn_tok):
        l0 = (lid == 0)
        pre = f"l{lid}_"
        ws = st["wstream"]
        g_ = xn_tok[-1]
        ws.gate(g_)
        st["vstream"].gate(g_)
        st["k2tmp_free"] = [g_] * 2
        st["vstage_free"] = [g_] * 2

        def proj_qk(which):
            contrib_toks = []
            norm_done = None
            for hp in range(8):
                buf, slot = ws.load(
                    din[pre + ("wq" if which == "q" else "wk")]
                    [:, hp * 128:(hp + 1) * 128].rearrange("(g p) c -> p g c", p=128))
                ld = ws.ld[slot]
                if which == "q":
                    dest, dw = q2[hp], st["q2_free"][hp]
                else:
                    dest = k2tmp[hp % 2]
                    dw = contrib_toks[hp - 2] if hp >= 2 else st["k2tmp_free"][hp % 2]
                evac = []
                for tb in range(NCB):
                    bi = tb % 2
                    w = bk_acq(bi, [ld] + xn_tok)
                    t = None
                    for fc in range(FCH):
                        t = mm(pbk[bi][:, 0:CB], buf[:, fc, :],
                               xn[fc][:, tb * CB:(tb + 1) * CB],
                               start=(fc == 0), stop=(fc == FCH - 1),
                               waits=w if fc == 0 else (),
                               inc="TE" if fc == FCH - 1 else None)
                    ew = [t] + ([dw] if dw and tb == 0 else [])
                    if not l0:
                        b = bq_sb if which == "q" else bk_sb
                        et = vec(lambda e, d=dest, tb=tb, bi=bi, b=b, hp=hp:
                                 e.tensor_scalar_add(d[:, tb * CB:(tb + 1) * CB],
                                                     pbk[bi][:, 0:CB],
                                                     b[:, hp:hp + 1]),
                                 waits=ew + [cst])
                    else:
                        et = vec(lambda e, d=dest, tb=tb, bi=bi:
                                 e.tensor_copy(d[:, tb * CB:(tb + 1) * CB],
                                               pbk[bi][:, 0:CB]), waits=ew)
                    bk_read(bi, et)
                    evac.append(et)
                ws.release(slot, evac[-1])
                s = act(lambda e, d=dest: e.activation(xsq[:], d[:], AF.Square),
                        waits=evac + [st["xsq_gate"]])
                red = []
                for tb in range(NCB):
                    w = bk_acq(6, [s])
                    t = mm(pbk[6][0:1, 0:CB], ones_col[:],
                           xsq[:, tb * CB:(tb + 1) * CB], True, True, w, inc="TE")
                    rt = vec(lambda e, tb=tb: e.reduce_max(
                        ntmp[0:1, tb:tb + 1], pbk[6][0:1, 0:CB], AX.X),
                        waits=[t, norm_done])
                    bk_read(6, rt)
                    red.append(rt)
                dstrow = qn_sb if which == "q" else kn_sb
                norm_done = vec(lambda e, d=dstrow, hp=hp: e.reduce_max(
                    d[0:1, hp:hp + 1], ntmp[0:1, 0:NCB], AX.X), waits=red)
                st["xsq_gate"] = s
                st["xsq_gate"] = red[-1]
                if which == "q":
                    pk = dma("sync", q_park[hp * 128:(hp + 1) * 128, :], q2[hp][:],
                             waits=evac + ([st["qpark_free"]] if st["qpark_free"]
                                           else []), sem=f"pk{hp}")
                    st["q2_free"][hp] = pk
                else:
                    cw = [st["contrib_free"]] if st["contrib_free"] else []
                    kc = dma("sync", k_contrib[hp * 128:(hp + 1) * 128, :],
                             k2tmp[hp % 2][:], waits=evac + cw, sem=f"kc{hp % 2}")
                    contrib_toks.append(kc)
            return contrib_toks, norm_done

        _, qn_done = proj_qk("q")
        k_contrib_toks, kn_done = proj_qk("k")
        st["k2tmp_free"] = [k_contrib_toks[-2], k_contrib_toks[-1]]
        kc_norm = dma("gpsimd", k_contrib[D:D + 1, 0:8], kn_sb[0:1, 0:8],
                      waits=[kn_done], sem="KN")
        k_contrib_toks.append(kc_norm)

        vs = st["vstream"]
        ones_done = [None, None]
        for i in range(2):
            prev = st["vstage_free"][i]
            for q4 in range(4):
                prev = vec(lambda e, i=i, q4=q4: e.memset(
                    vstage[i].ap()[:, q4, :].rearrange("p (h c) -> p h c", c=65)
                    [:, :, 64:65], 1.0), waits=[prev])
            ones_done[i] = prev
        v_contrib_toks = []
        ev_last = None
        for vh in range(2):
            buf, slot = vs.load(din[pre + "wv"][:, vh * 512:(vh + 1) * 512]
                                .rearrange("(g p) c -> p g c", p=128))
            ld = vs.ld[slot]
            for tb in range(NTB):
                bi = 2 + tb % 2
                w = bk_acq(bi, [ld] + xn_tok)
                t = None
                for fc in range(FCH):
                    last = (fc == FCH - 1) and l0
                    t = mm(pbk[bi][:, 0:512], xn[fc][:, tb * 128:(tb + 1) * 128],
                           buf[:, fc, :], start=(fc == 0), stop=last,
                           waits=w if fc == 0 else (), inc="TE" if last else None)
                if not l0:
                    t = mm(pbk[bi][:, 0:512], ones_r1_bf[0:1, :],
                           bv_sb[0:1, vh * 512:(vh + 1) * 512], False, True,
                           waits=[cstg], inc="TE")
                si = tb % 2
                sw = [t, ones_done[si]]
                if len(v_contrib_toks) >= 2:
                    sw.append(v_contrib_toks[-2])
                ev = vec(lambda e, si=si, bi=bi, vh=vh: e.tensor_copy(
                    vstage[si].ap()[:, 2 * vh:2 * vh + 2, :]
                    .rearrange("p q (h c) -> p (q h) c", c=65)[:, :, 0:64],
                    pbk[bi][:, 0:512]), waits=sw)
                bk_read(bi, ev)
                cw = [st["contrib_free"]] if st["contrib_free"] else []
                vc = dma("sync",
                         v_contrib.rearrange("(q t p) c -> p q t c", q=4, p=128)
                         [:, 2 * vh:2 * vh + 2, tb, :],
                         vstage[si][:, 2 * vh:2 * vh + 2, :], waits=[ev] + cw,
                         sem=f"vc{si}")
                v_contrib_toks.append(vc)
                ev_last = ev
            vs.release(slot, ev_last)
        st["vstage_free"] = [v_contrib_toks[-2], v_contrib_toks[-1]]

        coll_w = k_contrib_toks + [v_contrib_toks[-1], v_contrib_toks[-2]]
        if st["kgath_free"]:
            coll_w.append(st["kgath_free"])
        ck = P.emit("gpsimd", lambda e: e.collective_compute(
            "AllGather", ALU.bypass, ins=[k_contrib.opt()], outs=[k_gath.opt()],
            replica_groups=[[0, 1, 2, 3], [4, 5, 6, 7]]), waits=coll_w, inc="CCK")
        cv_w = [v_contrib_toks[-1]]
        if st["vgath_free"]:
            cv_w.append(st["vgath_free"])
        cv = P.emit("gpsimd", lambda e: e.collective_compute(
            "AllGather", ALU.bypass, ins=[v_contrib.opt()], outs=[v_gath.opt()],
            replica_groups=[[0, 1, 2, 3], [4, 5, 6, 7]]), waits=cv_w, inc="CCV")
        st["contrib_free"] = cv
        return ck, cv

    def emit_ffn_l0(xn_tok):
        gs, us, ds = st["gs"], st["us"], st["ds"]
        NHB = cfg.NHB_F
        h_tok = [None] * NHB
        g_free = [None, None]
        for tp in range(NTP):
            for hb in range(NHB):
                b1, s1 = gs.load(din["l0_wg"][:, hb * 128:(hb + 1) * 128]
                                 .rearrange("(g p) c -> p g c", p=128))
                b2, s2 = us.load(din["l0_wu"][:, hb * 128:(hb + 1) * 128]
                                 .rearrange("(g p) c -> p g c", p=128))
                for tbl in range(TP // CB):
                    col0 = tp * TP + tbl * CB
                    bg, bu = tbl % 2, 2 + tbl % 2
                    w = bk_acq(bg, [gs.ld[s1]] + xn_tok)
                    tg = None
                    for fc in range(FCH):
                        tg = mm(pbk[bg][:, 0:CB], b1[:, fc, :],
                                xn[fc][:, col0:col0 + CB],
                                start=(fc == 0), stop=(fc == FCH - 1),
                                waits=w if fc == 0 else (),
                                inc="TE" if fc == FCH - 1 else None)
                    w = bk_acq(bu, [us.ld[s2]])
                    tu = None
                    for fc in range(FCH):
                        tu = mm(pbk[bu][:, 0:CB], b2[:, fc, :],
                                xn[fc][:, col0:col0 + CB],
                                start=(fc == 0), stop=(fc == FCH - 1),
                                waits=w if fc == 0 else (),
                                inc="TE" if fc == FCH - 1 else None)
                    gi = tbl % 2
                    ga = act(lambda e, gi=gi, bg=bg: e.activation(
                        g_sb[gi][:, 0:CB], pbk[bg][:, 0:CB], AF.Silu,
                        scale=1.0 / 127.0), waits=[tg, g_free[gi]])
                    bk_read(bg, ga)
                    hw = [tu, ga]
                    if tp == 0 and tbl == 0:
                        if hb < HCAP:
                            hw.append(st["q2_free"][hb * TP // TPC])
                        elif st["arena_free"]:
                            hw.append(st["arena_free"])
                    ht = vec(lambda e, hb=hb, tbl=tbl, bu=bu, gi=gi:
                             e.scalar_tensor_tensor(
                                 h_l0[hb][:, tbl * CB:(tbl + 1) * CB],
                                 pbk[bu][:, 0:CB], 1.0 / 127.0, g_sb[gi][:, 0:CB],
                                 ALU.mult, ALU.mult), waits=hw)
                    bk_read(bu, ht)
                    g_free[gi] = ht
                    h_tok[hb] = ht
                gs.release(s1, h_tok[hb])
                us.release(s2, h_tok[hb])
            for ob in range(FCH):
                b3, s3 = ds.load(din["l0_wd"][:, ob * 128:(ob + 1) * 128]
                                 .rearrange("(g p) c -> p g c", p=128))
                last_at = None
                for tbl in range(TP // CB):
                    col0 = tp * TP + tbl * CB
                    bd = 4 + tbl % 2
                    w = bk_acq(bd, [ds.ld[s3]] + h_tok)
                    t = None
                    for hb in range(NHB):
                        t = mm(pbk[bd][:, 0:CB], b3[:, hb, :],
                               h_l0[hb][:, tbl * CB:(tbl + 1) * CB],
                               start=(hb == 0), stop=(hb == NHB - 1),
                               waits=w if hb == 0 else (),
                               inc="TE" if hb == NHB - 1 else None)
                    at = vec(lambda e, ob=ob, col0=col0, bd=bd:
                             e.scalar_tensor_tensor(
                                 xacc[ob][:, col0:col0 + CB], pbk[bd][:, 0:CB],
                                 1.0 / 127.0, xacc[ob][:, col0:col0 + CB],
                                 ALU.mult, ALU.add),
                             waits=[t, st["xacc"][ob]])
                    bk_read(bd, at)
                    st["xacc"][ob] = at
                    last_at = at
                ds.release(s3, last_at)
        return

    def emit_moe(xn_tok):
        NHB = cfg.NHB_M
        arena_w = [st["arena_free"]] if st["arena_free"] else []
        cT_toks = []
        rprev = None
        for tb in range(NTB):
            w = bk_acq(7, xn_tok + [cstg] + arena_w)
            t = None
            for fc in range(FCH):
                t = mm(pbk[7][:, 0:8], xn[fc][:, tb * 128:(tb + 1) * 128],
                       rw_sb[:, fc * 8:(fc + 1) * 8],
                       start=(fc == 0), stop=(fc == FCH - 1),
                       waits=w if fc == 0 else (), inc="TE" if fc == FCH - 1 else None)
            c0 = vec(lambda e: e.tensor_copy(lg_sb[:], pbk[7][:, 0:8]),
                     waits=[t, rprev])
            bk_read(7, c0)
            t1 = vec(lambda e: e.max(r_m8[:], lg_sb[:]), waits=[c0])
            t2 = vec(lambda e: e.max_index(r_i8[:], r_m8[:], lg_sb[:]), waits=[t1])
            t2b = vec(lambda e: e.tensor_copy(r_if[:], r_i8[:, 0:2]), waits=[t2])
            t3 = vec(lambda e: e.tensor_single_scalar(r_oh1[:], iota_e[:],
                                                      r_if[:, 0:1], ALU.is_equal),
                     waits=[t2b, cst])
            t7 = vec(lambda e: e.tensor_single_scalar(r_oh2[:], iota_e[:],
                                                      r_if[:, 1:2], ALU.is_equal),
                     waits=[t3])
            t8 = vec(lambda e: e.tensor_scalar_mul(r_neg[:], r_m8[:, 0:1], -1.0),
                     waits=[t1])
            t9 = act(lambda e: e.activation(r_p[:], lg_sb[:], AF.Exp,
                                            bias=r_neg[:, 0:1]), waits=[t8, t2])
            ta = vec(lambda e: e.reduce_sum(r_s[:], r_p[:], AX.X), waits=[t9])
            tb_ = vec(lambda e: e.reciprocal(r_t[:], r_s[:]), waits=[ta])
            tc = vec(lambda e: e.tensor_scalar_mul(r_p[:], r_p[:], r_t[:, 0:1]),
                     waits=[tb_])
            td = vec(lambda e: e.tensor_add(r_oh2[:], r_oh1[:], r_oh2[:]),
                     waits=[t7, tc])
            te_ = vec(lambda e: e.tensor_mul(r_p[:], r_p[:], r_oh2[:]), waits=[td])
            tf = vec(lambda e: e.reduce_sum(r_s[:], r_p[:], AX.X), waits=[te_])
            tg = vec(lambda e: e.tensor_scalar_add(r_s[:], r_s[:], 1e-9), waits=[tf])
            th = vec(lambda e: e.reciprocal(r_t[:], r_s[:]), waits=[tg])
            ti = vec(lambda e: e.tensor_scalar_mul(r_cf[:], r_p[:], r_t[:, 0:1]),
                     waits=[th])
            w = bk_acq(6, [ti, cst])
            tj = P.emit("tensor", lambda e: e.transpose(pbk[6][0:8, 0:128],
                                                        r_cf[:], ident[:]),
                        waits=w, inc="TE")
            tk = vec(lambda e, tb=tb: e.tensor_copy(
                cT_sb[0:8, tb * 128:(tb + 1) * 128], pbk[6][0:8, 0:128]),
                waits=[tj] + arena_w)
            bk_read(6, tk)
            tl = vec(lambda e, tb=tb: e.tensor_copy(
                cTb_sb[0:8, tb * 128:(tb + 1) * 128],
                cT_sb[0:8, tb * 128:(tb + 1) * 128]), waits=[tk])
            cT_toks.append(tl)
            rprev = ti

        if gate_tap is not None:
            dma("sync", gate_tap, cT_sb[0:8, :], waits=cT_toks, sem="OUT")
        m1s, m2s = st["m1s"], st["m2s"]
        h_tok = [None] * NHB
        rl_free = [arena_w[0] if arena_w else None]
        gb_tok = [None]
        for e_i in range(E):
            for tp in range(NTP):
                gi = 0
                gt = None
                for yw in range(TP // YW):
                    w = bk_acq(6, cT_toks + ([gb_tok[gi]] if gb_tok[gi] else [])
                               + (arena_w if e_i == 0 else []))
                    t = mm(pbk[6][:, 0:YW],
                           sel8_sb[0:8, e_i * 128:(e_i + 1) * 128],
                           cT_sb[0:8, tp * TP + yw * YW:tp * TP + (yw + 1) * YW],
                           True, True, waits=w, inc="TE")
                    gt = vec(lambda e, gi=gi, yw=yw: e.tensor_copy(
                        gb_sb[gi][:, yw * YW:(yw + 1) * YW], pbk[6][:, 0:YW]),
                        waits=[t])
                    bk_read(6, gt)
                gb_tok[gi] = gt
                for hb in range(NHB):
                    b1, s1 = m1s.load(din["l1_w1"][e_i, :, hb * 128:(hb + 1) * 128]
                                      .rearrange("(g p) c -> p g c", p=128))
                    for tbl in range(TP // CB):
                        col0 = tp * TP + tbl * CB
                        bi = tbl % 2
                        w = bk_acq(bi, [m1s.ld[s1]] + xn_tok)
                        t = None
                        for fc in range(FCH):
                            t = mm(pbk[bi][:, 0:CB], b1[:, fc, :],
                                   xn[fc][:, col0:col0 + CB],
                                   start=(fc == 0), stop=(fc == FCH - 1),
                                   waits=w if fc == 0 else (),
                                   inc="TE" if fc == FCH - 1 else None)
                        rt = act(lambda e, bi=bi, e_i=e_i, hb=hb: e.activation(
                            rl_buf[:, 0:CB], pbk[bi][:, 0:CB], AF.Relu,
                            bias=b1c_sb[:, e_i * NHB + hb:e_i * NHB + hb + 1]),
                            waits=[t, rl_free[0]])
                        bk_read(bi, rt)
                        hw = [rt, gb_tok[gi]]
                        if e_i == 0 and tp == 0 and tbl == 0:
                            if hb < HCAP:
                                hw.append(st["q2_free"][hb * TP // TPC])
                            elif st["arena_free"]:
                                hw.append(st["arena_free"])
                        ht = vec(lambda e, hb=hb, tbl=tbl, gi=gi:
                                 e.tensor_mul(
                                     h_moe[hb][:, tbl * CB:(tbl + 1) * CB],
                                     rl_buf[:, 0:CB],
                                     gb_sb[gi][:, tbl * CB:(tbl + 1) * CB]),
                                 waits=hw)
                        rl_free[0] = ht
                        h_tok[hb] = ht
                    m1s.release(s1, h_tok[hb])
                if h_tap is not None and e_i == 3 and tp == 0:
                    for hb in range(NHB):
                        dma("gpsimd", h_tap[hb * 128:(hb + 1) * 128, :],
                            h_moe[hb][:], waits=[h_tok[hb]], sem="HT")
                    htt = ("HT", P.cnt["HT"])
                    h_tok = [htt] * NHB
                ytiles = [(cb, th) for cb in range(4) for th in range(NYW)]
                for ch in range(2):
                    if e_i == 0:
                        for (cb, th) in ytiles:
                            bi = cb * NYW + th
                            w = bk_acq(bi, cT_toks + [cstg])
                            mm(pbk[bi][:, 0:YW],
                               b2_sb[0:8, ch * 512 + cb * 128:ch * 512 + (cb + 1) * 128],
                               cTb_sb[0:8, tp * TP + th * YW:tp * TP + (th + 1) * YW],
                               True, False, waits=w)
                    last = {}
                    for hbg in range(NHB // 2):
                        b2w, s2 = m2s.load(
                            din["l1_w2"][e_i, hbg * 256:(hbg + 1) * 256,
                                         ch * 512:(ch + 1) * 512]
                            .rearrange("(g p) c -> p g c", p=128))
                        slab_last = None
                        for g4 in range(2):
                            hb = hbg * 2 + g4
                            for (cb, th) in ytiles:
                                bi = cb * NYW + th
                                isend = (hb == NHB - 1)
                                isfirst = (e_i > 0 and hb == 0)
                                w = [m2s.ld[s2], h_tok[hb]] if (cb, th) == ytiles[0] else ()
                                if isfirst:
                                    w = bk_acq(bi, list(w) if w else [])
                                t = mm(pbk[bi][:, 0:YW],
                                       b2w[:, g4, cb * 128:(cb + 1) * 128],
                                       h_moe[hb][:, th * YW:(th + 1) * YW],
                                       isfirst, isend, waits=w,
                                       inc="TE" if isend or ((g4 == 1) and
                                                             (cb, th) == ytiles[-1])
                                       else None)
                                if isend:
                                    last[(cb, th)] = t
                                if (g4 == 1) and (cb, th) == ytiles[-1]:
                                    slab_last = t
                        m2s.release(s2, slab_last)
                    for (cb, th) in ytiles:
                        bi = cb * NYW + th
                        ob = ch * 4 + cb
                        col0 = tp * TP + th * YW
                        at = vec(lambda e, ob=ob, col0=col0, bi=bi:
                                 e.tensor_add(xacc[ob][:, col0:col0 + YW],
                                              pbk[bi][:, 0:YW],
                                              xacc[ob][:, col0:col0 + YW]),
                                 waits=[last[(cb, th)], st["xacc"][ob]])
                        bk_read(bi, at)
                        st["xacc"][ob] = at
        return

    def emit_attn(lid, ck, cv, xn_dead):
        l0 = (lid == 0)
        pre = f"l{lid}_"
        for r in range(4):
            dma("gpsimd", kn4_sb[0:1, r * 8:(r + 1) * 8],
                k_gath[r * (D + 1) + D:r * (D + 1) + D + 1, 0:8],
                waits=[ck], sem="K4")
        kn_toks = [("K4", P.cnt["K4"])]
        t1 = vec(lambda e: e.reduce_max(
            msc[0:1, 0:8],
            kn4_sb.ap()[0:1, 0:32].rearrange("a (r h) -> a h r", r=4), AX.X),
            waits=kn_toks)
        t2 = vec(lambda e: e.tensor_mul(msc[0:1, 0:8], msc[0:1, 0:8],
                                        qn_sb[0:1, 0:8]), waits=[t1])
        t3 = vec(lambda e: e.reduce_max(msc[0:1, 0:1], msc[0:1, 0:8], AX.X),
                 waits=[t2])
        t4 = act(lambda e: e.activation(msc[0:1, 0:1], msc[0:1, 0:1], AF.Ln),
                 waits=[t3])
        t5 = act(lambda e: e.activation(msc[0:1, 0:1], msc[0:1, 0:1], AF.Exp,
                                        scale=0.5), waits=[t4])
        t6 = vec(lambda e: e.tensor_scalar(negM1[0:1, 0:1], msc[0:1, 0:1],
                                           -0.1275, -1.0, ALU.mult, ALU.add),
                 waits=[t5])
        w = bk_acq(6, [t6, cst])
        t7 = mm(pbk[6][:, 0:1], ones_r1[0:1, :], negM1[0:1, 0:1], True, True, w,
                inc="TE")
        negM_tok = vec(lambda e: e.tensor_copy(negM[:], pbk[6][:, 0:1]), waits=[t7])
        bk_read(6, negM_tok)

        unpark = []
        for hp in range(8):
            w = [st["q2_free"][hp], st["h_free"]]
            unpark.append(dma("sync", q2[hp][:], q_park[hp * 128:(hp + 1) * 128, :],
                              waits=w, sem=f"up{hp}"))
        st["qpark_free"] = unpark[-1]

        def jmm(n=1):
            for _ in range(n):
                mm(pbk[5][:, 0:QW], masks_sb[:, 0:128], masks_sb[:, 0:QW],
                   True, True, waits=[cst])

        os_ = st["ostream"]
        gate = st["h_free"]
        os_.gate(gate)
        o_last = None
        kg_last = None
        vg_last = None
        rec_chain = gate
        recb_chain = gate
        attn_gate = gate
        slot_exp = [gate, gate]
        slot_av = [gate, gate]
        for quad in range(4):
            k_toks = {}
            for hpi in range(2):
                hp = quad * 2 + hpi
                for r in range(4):
                    ki = hpi * 4 + r
                    w = [ck] + ([st["B_free"][ki]] if st["B_free"][ki] else [])
                    if quad == 0:
                        w += xn_dead
                    t = dma("sync", kwin[ki][:],
                            k_gath[r * (D + 1) + hp * 128:
                                   r * (D + 1) + (hp + 1) * 128, :], waits=w,
                            sem=f"kw{ki}")
                    k_toks[(hpi, r)] = t
                    kg_last = t
            v_toks = {}
            for r in range(4):
                w = [cv, attn_gate]
                t = dma("sync", vw[r][:],
                        v_gath.rearrange("(r q t p) c -> p r q t c", r=4, q=4, p=128)
                        [:, r, quad, :, :], waits=w, sem=f"vq{r}")
                v_toks[r] = t
                vg_last = t
            oz_toks = []
            z_free = [attn_gate, attn_gate]
            for hpi in range(2):
                hp = quad * 2 + hpi
                hp_qk_last = None
                for hi in range(2):
                    hloc = 2 * hpi + hi
                    rlo, rhi = hi * 64, (hi + 1) * 64
                    for qb in range(NQB):
                        diag_kb = qb * QW // 128
                        if l0:
                            kl = [(r, kb) for r in range(4)
                                  for kb in range(diag_kb + 1)]
                        else:
                            kl = [(r, kb) for r in range(4) for kb in range(KTPR)]
                        pairs = [kl[i:i + 2] for i in range(0, len(kl), 2)]
                        np_ = len(pairs)
                        qk_tok = [None] * np_

                        def emit_qk(p_i):
                            pr = pairs[p_i]
                            slot = p_i % 2
                            for j, (r, kb) in enumerate(pr):
                                ww = [k_toks[(hpi, r)]]
                                if j == 0:
                                    ww += [slot_exp[slot], unpark[hp]]
                                qk_tok[p_i] = mm(
                                    pq[slot][:, j * QW:(j + 1) * QW],
                                    kwin[hpi * 4 + r][rlo:rhi,
                                                      kb * 128:(kb + 1) * 128],
                                    q2[hp][rlo:rhi, qb * QW:(qb + 1) * QW],
                                    True, True, waits=ww, inc="TE")

                        emit_qk(0)
                        zlast_t = None
                        for i in range(np_):
                            jmm(1)
                            if i + 1 < np_:
                                emit_qk(i + 1)
                            pr = pairs[i]
                            slot = i % 2
                            W = len(pr) * QW
                            et = act(lambda e, slot=slot, W=W: e.activation(
                                p_buf[slot][:, 0:W], pq[slot][:, 0:W], AF.Exp,
                                bias=negM[:, 0:1], scale=0.125),
                                waits=[qk_tok[i], negM_tok, slot_av[slot]])
                            slot_exp[slot] = et
                            half_tok = [et] * len(pr)
                            for j, (r, kb) in enumerate(pr):
                                if l0 and kb == diag_kb:
                                    half_tok[j] = vec(
                                        lambda e, slot=slot, j=j, r=r:
                                        e.tensor_mul(
                                            p_buf[slot][:, j * QW:(j + 1) * QW],
                                            p_buf[slot][:, j * QW:(j + 1) * QW],
                                            masks_sb[:, r * QW:(r + 1) * QW]),
                                        waits=[et, cst])
                            for j, (r, kb) in enumerate(pr):
                                first = (i == 0 and j == 0)
                                last = (i == np_ - 1 and j == len(pr) - 1)
                                w2 = [half_tok[j], v_toks[r]]
                                if first:
                                    w2 = bk_acq(4, w2)
                                zt = mm(pbk[4][0:65, 0:QW],
                                        vw[r][:, kb, hloc * 65:(hloc + 1) * 65],
                                        p_buf[slot][:, j * QW:(j + 1) * QW],
                                        start=first, stop=last,
                                        waits=w2, inc="TE")
                                slot_av[slot] = zt
                                if last:
                                    zlast_t = zt
                        hp_qk_last = qk_tok[np_ - 1]
                        jmm(6)
                        e1 = act(lambda e: e.activation(
                            recrow[0:1, 0:QW], pbk[4][64:65, 0:QW], AF.Ln),
                            waits=[zlast_t, rec_chain])
                        e2 = act(lambda e: e.activation(
                            recrow[0:1, 0:QW], recrow[0:1, 0:QW], AF.Exp,
                            scale=-1.0), waits=[e1])
                        w = bk_acq(6, [e2])
                        e3 = mm(pbk[6][0:64, 0:QW], ones_r1[0:1, 0:64],
                                recrow[0:1, 0:QW], True, True, w, inc="TE")
                        rec_chain = e3
                        e4 = vec(lambda e: e.tensor_copy(recb[:, 0:QW],
                                                         pbk[6][0:64, 0:QW]),
                                 waits=[e3, recb_chain])
                        bk_read(6, e4)
                        e5 = vec(lambda e, hpi=hpi, hi=hi, qb=qb: e.tensor_mul(
                            z_sb[hpi][hi * 64:(hi + 1) * 64,
                                      qb * QW:(qb + 1) * QW],
                            pbk[4][0:64, 0:QW], recb[:, 0:QW]),
                            waits=[e4, zlast_t, z_free[hpi]])
                        bk_read(4, e5)
                        recb_chain = e5
                        oz_toks.append(e5)
                st["q2_free"][hp] = hp_qk_last
            buf0, s0 = os_.load(din[pre + "wo"]
                                [(quad * 2) * 128:(quad * 2 + 1) * 128, :])
            buf1, s1 = os_.load(din[pre + "wo"]
                                [(quad * 2 + 1) * 128:(quad * 2 + 2) * 128, :])
            for ob in range(FCH):
                for tbl in range(NCB):
                    bo_i = 6 + tbl % 2
                    col0 = tbl * CB
                    w = bk_acq(bo_i, [os_.ld[s0], os_.ld[s1]] + oz_toks)
                    mm(pbk[bo_i][:, 0:CB], buf0[:, ob * 128:(ob + 1) * 128],
                       z_sb[0][:, col0:col0 + CB], True, False, w)
                    t = mm(pbk[bo_i][:, 0:CB], buf1[:, ob * 128:(ob + 1) * 128],
                           z_sb[1][:, col0:col0 + CB], False, True, (), inc="TE")
                    if (not l0) and quad == 0:
                        at = vec(lambda e, ob=ob, col0=col0, bo_i=bo_i:
                                 e.scalar_tensor_tensor(
                                     xacc[ob][:, col0:col0 + CB],
                                     pbk[bo_i][:, 0:CB], bo_sb[:, ob:ob + 1],
                                     xacc[ob][:, col0:col0 + CB],
                                     ALU.add, ALU.add),
                                 waits=[t, st["xacc"][ob], cst])
                    else:
                        at = vec(lambda e, ob=ob, col0=col0, bo_i=bo_i:
                                 e.tensor_add(xacc[ob][:, col0:col0 + CB],
                                              pbk[bo_i][:, 0:CB],
                                              xacc[ob][:, col0:col0 + CB]),
                                 waits=[t, st["xacc"][ob]])
                    bk_read(bo_i, at)
                    st["xacc"][ob] = at
                    o_last = at
            os_.release(s0, o_last)
            os_.release(s1, o_last)
            for ki in range(8):
                st["B_free"][ki] = o_last
            attn_gate = o_last
        for s_i in range(2):
            P.bank_r[2 * s_i] = slot_exp[s_i]
            P.bank_r[2 * s_i + 1] = slot_exp[s_i]
        st["kgath_free"] = kg_last
        st["vgath_free"] = vg_last
        return o_last

    for lid in layers:
        xn_tok = emit_ln(lid)
        ck, cv = emit_qkv(lid, xn_tok)
        if lid == 0:
            emit_ffn_l0(xn_tok)
        else:
            emit_moe(xn_tok)
        lastacc = max(st["xacc"], key=lambda t: t[1])
        st["h_free"] = lastacc
        if f"mid{lid}" in taps:
            for fc in range(FCH):
                dma("sync", taps[f"mid{lid}"][fc * 128:(fc + 1) * 128, :],
                    xacc[fc][:], waits=[st["xacc"][fc]], sem="OUT")
            ttok = ("OUT", P.cnt["OUT"])
            st["xacc"] = [ttok] * FCH
        xn_dead = [lastacc]
        o_last = emit_attn(lid, ck, cv, xn_dead)
        st["arena_free"] = o_last
        if f"x{lid}" in taps:
            for fc in range(FCH):
                dma("sync", taps[f"x{lid}"][fc * 128:(fc + 1) * 128, :], xacc[fc][:],
                    waits=[st["xacc"][fc]], sem="OUT")

    for fc in range(FCH):
        dma("sync", out_fm[fc * 128:(fc + 1) * 128, :], xacc[fc][:],
            waits=[st["xacc"][fc]], sem="OUT")
    final_v = P.cnt["OUT"]
    P.emit("sync", lambda e: e.nop(), waits=[("OUT", final_v)])

    import contextlib
    ctx = contextlib.ExitStack()
    sems = {s: ctx.enter_context(nc.semaphore(s)) for s in sorted(P.sems)}
    runners = P.replay_into(sems)
    with nc.Block() as blk:
        blk.tensor(lambda e: runners["tensor"](e))
        blk.vector(lambda e: runners["vector"](e))
        blk.scalar(lambda e: runners["scalar"](e))
        blk.sync(lambda e: runners["sync"](e))
        blk.gpsimd(lambda e: runners["gpsimd"](e))
    ctx.close()
    return nc


_BUILD_CACHE = {}


def kernel(**inputs):
    cfg = CFG()
    in_maps, idxs = host_prepare(cfg, inputs)
    key = (cfg.S, cfg.FH, cfg.MH, cfg.layers)
    if key not in _BUILD_CACHE:
        _BUILD_CACHE[key] = build_program(cfg, cfg.layers)
    nc = _BUILD_CACHE[key]
    res = run_bass_kernel_spmd(nc, in_maps, core_ids=list(range(8)))
    return host_assemble(cfg, res.results, idxs)
